# version 05
# baseline (speedup 1.0000x reference)
import numpy as np
import ml_dtypes
from contextlib import ExitStack
import concourse.bass as bass
import concourse.mybir as mybir
from concourse.bass_utils import run_bass_kernel_spmd

F32 = mybir.dt.float32
BF16 = mybir.dt.bfloat16
AF = mybir.ActivationFunctionType
ALU = mybir.AluOpType
AX = mybir.AxisListType

T = 2048
C = 1024
NCB = 8
NTB = 16
NTILE = 4
TN = 512
E = 16
LN_EPS = 1e-5

DEBUG = False


class Sem:
    def __init__(self, h):
        self.h = h
        self.n = 0


class Plan:
    ENGINES = ("sync", "gpsimd", "tensor", "scalar", "vector")

    def __init__(self):
        self.steps = {e: [] for e in self.ENGINES}
        self.free = {}

    def wait(self, eng, sem, val):
        if val > sem.n:
            raise RuntimeError(f"wait for {val}, sem at {sem.n}")
        if val > 0:
            self.steps[eng].append(lambda e, s=sem.h, v=val: e.wait_ge(s, v))

    def emit(self, eng, build, sem=None, k=1):
        if sem is not None:
            sem.n += k
            self.steps[eng].append(lambda e, b=build, s=sem.h, kk=k: b(e).then_inc(s, kk))
            return sem.n
        self.steps[eng].append(build)
        return None

    def use(self, eng, key):
        for sem, val in self.free.get(key, ()):
            self.wait(eng, sem, val)

    def release(self, key, sem, val=None):
        self.free[key] = [(sem, sem.n if val is None else val)]

    def release_multi(self, key, toks):
        self.free[key] = list(toks)

    def drain(self, eng):
        self.steps[eng].append(lambda e: e.drain())

    def waitp(self, eng, tok):
        self.wait(eng, tok[0], tok[1])


def build_nc():
    nc = bass.Bass("TRN2", target_bir_lowering=False)

    def din(name, shape, dt=F32):
        return nc.dram_tensor(name, shape, dt, kind="ExternalInput").ap()

    x_d = din("x", [T, C])
    Wk_d = din("Wk", [C, C]); Wv_d = din("Wv", [C, C]); Wr_d = din("Wr", [C, C])
    Wck_d = din("Wck", [4096, C]); Wcv_d = din("Wcv", [C, 4096]); Wcr_d = din("Wcr", [C, C])
    ln1w_d = din("ln1w", [C]); ln1b_d = din("ln1b", [C])
    ln2w_d = din("ln2w", [C]); ln2b_d = din("ln2b", [C])
    ts_d = din("tshift", [1, 1, C]); u_d = din("tfirst", [C])
    gw_d = din("gate_w", [E, C]); gb_d = din("gate_bias", [E])
    w1_d = din("w1", [E, 512, C]); w2_d = din("w2", [E, C, 512]); w3_d = din("w3", [E, 512, C])
    ws1_d = din("ws1", [512, C]); ws2_d = din("ws2", [C, 512]); ws3_d = din("ws3", [512, C])
    idb_d = din("identb", [128, 128], BF16)
    id32_d = din("ident32", [128, 128], F32)
    bas_d = din("basis", [E, E * 128], BF16)

    out_d = nc.dram_tensor("out", [T, C], F32, kind="ExternalOutput").ap()

    def scrt(name, shape):
        return nc.dram_tensor(name, shape, BF16, kind="Internal").ap()

    wkT_s = scrt("wkT_s", [8, 128, 1024])
    wvT_s = scrt("wvT_s", [8, 128, 1024])
    wrT_s = scrt("wrT_s", [8, 128, 1024])
    wckT_s = scrt("wckT_s", [32, 128, 1024])
    wcrT_s = scrt("wcrT_s", [8, 128, 1024])
    wcvT_s = scrt("wcvT_s", [8, 4, 128, 1024])
    w1T_s = scrt("w1T_s", [E, 4, 128, 1024])
    w3T_s = scrt("w3T_s", [E, 4, 128, 1024])
    w2T_s = scrt("w2T_s", [8, E, 128, 512])
    ws1T_s = scrt("ws1T_s", [4, 128, 1024])
    ws3T_s = scrt("ws3T_s", [4, 128, 1024])
    ws2T_s = scrt("ws2T_s", [8, 128, 512])
    H_s = scrt("H_s", [NTILE, E, 128, 2048])
    g_s = scrt("g_s", [NTILE, 128, 4096])

    dbg = {}
    if DEBUG:
        dbg["t_p0"] = nc.dram_tensor("t_p0", [128, NCB * T], F32, kind="ExternalOutput").ap()
        dbg["t_xxx"] = nc.dram_tensor("t_xxx", [128, NCB * T], BF16, kind="ExternalOutput").ap()
        dbg["t_x1"] = nc.dram_tensor("t_x1", [128, NCB * T], F32, kind="ExternalOutput").ap()
        dbg["t_s"] = nc.dram_tensor("t_s", [128, NCB * T], F32, kind="ExternalOutput").ap()
        dbg["t_cmf"] = nc.dram_tensor("t_cmf", [16, E * 128], BF16, kind="ExternalOutput").ap()
        dbg["t_sco"] = nc.dram_tensor("t_sco", [128, NTB * E], F32, kind="ExternalOutput").ap()
        dbg["t_xnb"] = nc.dram_tensor("t_xnb", [128, NCB * T], BF16, kind="ExternalOutput").ap()
        dbg["t_gt"] = nc.dram_tensor("t_gt", [128, NCB * E], BF16, kind="ExternalOutput").ap()
        dbg["t_gbr"] = nc.dram_tensor("t_gbr", [128, E], F32, kind="ExternalOutput").ap()
        dbg["t_g"] = nc.dram_tensor("t_g", [NTILE, 128, 4096], BF16, kind="ExternalOutput").ap()
        dbg["t_kv"] = nc.dram_tensor("t_kv", [128, 5120], F32, kind="ExternalOutput").ap()

    es = ExitStack()
    sb = lambda name, shape, dt: es.enter_context(nc.sbuf_tensor(name, shape, dt))

    A = sb("A_stream", [128, NCB * T], F32)
    XNB = sb("A_xnb", [128, NCB * T], BF16)
    XX = sb("A_xxx", [128, NCB * T], BF16)
    AR = sb("AR", [128, 13376], F32)
    WROW = sb("W_row", [128, 2 * 1024], BF16)
    WT = sb("W_T", [128, 4 * 1024], BF16)
    IDB = sb("identb_s", [128, 128], BF16)
    ID32 = sb("ident32_s", [128, 128], F32)
    ONECOL = sb("onecol", [128, 1], F32)
    ONEROW = sb("onerow", [1, 128], F32)
    ROWS = sb("rows", [1, 1024], F32)
    MU1C = sb("mu1c", [128, NTB], F32)
    RS1C = sb("rs1c", [128, NTB], F32)
    VEC = sb("veccols", [128, 10 * NCB], F32)
    CAR = sb("carries", [128, 2 * NCB], F32)
    H2P = sb("h2prev", [128, NCB], F32)
    GATET = sb("gateT", [128, NCB * E], BF16)
    BAS = sb("basis_s", [E, E * 128], BF16)
    GBR = sb("gbrow", [128, E], F32)
    GBS = sb("gbs", [1, E], F32)
    SCR = sb("scr", [128, 512], F32)

    V_U, V_TS, V_1PTS, V_NTS, V_1MTS, V_L1W, V_L1B, V_L2W, V_L2B, V_SP = range(10)

    def vcol(i, cb):
        return VEC[:, i * NCB + cb: i * NCB + cb + 1]

    def vcols(i):
        return VEC[:, i * NCB: (i + 1) * NCB]

    PS = [es.enter_context(nc.psum_tensor(f"ps{i}", [128, 512], F32)) for i in range(8)]
    MM0, MM1, MM2, MM3, T0, T1, ST, SP2 = range(8)

    mksem = lambda nm: Sem(es.enter_context(nc.semaphore(nm)))
    TE = mksem("TE"); SE = mksem("SE"); VE = mksem("VE"); OUT = mksem("OUT")
    S_CONST = mksem("S_CONST"); S_GW = mksem("S_GW"); S_MEM = mksem("S_MEM")
    S_XT = [mksem(f"S_XT{i}") for i in range(3)]
    S_WR = [mksem(f"S_WR{i}") for i in range(2)]
    S_SP = [mksem(f"S_SP{i}") for i in range(4)]
    S_WT = [mksem(f"S_WT{i}") for i in range(4)]
    S_W3 = mksem("S_W3"); S_HS = [mksem(f"S_HS{i}") for i in range(2)]
    S_H = mksem("S_H"); S_WS = mksem("S_WS"); S_GS = mksem("S_GS")
    S_GT = mksem("S_GT")
    S_X2 = [mksem(f"S_X2{i}") for i in range(2)]
    S_OT = [mksem(f"S_OT{i}") for i in range(2)]

    P = Plan()
    free = P.free

    A3 = A[:].rearrange("p (cb t) -> p cb t", t=T)
    XNB3 = XNB[:].rearrange("p (cb t) -> p cb t", t=T)
    XX3b = XX[:].rearrange("p (s n) -> p s n", n=2048)

    def a_sl(cb, a, b):
        return A[:, cb * T + a: cb * T + b]

    def xnb_sl(cb, a, b):
        return XNB[:, cb * T + a: cb * T + b]

    def xx_sl(cb, a, b):
        return XX[:, cb * T + a: cb * T + b]

    def dma(eng, sem, out, in_):
        return P.emit(eng, lambda e, o=out, i=in_: e.dma_start(out=o, in_=i), sem, 16)

    prep_jobs = []

    def add_job(src, nkc, dst):
        prep_jobs.append((src, nkc, dst))

    for ob in range(8):
        add_job(Wk_d[ob * 128:(ob + 1) * 128, :], 8, wkT_s[ob])
    for ob in range(8):
        add_job(Wv_d[ob * 128:(ob + 1) * 128, :], 8, wvT_s[ob])
    for ob in range(8):
        add_job(Wr_d[ob * 128:(ob + 1) * 128, :], 8, wrT_s[ob])
    for ob in range(32):
        add_job(Wck_d[ob * 128:(ob + 1) * 128, :], 8, wckT_s[ob])
    for ob in range(8):
        add_job(Wcr_d[ob * 128:(ob + 1) * 128, :], 8, wcrT_s[ob])
    for ob in range(8):
        for ch in range(4):
            add_job(Wcv_d[ob * 128:(ob + 1) * 128, ch * 1024:(ch + 1) * 1024], 8, wcvT_s[ob][ch])
    for e_i in range(E):
        for ib in range(4):
            add_job(w1_d[e_i][ib * 128:(ib + 1) * 128, :], 8, w1T_s[e_i][ib])
        for ib in range(4):
            add_job(w3_d[e_i][ib * 128:(ib + 1) * 128, :], 8, w3T_s[e_i][ib])
    for e_i in range(E):
        for db in range(8):
            add_job(w2_d[e_i][db * 128:(db + 1) * 128, :], 4, w2T_s[db][e_i])
    for ib in range(4):
        add_job(ws1_d[ib * 128:(ib + 1) * 128, :], 8, ws1T_s[ib])
    for ib in range(4):
        add_job(ws3_d[ib * 128:(ib + 1) * 128, :], 8, ws3T_s[ib])
    for db in range(8):
        add_job(ws2_d[db * 128:(db + 1) * 128, :], 4, ws2T_s[db])
    N_JOBS = len(prep_jobs)
    JB_WK, JB_WV, JB_WR, JB_WCK, JB_WCR, JB_WCV = 0, 8, 16, 24, 56, 64
    JB_W1 = 96
    JB_W2 = 224
    JB_WS1, JB_WS3, JB_WS2 = 352, 356, 360

    job_spilled = {}
    prep_state = {"next": 0, "tb": 0}

    def prep_pump(n):
        for _ in range(n):
            j = prep_state["next"]
            if j >= N_JOBS:
                return
            prep_state["next"] = j + 1
            src, nkc, dst = prep_jobs[j]
            wr = j % 2
            row = WROW[:, wr * 1024: wr * 1024 + nkc * 128]
            P.use("gpsimd", ("wrow", wr))
            nwd = dma("gpsimd", S_WR[wr], row, src)
            bank = T0 if prep_state["tb"] == 0 else T1
            prep_state["tb"] ^= 1
            wt = j % 4
            tile = WT[:, wt * 1024: wt * 1024 + nkc * 128]
            P.use("tensor", ("bank", bank))
            P.use("tensor", ("wt", wt))
            P.wait("tensor", S_WR[wr], nwd)
            for kk in range(nkc):
                P.emit("tensor", lambda e, kk=kk, b=bank, r=row, nkc=nkc: e.matmul(
                    PS[b][:].bitcast(BF16)[:, kk * 128:(kk + 1) * 128],
                    r[:, kk * 128:(kk + 1) * 128], IDB[:],
                    is_transpose=True, start=(kk == 0), stop=(kk == nkc - 1)), TE)
            P.release(("wrow", wr), TE)
            P.wait("scalar", TE, TE.n)
            P.emit("scalar", lambda e, t=tile, b=bank, n=nkc * 128: e.copy(t, PS[b][:].bitcast(BF16)[:, 0:n]), SE)
            P.release(("bank", bank), SE)
            P.wait("sync", SE, SE.n)
            nsp = dma("sync", S_SP[wt], dst, tile)
            P.release(("wt", wt), S_SP[wt])
            job_spilled[j] = (S_SP[wt], nsp)

    reload_state = {"rr": 0}

    def w_reload(src_bf16, nelem=1024):
        sl = reload_state["rr"] % 4
        reload_state["rr"] += 1
        dst = WT[:, sl * 1024: sl * 1024 + nelem]
        P.use("sync", ("wt", sl))
        nre = dma("sync", S_WT[sl], dst, src_bf16)
        return dst, (S_WT[sl], nre), sl

    def w_reload2(src2):
        if reload_state["rr"] % 2:
            reload_state["rr"] += 1
        sl = reload_state["rr"] % 4
        reload_state["rr"] += 2
        dst = WT[:, sl * 1024: sl * 1024 + 2048]
        P.use("sync", ("wt", sl))
        P.use("sync", ("wt", sl + 1))
        nre = dma("sync", S_WT[sl], dst.rearrange("p (h n) -> p h n", n=1024), src2.rearrange("h p n -> p h n"))
        return dst, (S_WT[sl], nre), (sl, sl + 1)

    def w_reload4(src4):
        if reload_state["rr"] % 4:
            reload_state["rr"] += 4 - reload_state["rr"] % 4
        reload_state["rr"] += 4
        dst = WT[:, 0:4096]
        for s in range(4):
            P.use("sync", ("wt", s))
        nre = dma("sync", S_WT[0], dst.rearrange("p (h n) -> p h n", n=1024), src4.rearrange("h p n -> p h n"))
        return dst, (S_WT[0], nre), (0, 1, 2, 3)

    def need_job(j):
        assert j in job_spilled, f"prep job {j} not yet emitted"
        return job_spilled[j]

    dma("sync", S_CONST, IDB[:], idb_d)
    dma("sync", S_CONST, ID32[:], id32_d)
    dma("sync", S_CONST, BAS[:], bas_d)
    dma("sync", S_CONST, GBS[:], gb_d.rearrange("(a b) -> a b", a=1))
    VSTG = WT[:].bitcast(F32)[0:8, 0:768]
    vec_list = [(u_d, V_U), (ts_d.rearrange("x y c -> (x y c)"), V_TS), (ln1w_d, V_L1W),
                (ln1b_d, V_L1B), (ln2w_d, V_L2W), (ln2b_d, V_L2B)]
    for i, (src, _) in enumerate(vec_list):
        dma("sync", S_CONST, VSTG[:, i * 128:(i + 1) * 128], src.rearrange("(a b) -> a b", b=128))
    GWS = WROW[0:16, 1024:2048]
    dma("gpsimd", S_GW, GWS, gw_d)
    n_in_const = S_CONST.n
    n_gw = S_GW.n

    P.emit("gpsimd", lambda e: e.memset(ONECOL[:], 1.0))
    P.emit("gpsimd", lambda e: e.memset(ONEROW[:], 1.0))
    P.emit("gpsimd", lambda e: e.memset(CAR[:, 0:NCB], 1.0))
    P.emit("gpsimd", lambda e: e.memset(CAR[:, NCB:2 * NCB], 0.0))
    P.emit("gpsimd", lambda e: e.memset(SCR[:, 0:128], 1.0), S_MEM, 1)
    n_memset = S_MEM.n

    P.wait("tensor", S_CONST, n_in_const)
    P.wait("tensor", S_MEM, n_memset)
    for i in range(len(vec_list)):
        P.emit("tensor", lambda e, i=i: e.matmul(
            PS[T0][:, i * 8:(i + 1) * 8], VSTG[:, i * 128:(i + 1) * 128], ID32[0:8, 0:8],
            is_transpose=True, start=(i == 0), stop=(i == len(vec_list) - 1)), TE)
    t_vec = TE.n
    P.wait("scalar", TE, t_vec)
    for i, (_, vidx) in enumerate(vec_list):
        P.emit("scalar", lambda e, i=i, v=vidx: e.copy(vcols(v), PS[T0][:, i * 8:(i + 1) * 8]), SE)
    se_vec = SE.n
    P.release(("bank", T0), SE)
    P.release(("wt", 0), TE, t_vec)
    P.release(("wt", 1), TE, t_vec)
    P.wait("vector", SE, se_vec)
    P.emit("vector", lambda e: e.tensor_scalar(vcols(V_1PTS), vcols(V_TS), 1.0, None, ALU.add, ALU.bypass))
    P.emit("vector", lambda e: e.tensor_scalar(vcols(V_NTS), vcols(V_TS), -1.0, None, ALU.mult, ALU.bypass))
    P.emit("vector", lambda e: e.tensor_scalar(vcols(V_1MTS), vcols(V_TS), -1.0, 1.0, ALU.mult, ALU.add), VE)
    ve_vcols = VE.n

    P.wait("tensor", S_GW, n_gw)
    for kc in range(NCB):
        P.emit("tensor", lambda e, kc=kc: e.matmul(
            PS[T1][:].bitcast(BF16)[:, kc * 16:(kc + 1) * 16],
            GWS[:, kc * 128:(kc + 1) * 128], IDB[0:16, 0:16],
            is_transpose=True, start=(kc == 0), stop=(kc == 7)), TE)
    t_gate = TE.n
    P.wait("scalar", TE, t_gate)
    P.emit("scalar", lambda e: e.copy(GATET[:], PS[T1][:].bitcast(BF16)[:, 0:NCB * E]), SE)
    se_gate = SE.n
    P.release(("bank", T1), SE)
    P.release(("wrow", 1), SE)
    P.release(("wrow", 0), SE, se_gate)
    P.emit("tensor", lambda e: e.matmul(PS[SP2][:, 0:16], ONEROW[:], GBS[:], start=True, stop=True), TE)
    t_gbr = TE.n
    P.wait("scalar", TE, t_gbr)
    P.emit("scalar", lambda e: e.copy(GBR[:], PS[SP2][:, 0:16]), SE)
    P.release(("bank", SP2), SE)

    XTM = lambda sl: AR[:, 10240 + sl * 1024: 10240 + (sl + 1) * 1024]
    SQD = AR[:, 9216:10240]
    se_tb = {}
    for tb in range(NTB):
        sl = tb % 3
        CSX = lambda i, sl=sl: AR[:, 13320 + sl * 16 + i: 13321 + sl * 16 + i]
        P.use("sync", ("xtm", sl))
        n_ld = dma("sync", S_XT[sl], XTM(sl), x_d[tb * 128:(tb + 1) * 128, :])
        P.wait("vector", S_XT[sl], n_ld)
        P.use("vector", ("csx", sl))
        P.emit("vector", lambda e, sl=sl, CSX=CSX: e.tensor_reduce(CSX(0), XTM(sl), AX.X, ALU.add), VE)
        P.drain("vector")
        P.wait("scalar", S_XT[sl], n_ld)
        P.use("scalar", ("sqd",))
        P.use("scalar", ("csxs", sl))
        P.emit("scalar", lambda e, sl=sl, CSX=CSX: e.activation(SQD, XTM(sl), AF.Square, accum_out=CSX(1)), SE)
        se_sq = SE.n
        P.wait("vector", SE, se_sq)
        P.emit("vector", lambda e, CSX=CSX: e.tensor_scalar(CSX(2), CSX(0), 1.0 / C, None, ALU.mult, ALU.bypass))
        P.emit("vector", lambda e, CSX=CSX: e.tensor_scalar(CSX(3), CSX(1), 1.0 / C, None, ALU.mult, ALU.bypass))
        P.drain("vector")
        P.emit("vector", lambda e, CSX=CSX: e.tensor_scalar(CSX(4), CSX(2), CSX(2), None, ALU.mult, ALU.bypass))
        P.drain("vector")
        P.emit("vector", lambda e, CSX=CSX: e.tensor_scalar(CSX(5), CSX(3), CSX(4), LN_EPS, ALU.subtract, ALU.add), VE)
        ve_var = VE.n
        P.wait("scalar", VE, ve_var)
        P.emit("scalar", lambda e, CSX=CSX: e.activation(CSX(6), CSX(5), AF.Ln))
        P.drain("scalar")
        P.emit("scalar", lambda e, CSX=CSX: e.activation(CSX(6), CSX(6), AF.Exp, scale=-0.5), SE)
        se_rstd = SE.n
        P.wait("vector", SE, se_rstd)
        P.emit("vector", lambda e, CSX=CSX, sl=sl: e.tensor_scalar(XTM(sl), XTM(sl), CSX(2), CSX(6), ALU.subtract, ALU.mult), VE)
        ve_xn = VE.n
        P.emit("vector", lambda e, CSX=CSX, sl=sl: e.tensor_reduce(CSX(7), XTM(sl), AX.X, ALU.add), VE)
        P.drain("vector")
        P.wait("scalar", VE, ve_xn)
        P.emit("scalar", lambda e, CSX=CSX, sl=sl: e.activation(SQD, XTM(sl), AF.Square, accum_out=CSX(8)), SE)
        se_sq1 = SE.n
        P.release(("sqd",), SE)
        P.wait("vector", SE, se_sq1)
        P.emit("vector", lambda e, CSX=CSX, tb=tb: e.tensor_scalar(MU1C[:, tb:tb + 1], CSX(7), 1.0 / C, None, ALU.mult, ALU.bypass))
        P.emit("vector", lambda e, CSX=CSX: e.tensor_scalar(CSX(9), CSX(8), 1.0 / C, None, ALU.mult, ALU.bypass))
        P.drain("vector")
        P.emit("vector", lambda e, CSX=CSX, tb=tb: e.tensor_scalar(CSX(10), MU1C[:, tb:tb + 1], MU1C[:, tb:tb + 1], None, ALU.mult, ALU.bypass))
        P.drain("vector")
        P.emit("vector", lambda e, CSX=CSX: e.tensor_scalar(CSX(9), CSX(9), CSX(10), LN_EPS, ALU.subtract, ALU.add), VE)
        ve_v1 = VE.n
        P.wait("scalar", VE, ve_v1)
        P.emit("scalar", lambda e, CSX=CSX: e.activation(CSX(9), CSX(9), AF.Ln))
        P.drain("scalar")
        P.emit("scalar", lambda e, CSX=CSX, tb=tb: e.activation(RS1C[:, tb:tb + 1], CSX(9), AF.Exp, scale=-0.5), SE)
        prev_se = SE.n
        P.release(("csx", sl), SE)
        P.release(("csxs", sl), VE, ve_v1)
        toks = [(VE, ve_v1), (SE, prev_se)]
        for g in range(2):
            bank = T0 if g == 0 else T1
            P.use("tensor", ("bank", bank))
            P.wait("tensor", VE, ve_xn)
            for j in range(4):
                cb = g * 4 + j
                P.emit("tensor", lambda e, sl=sl, cb=cb, j=j, b=bank: e.matmul(
                    PS[b][:, j * 128:(j + 1) * 128], XTM(sl)[:, cb * 128:(cb + 1) * 128],
                    ID32[:], is_transpose=True, start=(j == 0), stop=(j == 3)), TE)
            t_tr = TE.n
            toks.append((TE, t_tr))
            P.wait("scalar", TE, t_tr)
            ps4 = PS[bank][:].rearrange("p (j t) -> p j t", t=128)
            P.emit("scalar", lambda e, d=A3[:, g * 4:g * 4 + 4, tb * 128:(tb + 1) * 128], s=ps4: e.copy(d, s), SE)
            P.emit("scalar", lambda e, d=XNB3[:, g * 4:g * 4 + 4, tb * 128:(tb + 1) * 128], s=ps4: e.copy(d, s), SE)
            P.release(("bank", bank), SE)
        P.release_multi(("xtm", sl), toks)
        se_tb[tb] = SE.n
        prep_pump(3)
    se_p0 = SE.n

    if DEBUG:
        P.wait("sync", SE, se_p0)
        n_dbg0 = dma("sync", OUT, dbg["t_p0"], A[:])

    HB = lambda cb, a, b: AR[:, 3104 + cb * 513 + a: 3104 + cb * 513 + b]
    rows_mu = ROWS[0:1, 0:512]
    rows_rs = ROWS[0:1, 512:1024]
    P.wait("vector", VE, ve_vcols)
    for cb in range(NCB):
        P.emit("vector", lambda e, cb=cb: e.tensor_copy(HB(cb, 0, 1), vcol(V_TS, cb)), VE)
    ve_xxx_tile = {}
    for tau in range(NTILE):
        t0 = tau * TN
        P.use("tensor", ("bank", ST))
        P.use("tensor", ("bank", SP2))
        P.wait("tensor", SE, se_tb[tau * 4 + 3])
        for j in range(4):
            P.emit("tensor", lambda e, j=j, tau=tau: e.matmul(
                PS[ST][0:1, j * 128:(j + 1) * 128], MU1C[:, tau * 4 + j:tau * 4 + j + 1],
                ID32[:], is_transpose=True, start=(j == 0), stop=(j == 3)), TE)
        for j in range(4):
            P.emit("tensor", lambda e, j=j, tau=tau: e.matmul(
                PS[SP2][0:1, j * 128:(j + 1) * 128], RS1C[:, tau * 4 + j:tau * 4 + j + 1],
                ID32[:], is_transpose=True, start=(j == 0), stop=(j == 3)), TE)
        t_rows = TE.n
        P.use("scalar", ("rows",))
        P.wait("scalar", TE, t_rows)
        P.emit("scalar", lambda e: e.copy(rows_mu, PS[ST][0:1, 0:512]))
        P.emit("scalar", lambda e: e.copy(rows_rs, PS[SP2][0:1, 0:512]), SE)
        se_rows = SE.n
        P.release(("bank", ST), SE)
        P.release(("bank", SP2), SE)
        P.use("tensor", ("bank", MM2))
        P.use("tensor", ("bank", MM3))
        P.wait("tensor", SE, se_rows)
        P.emit("tensor", lambda e: e.matmul(PS[MM2][:], ONEROW[:], rows_mu, start=True, stop=True), TE)
        P.emit("tensor", lambda e: e.matmul(PS[MM3][:], ONEROW[:], rows_rs, start=True, stop=True), TE)
        t_bc = TE.n
        P.release(("rows",), TE)
        P.wait("vector", TE, t_bc)
        for cb in range(NCB):
            tmp = AR[:, 7208 + (cb % 2) * 512: 7208 + (cb % 2) * 512 + 512]
            P.emit("vector", lambda e, cb=cb, t0=t0: e.tensor_tensor(HB(cb, 1, 513), a_sl(cb, t0, t0 + TN), PS[MM2][:], ALU.subtract))
            P.emit("vector", lambda e, cb=cb: e.tensor_tensor(HB(cb, 1, 513), HB(cb, 1, 513), PS[MM3][:], ALU.mult))
            P.emit("vector", lambda e, cb=cb: e.tensor_scalar(HB(cb, 1, 513), HB(cb, 1, 513), vcol(V_L1W, cb), vcol(V_L1B, cb), ALU.mult, ALU.add))
            P.emit("vector", lambda e, cb=cb, tmp=tmp: e.tensor_scalar(tmp, HB(cb, 1, 513), vcol(V_1PTS, cb), None, ALU.mult, ALU.bypass))
            P.emit("vector", lambda e, cb=cb, tmp=tmp, t0=t0: e.scalar_tensor_tensor(
                xx_sl(cb, t0, t0 + TN), HB(cb, 0, 512), vcol(V_NTS, cb), tmp, ALU.mult, ALU.add), VE)
        ve_xxx_tile[tau] = VE.n
        P.release(("bank", MM2), VE)
        P.release(("bank", MM3), VE)
        if tau < NTILE - 1:
            for cb in range(NCB):
                P.emit("vector", lambda e, cb=cb: e.tensor_copy(HB(cb, 0, 1), HB(cb, 512, 513)), VE)
        prep_pump(3)
    ve_p1 = VE.n

    if DEBUG:
        P.wait("sync", VE, ve_p1)
        dma("sync", OUT, dbg["t_xxx"], XX[:])

    kv_k = lambda j, a, b: AR[:, j * 2560 + a: j * 2560 + b]
    kv_v = lambda j, a, b: AR[:, j * 2560 + 1024 + a: j * 2560 + 1024 + b]
    kv_r = lambda j: AR[:, j * 2560 + 2048: j * 2560 + 2560].bitcast(BF16)
    SBASE = lambda p: 10240 + p * 770
    S_EK = lambda p: AR[:, SBASE(p): SBASE(p) + 128]
    S_EU = lambda p: AR[:, SBASE(p) + 128: SBASE(p) + 256]
    S_T1 = lambda p: AR[:, SBASE(p) + 256: SBASE(p) + 384]
    S_T2 = lambda p: AR[:, SBASE(p) + 384: SBASE(p) + 512]
    S_DB = lambda p: AR[:, SBASE(p) + 512: SBASE(p) + 641]
    S_AB = lambda p: AR[:, SBASE(p) + 641: SBASE(p) + 770]
    S_ONE = SCR[:, 0:128]

    prod_done = {}

    def emit_prod_half(ob, h):
        j = ob % 4
        for kind, scr_t, jb0 in (("k", wkT_s, JB_WK), ("v", wvT_s, JB_WV), ("r", wrT_s, JB_WR)):
            P.waitp("sync", need_job(jb0 + ob))
            tile, retok, wsl = w_reload(scr_t[ob])
            for sub in range(2):
                tt = 2 * h + sub
                bank = MM0 if sub == 0 else MM1
                P.use("tensor", ("bank", bank))
                if sub == 0:
                    if kind == "k":
                        P.use("tensor", ("kvr", j))
                    P.waitp("tensor", retok)
                    P.wait("tensor", VE, ve_p1)
                for kc in range(NCB):
                    P.emit("tensor", lambda e, b=bank, kc=kc, t=tile, tt=tt: e.matmul(
                        PS[b][:], t[:, kc * 128:(kc + 1) * 128], xx_sl(kc, tt * TN, (tt + 1) * TN),
                        start=(kc == 0), stop=(kc == 7)), TE)
                t_mm = TE.n
                if sub == 1:
                    P.release(("wt", wsl), TE)
                P.wait("scalar", TE, t_mm)
                if kind == "k":
                    dst = kv_k(j, sub * TN, (sub + 1) * TN)
                elif kind == "v":
                    dst = kv_v(j, sub * TN, (sub + 1) * TN)
                else:
                    dst = kv_r(j)[:, sub * TN:(sub + 1) * TN]
                P.emit("scalar", lambda e, d=dst, b=bank: e.copy(d, PS[b][:]), SE)
                P.release(("bank", bank), SE)
        prod_done[(ob, h)] = SE.n

    def emit_scan_quad(cbs, h):
        TC = 128
        for p, cb in enumerate(cbs):
            P.wait("vector", SE, prod_done[(cb, h)])
        for q in range(8):
            a = q * TC
            ga = h * 1024 + a
            se_ek = [0] * 4; ve_m = [0] * 4; se_rm = [0] * 4; ve_den = [0] * 4; se_sig = [0] * 4
            for p, cb in enumerate(cbs):
                j = cb % 4
                kq = kv_k(j, a, a + TC)
                P.wait("scalar", S_MEM, n_memset)
                P.emit("scalar", lambda e, k=kq, p=p: e.activation(S_EK(p), k, AF.Exp))
                P.emit("scalar", lambda e, k=kq, p=p, cb=cb: e.activation(S_EU(p), k, AF.Exp, bias=vcol(V_U, cb)), SE)
                se_ek[p] = SE.n
            for p, cb in enumerate(cbs):
                dcar = CAR[:, cb:cb + 1]
                P.wait("vector", SE, se_ek[p])
                P.emit("vector", lambda e, d=dcar, p=p: e.tensor_copy(S_DB(p)[:, 0:1], d))
                P.emit("vector", lambda e, d=dcar, p=p: e.tensor_tensor_scan(S_DB(p)[:, 1:129], S_ONE, S_EK(p), d, ALU.mult, ALU.add))
                P.emit("vector", lambda e, p=p: e.tensor_tensor(S_T1(p), S_DB(p)[:, 0:128], S_EK(p), ALU.max), VE)
                ve_m[p] = VE.n
            for p in range(4):
                P.wait("scalar", VE, ve_m[p])
                P.emit("scalar", lambda e, p=p: e.activation(S_T2(p), S_T1(p), AF.Ln))
                P.emit("scalar", lambda e, p=p: e.activation(S_T1(p), S_T2(p), AF.Exp, scale=-1.0), SE)
                se_rm[p] = SE.n
            for p, cb in enumerate(cbs):
                j = cb % 4
                vq = kv_v(j, a, a + TC)
                acar = CAR[:, NCB + cb:NCB + cb + 1]
                dcar = CAR[:, cb:cb + 1]
                P.wait("vector", SE, se_rm[p])
                P.emit("vector", lambda e, p=p: e.tensor_tensor(S_T2(p), S_DB(p)[:, 0:128], S_T1(p), ALU.mult))
                P.emit("vector", lambda e, p=p: e.tensor_tensor(S_EK(p), S_EK(p), S_T1(p), ALU.mult))
                P.emit("vector", lambda e, v=vq, p=p: e.tensor_tensor(S_EK(p), S_EK(p), v, ALU.mult))
                P.emit("vector", lambda e, ac=acar, p=p: e.tensor_copy(S_AB(p)[:, 0:1], ac))
                P.emit("vector", lambda e, ac=acar, p=p: e.tensor_tensor_scan(S_AB(p)[:, 1:129], S_T2(p), S_EK(p), ac, ALU.mult, ALU.add))
                P.emit("vector", lambda e, d=dcar, p=p: e.tensor_copy(d, S_DB(p)[:, 128:129]))
                P.emit("vector", lambda e, ac=acar, p=p: e.tensor_copy(ac, S_AB(p)[:, 128:129]))
                P.emit("vector", lambda e, p=p: e.tensor_tensor(S_T1(p), S_DB(p)[:, 0:128], S_AB(p)[:, 0:128], ALU.mult))
                P.emit("vector", lambda e, v=vq, p=p: e.tensor_tensor(S_T2(p), S_EU(p), v, ALU.mult))
                P.emit("vector", lambda e, p=p: e.tensor_tensor(S_T1(p), S_T1(p), S_T2(p), ALU.add))
                P.emit("vector", lambda e, p=p: e.tensor_tensor(S_T2(p), S_DB(p)[:, 0:128], S_EU(p), ALU.add), VE)
                ve_den[p] = VE.n
            for p, cb in enumerate(cbs):
                j = cb % 4
                rq = kv_r(j)[:, a:a + TC]
                P.wait("scalar", VE, ve_den[p])
                P.emit("scalar", lambda e, p=p: e.activation(S_T2(p), S_T2(p), AF.Ln))
                P.emit("scalar", lambda e, p=p: e.activation(S_T2(p), S_T2(p), AF.Exp, scale=-1.0))
                P.emit("scalar", lambda e, r=rq, p=p: e.activation(S_EU(p), r, AF.Exp, scale=-1.0))
                P.emit("scalar", lambda e, p=p: e.add(S_EU(p), S_EU(p), 1.0))
                P.emit("scalar", lambda e, p=p: e.activation(S_EU(p), S_EU(p), AF.Ln))
                P.emit("scalar", lambda e, p=p: e.activation(S_EU(p), S_EU(p), AF.Exp, scale=-1.0), SE)
                se_sig[p] = SE.n
            for p, cb in enumerate(cbs):
                P.wait("vector", SE, se_sig[p])
                P.emit("vector", lambda e, p=p: e.tensor_tensor(S_T1(p), S_T1(p), S_T2(p), ALU.mult))
                P.emit("vector", lambda e, p=p: e.tensor_tensor(S_T1(p), S_T1(p), S_EU(p), ALU.mult))
                P.emit("vector", lambda e, cb=cb, ga=ga, p=p: e.tensor_tensor(a_sl(cb, ga, ga + TC), S_T1(p), a_sl(cb, ga, ga + TC), ALU.add), VE)
        for cb in cbs:
            P.release(("kvr", cb % 4), VE)

    for ob in range(4):
        emit_prod_half(ob, 0)
    emit_scan_quad([0, 1, 2, 3], 0)
    prep_pump(12)
    for ob in range(4):
        emit_prod_half(ob, 1)
    emit_scan_quad([0, 1, 2, 3], 1)
    prep_pump(12)
    for ob in range(4, 8):
        emit_prod_half(ob, 0)
    emit_scan_quad([4, 5, 6, 7], 0)
    prep_pump(12)
    for ob in range(4, 8):
        emit_prod_half(ob, 1)
    emit_scan_quad([4, 5, 6, 7], 1)
    prep_pump(12)
    ve_scan = VE.n
    prep_pump(N_JOBS)

    if DEBUG:
        P.wait("sync", VE, ve_scan)
        dma("sync", OUT, dbg["t_x1"], A[:])

    KCSQ = AR[:, 0:8192].bitcast(BF16)
    H2 = lambda cb, a, b: AR[:, 8192 + cb * 512 + a: 8192 + cb * 512 + b]
    RCT = AR[:, 12288:12800]
    VVT = AR[:, 12800:13312]
    TROW = SCR[0:1, 0:512]

    P.use("tensor", ("bank", ST))
    P.wait("tensor", VE, ve_scan)
    if DEBUG:
        P.wait("tensor", OUT, OUT.n)
    for cb in range(NCB):
        P.emit("tensor", lambda e, cb=cb: e.matmul(PS[ST][0:1, 0:1], ONECOL[:], A3[:, cb, 2047:2048],
                                                   start=(cb == 0), stop=(cb == 7)), TE)
    P.wait("scalar", VE, ve_scan)
    P.emit("scalar", lambda e: e.activation(H2P[:], A3[:, 0:8, 2047:2048].rearrange("p a b -> p (a b)"), AF.Square), SE)
    se_wsq = SE.n
    P.use("tensor", ("bank", SP2))
    P.wait("tensor", SE, se_wsq)
    for cb in range(NCB):
        P.emit("tensor", lambda e, cb=cb: e.matmul(PS[SP2][0:1, 0:1], ONECOL[:], H2P[:, cb:cb + 1],
                                                   start=(cb == 0), stop=(cb == 7)), TE)
    t_wssq = TE.n
    P.use("scalar", ("rows",))
    P.wait("scalar", TE, t_wssq)
    P.emit("scalar", lambda e: e.copy(ROWS[0:1, 0:1], PS[ST][0:1, 0:1]))
    P.emit("scalar", lambda e: e.copy(ROWS[0:1, 512:513], PS[SP2][0:1, 0:1]), SE)
    se_wrow = SE.n
    P.release(("bank", ST), SE)
    P.release(("bank", SP2), SE)
    P.use("vector", ("scr_p3",))
    P.wait("vector", SE, se_wrow)
    P.emit("vector", lambda e: e.tensor_scalar(rows_mu[:, 0:1], rows_mu[:, 0:1], 1.0 / C, None, ALU.mult, ALU.bypass))
    P.emit("vector", lambda e: e.tensor_scalar(rows_rs[:, 0:1], rows_rs[:, 0:1], 1.0 / C, None, ALU.mult, ALU.bypass))
    P.drain("vector")
    P.emit("vector", lambda e: e.tensor_scalar(TROW[:, 0:1], rows_mu[:, 0:1], rows_mu[:, 0:1], None, ALU.mult, ALU.bypass))
    P.drain("vector")
    P.emit("vector", lambda e: e.tensor_scalar(rows_rs[:, 0:1], rows_rs[:, 0:1], TROW[:, 0:1], LN_EPS, ALU.subtract, ALU.add), VE)
    ve_wv = VE.n
    P.wait("scalar", VE, ve_wv)
    P.emit("scalar", lambda e: e.activation(rows_rs[:, 0:1], rows_rs[:, 0:1], AF.Ln))
    P.drain("scalar")
    P.emit("scalar", lambda e: e.activation(rows_rs[:, 0:1], rows_rs[:, 0:1], AF.Exp, scale=-0.5), SE)
    se_wst = SE.n
    P.use("tensor", ("bank", SP2))
    P.wait("tensor", SE, se_wst)
    P.emit("tensor", lambda e: e.matmul(PS[SP2][:, 0:1], ONEROW[:], rows_mu[:, 0:1], start=True, stop=True))
    P.emit("tensor", lambda e: e.matmul(PS[SP2][:, 1:2], ONEROW[:], rows_rs[:, 0:1], start=True, stop=True), TE)
    t_wbc = TE.n
    P.release(("rows",), TE)
    P.wait("vector", TE, t_wbc)
    for cb in range(NCB):
        P.emit("vector", lambda e, cb=cb: e.tensor_tensor(H2P[:, cb:cb + 1], A3[:, cb, 2047:2048], PS[SP2][:, 0:1], ALU.subtract))
        P.drain("vector")
        P.emit("vector", lambda e, cb=cb: e.tensor_tensor(H2P[:, cb:cb + 1], H2P[:, cb:cb + 1], PS[SP2][:, 1:2], ALU.mult))
        P.drain("vector")
        P.emit("vector", lambda e, cb=cb: e.tensor_scalar(H2P[:, cb:cb + 1], H2P[:, cb:cb + 1], vcol(V_L2W, cb), vcol(V_L2B, cb), ALU.mult, ALU.add), VE)
        P.drain("vector")
    ve_h2p = VE.n
    P.release(("bank", SP2), VE)

    prev_tau_ve = ve_h2p
    ve_xxx2_tau = {}
    for tau in range(NTILE):
        t0 = tau * TN
        P.use("tensor", ("bank", ST))
        for cb in range(NCB):
            P.emit("tensor", lambda e, cb=cb, t0=t0: e.matmul(PS[ST][0:1, 0:512], ONECOL[:], a_sl(cb, t0, t0 + TN),
                                                              start=(cb == 0), stop=(cb == 7)), TE)
        t_sum = TE.n
        P.use("scalar", ("rows",))
        P.wait("scalar", TE, t_sum)
        P.emit("scalar", lambda e: e.copy(rows_mu, PS[ST][0:1, 0:512]), SE)
        P.release(("bank", ST), SE)
        P.use("tensor", ("bank", SP2))
        for cb in range(NCB):
            sq = cb % 2
            P.use("scalar", ("h2sq", sq))
            P.wait("scalar", VE, prev_tau_ve)
            P.emit("scalar", lambda e, cb=cb, t0=t0, s=sq: e.activation(H2(s, 0, TN), a_sl(cb, t0, t0 + TN), AF.Square), SE)
            se_q = SE.n
            P.wait("tensor", SE, se_q)
            P.emit("tensor", lambda e, cb=cb, s=sq: e.matmul(PS[SP2][0:1, 0:512], ONECOL[:], H2(s, 0, TN),
                                                             start=(cb == 0), stop=(cb == 7)), TE)
            P.release(("h2sq", sq), TE)
        t_ssq = TE.n
        P.wait("scalar", TE, t_ssq)
        P.emit("scalar", lambda e: e.copy(rows_rs, PS[SP2][0:1, 0:512]), SE)
        se_ssq = SE.n
        P.release(("bank", SP2), SE)
        P.use("vector", ("scr_p3",))
        P.wait("vector", SE, se_ssq)
        P.emit("vector", lambda e: e.tensor_scalar(rows_mu, rows_mu, 1.0 / C, None, ALU.mult, ALU.bypass))
        P.emit("vector", lambda e: e.tensor_scalar(rows_rs, rows_rs, 1.0 / C, None, ALU.mult, ALU.bypass))
        P.emit("vector", lambda e: e.tensor_tensor(TROW, rows_mu, rows_mu, ALU.mult))
        P.emit("vector", lambda e: e.tensor_tensor(rows_rs, rows_rs, TROW, ALU.subtract))
        P.emit("vector", lambda e: e.tensor_scalar(rows_rs, rows_rs, LN_EPS, None, ALU.add, ALU.bypass), VE)
        ve_v2 = VE.n
        P.release(("scr_p3",), VE)
        P.wait("scalar", VE, ve_v2)
        P.emit("scalar", lambda e: e.activation(rows_rs, rows_rs, AF.Ln))
        P.emit("scalar", lambda e: e.activation(rows_rs, rows_rs, AF.Exp, scale=-0.5), SE)
        se_st2 = SE.n
        P.use("tensor", ("bank", MM2))
        P.use("tensor", ("bank", MM3))
        P.wait("tensor", SE, se_st2)
        P.emit("tensor", lambda e: e.matmul(PS[MM2][:], ONEROW[:], rows_mu, start=True, stop=True), TE)
        P.emit("tensor", lambda e: e.matmul(PS[MM3][:], ONEROW[:], rows_rs, start=True, stop=True), TE)
        t_bc2 = TE.n
        P.release(("rows",), TE)
        P.wait("vector", TE, t_bc2)
        for cb in range(NCB):
            P.emit("vector", lambda e, cb=cb, t0=t0: e.tensor_tensor(H2(cb, 0, TN), a_sl(cb, t0, t0 + TN), PS[MM2][:], ALU.subtract))
            P.emit("vector", lambda e, cb=cb: e.tensor_tensor(H2(cb, 0, TN), H2(cb, 0, TN), PS[MM3][:], ALU.mult))
            P.emit("vector", lambda e, cb=cb: e.tensor_scalar(H2(cb, 0, TN), H2(cb, 0, TN), vcol(V_L2W, cb), vcol(V_L2B, cb), ALU.mult, ALU.add))
            P.emit("vector", lambda e, cb=cb: e.tensor_scalar(VVT, H2(cb, 0, TN), vcol(V_1MTS, cb), None, ALU.mult, ALU.bypass))
            P.emit("vector", lambda e, cb=cb, t0=t0: e.scalar_tensor_tensor(
                xx_sl(cb, t0, t0 + 1), H2P[:, cb:cb + 1], vcol(V_TS, cb), VVT[:, 0:1], ALU.mult, ALU.add))
            P.emit("vector", lambda e, cb=cb, t0=t0: e.scalar_tensor_tensor(
                xx_sl(cb, t0 + 1, t0 + TN), H2(cb, 0, TN - 1), vcol(V_TS, cb), VVT[:, 1:TN], ALU.mult, ALU.add), VE)
        ve_xxx2_tau[tau] = VE.n
        P.release(("bank", MM2), VE)
        P.release(("bank", MM3), VE)
        for cb in range(NCB):
            P.emit("vector", lambda e, cb=cb: e.tensor_copy(H2P[:, cb:cb + 1], H2(cb, TN - 1, TN)), VE)
        prev_tau_ve = VE.n
        prep_pump(4)

    for tau in range(NTILE):
        t0 = tau * TN
        ve_xxx2 = ve_xxx2_tau[tau]
        for hp in range(16):
            P.waitp("sync", need_job(JB_WCK + 2 * hp))
            P.waitp("sync", need_job(JB_WCK + 2 * hp + 1))
            tile2, retok, wsls = w_reload2(wckT_s[2 * hp:2 * hp + 2])
            for sub in range(2):
                hb = 2 * hp + sub
                bank = MM0 if hb % 2 == 0 else MM1
                P.use("tensor", ("bank", bank))
                if sub == 0:
                    P.waitp("tensor", retok)
                    P.wait("tensor", VE, ve_xxx2)
                for kc in range(NCB):
                    P.emit("tensor", lambda e, b=bank, kc=kc, t=tile2, sub=sub, t0=t0: e.matmul(
                        PS[b][:], t[:, sub * 1024 + kc * 128: sub * 1024 + (kc + 1) * 128], xx_sl(kc, t0, t0 + TN),
                        start=(kc == 0), stop=(kc == 7)), TE)
                t_kc = TE.n
                if sub == 1:
                    for wsl in wsls:
                        P.release(("wt", wsl), TE)
                P.use("scalar", ("scanbuf",))
                P.wait("scalar", TE, t_kc)
                P.emit("scalar", lambda e, b=bank: e.activation(SCR[:], PS[b][:], AF.Relu), SE)
                P.release(("bank", bank), SE)
                P.emit("scalar", lambda e, hb=hb: e.activation(KCSQ[:, hb * 512:(hb + 1) * 512], SCR[:], AF.Square), SE)
                P.release(("scanbuf",), SE)
        se_kcsq = SE.n
        prev_s_ve = 0
        for cb in range(NCB):
            P.waitp("sync", need_job(JB_WCR + cb))
            tile, retok, wsl = w_reload(wcrT_s[cb])
            P.use("tensor", ("bank", MM2))
            P.waitp("tensor", retok)
            for kc in range(NCB):
                P.emit("tensor", lambda e, kc=kc, t=tile, t0=t0: e.matmul(
                    PS[MM2][:], t[:, kc * 128:(kc + 1) * 128], xx_sl(kc, t0, t0 + TN),
                    start=(kc == 0), stop=(kc == 7)), TE)
            t_rc = TE.n
            P.release(("wt", wsl), TE)
            P.wait("scalar", TE, t_rc)
            P.wait("scalar", VE, prev_s_ve)
            P.emit("scalar", lambda e: e.activation(RCT, PS[MM2][:], AF.Exp, scale=-1.0), SE)
            se_rc1 = SE.n
            P.release(("bank", MM2), SE)
            P.emit("scalar", lambda e: e.add(RCT, RCT, 1.0))
            P.emit("scalar", lambda e: e.activation(RCT, RCT, AF.Ln))
            P.emit("scalar", lambda e: e.activation(RCT, RCT, AF.Exp, scale=-1.0), SE)
            se_rc = SE.n
            for ch in range(4):
                P.waitp("sync", need_job(JB_WCV + cb * 4 + ch))
            tile4, retok4, wsls4 = w_reload4(wcvT_s[cb])
            P.use("tensor", ("bank", MM3))
            P.wait("tensor", SE, se_kcsq)
            P.waitp("tensor", retok4)
            for hb in range(32):
                P.emit("tensor", lambda e, t=tile4, hb=hb: e.matmul(
                    PS[MM3][:], t[:, hb * 128:(hb + 1) * 128], KCSQ[:, hb * 512:(hb + 1) * 512],
                    start=(hb == 0), stop=(hb == 31)), TE)
            t_vv = TE.n
            for wsl2 in wsls4:
                P.release(("wt", wsl2), TE)
            P.wait("vector", TE, t_vv)
            P.wait("vector", SE, se_rc)
            P.emit("vector", lambda e: e.tensor_tensor(VVT, RCT, PS[MM3][:], ALU.mult), VE)
            P.emit("vector", lambda e, cb=cb, t0=t0: e.tensor_tensor(a_sl(cb, t0, t0 + TN), VVT, a_sl(cb, t0, t0 + TN), ALU.add), VE)
            prev_s_ve = VE.n
            P.release(("bank", MM3), VE)
        prep_pump(6)
    ve_p3 = VE.n
    te_p3 = TE.n
    se_p3 = SE.n

    if DEBUG:
        P.wait("sync", VE, ve_p3)
        dma("sync", OUT, dbg["t_s"], A[:])

    GPRE = AR[:, 0:2048].bitcast(BF16)
    WS1T = AR[:, 2048:4096].bitcast(BF16)
    WS3T = AR[:, 4096:6144].bitcast(BF16)
    WS2T = AR[:, 6144:8192].bitcast(BF16)
    ZMID = AR[:, 8192:9216].bitcast(BF16)
    SILU = AR[:, 9216:9728]
    HST = lambda r: AR[:, 9728 + r * 1024: 10752 + r * 1024].bitcast(BF16)
    SCO = AR[:, 9728:9984]
    CMF = AR[0:16, 11776:12800].bitcast(BF16)

    for tb in range(NTB):
        bank = MM0 if tb % 2 == 0 else MM1
        P.use("tensor", ("bank", bank))
        if tb == 0:
            P.wait("tensor", SE, se_gate)
        for kc in range(NCB):
            P.emit("tensor", lambda e, b=bank, kc=kc, tb=tb: e.matmul(
                PS[b][:, 0:16], XNB3[:, kc, tb * 128:(tb + 1) * 128],
                GATET[:, kc * 16:(kc + 1) * 16], start=(kc == 0), stop=(kc == 7)), TE)
        t_sc = TE.n
        P.wait("scalar", TE, t_sc)
        if tb == 0:
            P.wait("scalar", VE, ve_p3)
        stb = SCO[:, tb * 16:(tb + 1) * 16]
        P.emit("scalar", lambda e, b=bank, stb=stb: e.activation(stb, PS[b][:, 0:16], AF.Exp, scale=-1.0), SE)
        se_s1 = SE.n
        P.release(("bank", bank), SE)
        P.drain("scalar")
        P.emit("scalar", lambda e, stb=stb: e.add(stb, stb, 1.0))
        P.drain("scalar")
        P.emit("scalar", lambda e, stb=stb: e.activation(stb, stb, AF.Ln))
        P.drain("scalar")
        P.emit("scalar", lambda e, stb=stb: e.activation(stb, stb, AF.Exp, scale=-1.0), SE)
        if DEBUG:
            P.emit("scalar", lambda e, tb=tb, stb=stb: e.copy(WROW[:].bitcast(F32)[:, tb * 16:(tb + 1) * 16], stb), SE)
        se_sco = SE.n
        sc = lambda x, y: SCR[:, x:y]
        P.use("vector", ("scr_p4",))
        P.wait("vector", SE, se_sco)
        P.emit("vector", lambda e, stb=stb: e.tensor_tensor(sc(96, 112), stb, GBR[:], ALU.add))
        P.drain("vector")
        P.emit("vector", lambda e: e.tensor_reduce(sc(0, 4), sc(96, 112).rearrange("p (a b) -> p a b", b=4), AX.X, ALU.max))
        P.drain("vector")
        for G in range(4):
            P.emit("vector", lambda e, G=G: e.tensor_scalar(sc(4, 8), sc(0, 4), sc(G, G + 1), 0.0, ALU.is_gt, ALU.add, accum_out=sc(8 + G, 9 + G)))
        P.drain("vector")
        for G in range(4):
            P.emit("vector", lambda e, G=G: e.tensor_scalar(sc(12 + G, 13 + G), sc(8 + G, 9 + G), 2.0, None, ALU.is_lt, ALU.bypass))
        P.drain("vector")
        for G in range(4):
            P.emit("vector", lambda e, G=G: e.tensor_scalar(sc(16 + G, 17 + G), sc(12 + G, 13 + G), 4.0, None, ALU.mult, ALU.bypass))
        P.drain("vector")
        for G in range(4):
            P.emit("vector", lambda e, G=G: e.tensor_scalar(sc(96 + 4 * G, 100 + 4 * G), sc(96 + 4 * G, 100 + 4 * G), sc(16 + G, 17 + G), -4.0, ALU.add, ALU.add))
        P.drain("vector")
        for ee in range(E):
            P.emit("vector", lambda e, ee=ee: e.tensor_scalar(sc(32, 48), sc(96, 112), sc(96 + ee, 97 + ee), 0.0, ALU.is_gt, ALU.add, accum_out=sc(48 + ee, 49 + ee)))
        P.drain("vector")
        for ee in range(E):
            P.emit("vector", lambda e, ee=ee: e.tensor_scalar(sc(64 + ee, 65 + ee), sc(48 + ee, 49 + ee), 4.0, None, ALU.is_lt, ALU.bypass))
        P.drain("vector")
        for ee in range(E):
            P.emit("vector", lambda e, ee=ee, stb=stb: e.tensor_tensor(
                stb[:, ee:ee + 1], sc(64 + ee, 65 + ee), stb[:, ee:ee + 1], ALU.mult), VE)
        P.drain("vector")
        ve_comb = VE.n
        P.release(("scr_p4",), VE)
        grp = tb % 4
        bank2 = T0 if (tb // 4) % 2 == 0 else T1
        if grp == 0:
            P.use("tensor", ("bank", bank2))
        P.wait("tensor", VE, ve_comb)
        P.emit("tensor", lambda e, stb=stb, b=bank2, grp=grp: e.matmul(
            PS[b][0:16, grp * 128:(grp + 1) * 128], stb, ID32[:],
            is_transpose=True, start=(grp == 0), stop=(grp == 3)), TE)
        if grp == 3:
            t_cmt = TE.n
            P.wait("scalar", TE, t_cmt)
            P.emit("scalar", lambda e, tb=tb, b=bank2: e.copy(CMF[:, (tb - 3) * 128:(tb + 1) * 128], PS[b][0:16, 0:512]), SE)
            P.release(("bank", bank2), SE)
    se_cmf = SE.n

    if DEBUG:
        P.wait("sync", SE, se_cmf)
        dma("sync", OUT, dbg["t_cmf"], CMF[:])
        dma("sync", OUT, dbg["t_sco"], WROW[:].bitcast(F32)[:, 0:NTB * E])
        dma("sync", OUT, dbg["t_xnb"], XNB[:])
        dma("sync", OUT, dbg["t_gt"], GATET[:])
        dma("sync", OUT, dbg["t_gbr"], GBR[:])

    W3SL = lambda ib: AR[:, ib * 512:(ib + 1) * 512].bitcast(BF16)
    hst_r = [0]
    ve_he = 0
    for e_i in range(E):
        tiles1 = []
        tiles3 = []
        for ib in range(4):
            P.waitp("sync", need_job(JB_W1 + e_i * 8 + ib))
            tiles1.append(w_reload(w1T_s[e_i][ib]))
        for ib in range(4):
            P.waitp("sync", need_job(JB_W1 + e_i * 8 + 4 + ib))
            P.use("sync", ("w3sl", ib))
            if e_i == 0 and ib == 0:
                P.wait("sync", TE, te_p3)
                P.wait("sync", VE, ve_p3)
            n3 = dma("sync", S_W3, W3SL(ib), w3T_s[e_i][ib])
            tiles3.append((W3SL(ib), (S_W3, n3), ib))
        last_te_tiles = None
        for tau in range(NTILE):
            t0 = tau * TN
            P.use("tensor", ("bank", SP2))
            if e_i == 0 and tau == 0:
                P.wait("tensor", SE, se_cmf)
                if DEBUG:
                    P.wait("tensor", OUT, OUT.n)
            P.emit("tensor", lambda e, e_i=e_i, tau=tau: e.matmul(
                PS[SP2][:], BAS[:, e_i * 128:(e_i + 1) * 128], CMF[:, tau * 512:(tau + 1) * 512],
                start=True, stop=True), TE)
            t_cb = TE.n
            r = hst_r[0] % 2
            for ib in range(4):
                b1 = MM0 if ib % 2 == 0 else MM2
                b3 = MM1 if ib % 2 == 0 else MM3
                P.use("tensor", ("bank", b1))
                P.use("tensor", ("bank", b3))
                P.waitp("tensor", tiles1[ib][1])
                P.waitp("tensor", tiles3[ib][1])
                for kc in range(NCB):
                    P.emit("tensor", lambda e, b=b1, t=tiles1[ib][0], kc=kc, t0=t0: e.matmul(
                        PS[b][:], t[:, kc * 128:(kc + 1) * 128], xnb_sl(kc, t0, t0 + TN),
                        start=(kc == 0), stop=(kc == 7)), TE)
                for kc in range(NCB):
                    P.emit("tensor", lambda e, b=b3, t=tiles3[ib][0], kc=kc, t0=t0: e.matmul(
                        PS[b][:], t[:, kc * 128:(kc + 1) * 128], xnb_sl(kc, t0, t0 + TN),
                        start=(kc == 0), stop=(kc == 7)), TE)
                t_h13 = TE.n
                P.use("scalar", ("silu",))
                P.wait("scalar", TE, t_h13)
                if e_i == 0 and tau == 0 and ib == 0:
                    P.wait("scalar", VE, ve_p3)
                P.emit("scalar", lambda e, b=b1: e.activation(SILU, PS[b][:], AF.Silu), SE)
                se_si = SE.n
                P.release(("bank", b1), SE)
                if ib == 0:
                    P.use("vector", ("hst", r))
                P.wait("vector", SE, se_si)
                P.wait("vector", TE, t_cb)
                P.emit("vector", lambda e, b=b3: e.tensor_tensor(SILU, SILU, PS[b][:], ALU.mult), VE)
                P.release(("bank", b3), VE)
                P.emit("vector", lambda e, r=r, ib=ib: e.tensor_tensor(HST(r)[:, ib * 512:(ib + 1) * 512], SILU, PS[SP2][:], ALU.mult), VE)
                ve_he = VE.n
                P.release(("silu",), VE)
            P.release(("bank", SP2), VE)
            P.wait("sync", VE, ve_he)
            dma("sync", S_HS[r], H_s[tau][e_i], HST(r))
            P.release(("hst", r), S_HS[r])
            hst_r[0] += 1
            last_te_tiles = TE.n
        for tl, _, wsl in tiles1:
            P.release(("wt", wsl), TE, last_te_tiles)
        for tl, _, ib in tiles3:
            P.release(("w3sl", ib), TE, last_te_tiles)
    f1_hs_toks = [(S_HS[0], S_HS[0].n), (S_HS[1], S_HS[1].n)]

    P.wait("sync", TE, te_p3)
    P.wait("sync", VE, ve_p3)
    for ib in range(4):
        P.waitp("sync", need_job(JB_WS1 + ib))
        dma("sync", S_WS, WS1T[:, ib * 1024:(ib + 1) * 1024], ws1T_s[ib])
    for ib in range(4):
        P.waitp("sync", need_job(JB_WS3 + ib))
        dma("sync", S_WS, WS3T[:, ib * 1024:(ib + 1) * 1024], ws3T_s[ib])
    for db in range(8):
        P.waitp("sync", need_job(JB_WS2 + db))
        dma("sync", S_WS, WS2T[:, db * 512:(db + 1) * 512], ws2T_s[db])
    re_wsz = S_WS.n

    YB = [MM0, MM1, MM2, MM3, T0, T1, ST, SP2]
    for tau in range(NTILE):
        t0 = tau * TN
        for ib in range(4):
            P.use("tensor", ("bank", MM2))
            P.use("tensor", ("bank", MM3))
            P.wait("tensor", S_WS, re_wsz)
            if tau == 0 and ib == 0:
                P.wait("tensor", VE, ve_he)
            for kc in range(NCB):
                P.emit("tensor", lambda e, ib=ib, kc=kc, t0=t0: e.matmul(
                    PS[MM2][:], WS1T[:, ib * 1024 + kc * 128: ib * 1024 + (kc + 1) * 128], xnb_sl(kc, t0, t0 + TN),
                    start=(kc == 0), stop=(kc == 7)), TE)
            for kc in range(NCB):
                P.emit("tensor", lambda e, ib=ib, kc=kc, t0=t0: e.matmul(
                    PS[MM3][:], WS3T[:, ib * 1024 + kc * 128: ib * 1024 + (kc + 1) * 128], xnb_sl(kc, t0, t0 + TN),
                    start=(kc == 0), stop=(kc == 7)), TE)
            t_z = TE.n
            P.use("scalar", ("silu",))
            P.wait("scalar", TE, t_z)
            P.emit("scalar", lambda e: e.activation(SILU, PS[MM2][:], AF.Silu), SE)
            se_zs = SE.n
            P.release(("bank", MM2), SE)
            P.use("vector", ("zmid", ib))
            P.wait("vector", SE, se_zs)
            P.emit("vector", lambda e, ib=ib: e.tensor_tensor(ZMID[:, ib * 512:(ib + 1) * 512], SILU, PS[MM3][:], ALU.mult), VE)
            P.release(("bank", MM3), VE)
            P.release(("silu",), VE)
        ve_zmid = VE.n
        h_re = {}
        for b in YB:
            P.use("tensor", ("bank", b))
        for grp in range(2):
            for esub in range(8):
                e_i = grp * 8 + esub
                P.use("sync", ("xxh", esub))
                if tau == 0 and grp == 0 and esub == 0:
                    P.wait("sync", TE, te_p3)
                    for tok in f1_hs_toks:
                        P.waitp("sync", tok)
                h_re[e_i] = (S_H, dma("sync", S_H, XX3b[:, esub, :], H_s[tau][e_i]))
            w2tiles = []
            for db in range(8):
                for ep in range(4):
                    e0 = grp * 8 + 2 * ep
                    P.waitp("sync", need_job(JB_W2 + e0 * 8 + db))
                    P.waitp("sync", need_job(JB_W2 + (e0 + 1) * 8 + db))
                    tl, retok, wsl = w_reload(w2T_s[db][e0:e0 + 2].rearrange("e p n -> p e n"))
                    P.waitp("tensor", retok)
                    for sub in range(2):
                        e_i = e0 + sub
                        P.waitp("tensor", h_re[e_i])
                        if grp == 1 and db == 0 and ep == 0 and sub == 0:
                            pass
                        for ikc in range(4):
                            first = (grp == 0 and ep == 0 and sub == 0 and ikc == 0)
                            P.emit("tensor", lambda e, b=YB[db], t=tl, sub=sub, ikc=ikc, esub=e_i % 8, first=first: e.matmul(
                                PS[b][:], t[:, sub * 512 + ikc * 128: sub * 512 + (ikc + 1) * 128],
                                XX3b[:, esub, ikc * 512:(ikc + 1) * 512],
                                start=first, stop=False), TE)
                    P.release(("wt", wsl), TE)
            if grp == 0:
                te_grp0 = TE.n
                for esub in range(8):
                    P.release(("xxh", esub), TE, te_grp0)
        P.wait("tensor", VE, ve_zmid)
        for db in range(8):
            for ikc in range(4):
                P.emit("tensor", lambda e, b=YB[db], db=db, ikc=ikc: e.matmul(
                    PS[b][:], WS2T[:, db * 512 + ikc * 128: db * 512 + (ikc + 1) * 128],
                    ZMID[:, ikc * 512:(ikc + 1) * 512], start=False, stop=(ikc == 3)), TE)
        t_y = TE.n
        for esub in range(8):
            P.release(("xxh", esub), TE, t_y)
        for ib in range(4):
            P.release(("zmid", ib), TE, t_y)
        for db in range(8):
            P.use("scalar", ("gpre", db))
            P.wait("scalar", TE, t_y)
            if tau == 0 and db == 0:
                pass
            P.emit("scalar", lambda e, db=db, b=YB[db]: e.copy(GPRE[:, db * 512:(db + 1) * 512], PS[b][:]), SE)
            P.release(("bank", YB[db]), SE)
        se_g = SE.n
        P.wait("sync", SE, se_g)
        if DEBUG:
            dma("sync", OUT, dbg["t_g"][tau], GPRE[:])
        n_gs = dma("sync", S_GS, g_s[tau], GPRE[:])
        for db in range(8):
            P.release(("gpre", db), S_GS, S_GS.n)
    spl_g = S_GS.n
    te_f2 = TE.n
    ve_f2 = VE.n

    STM = lambda r: AR[:, r * 1024: (r + 1) * 1024]
    XT2 = lambda r: AR[:, 2048 + r * 1024: 2048 + (r + 1) * 1024]
    OTB = lambda r: AR[:, 4096 + r * 1024: 4096 + (r + 1) * 1024]
    GTL = AR[:, 6144:8192].bitcast(BF16)
    GSL = lambda r: AR[:, 8192 + r * 1024: 8192 + (r + 1) * 1024]

    first_stm = [True]
    first_xt2 = [True]
    first_gs = [True]
    for tau in range(NTILE):
        P.use("sync", ("gtile",))
        if tau == 0:
            P.wait("sync", S_GS, spl_g)
            P.wait("sync", TE, te_f2)
        n_g = dma("sync", S_GT, GTL[:], g_s[tau])
        for tb4 in range(4):
            tb = tau * 4 + tb4
            r = tb % 2
            P.use("sync", ("xt2", r))
            if first_xt2[0]:
                P.wait("sync", TE, te_f2)
                first_xt2[0] = False
            n_x = dma("sync", S_X2[r], XT2(r), x_d[tb * 128:(tb + 1) * 128, :])
            P.use("scalar", ("stm", r))
            for g in range(2):
                bank = T0 if g == 0 else T1
                P.use("tensor", ("bank", bank))
                for j in range(4):
                    cb = g * 4 + j
                    P.emit("tensor", lambda e, cb=cb, j=j, b=bank, tb=tb: e.matmul(
                        PS[b][:, j * 128:(j + 1) * 128], A3[:, cb, tb * 128:(tb + 1) * 128],
                        ID32[:], is_transpose=True, start=(j == 0), stop=(j == 3)), TE)
                t_tr = TE.n
                P.wait("scalar", TE, t_tr)
                if first_stm[0]:
                    P.wait("scalar", S_GS, spl_g)
                    first_stm[0] = False
                P.emit("scalar", lambda e, r=r, g=g, b=bank: e.copy(STM(r)[:, g * 512:(g + 1) * 512], PS[b][:]), SE)
                P.release(("bank", bank), SE)
            bank = T0 if tb % 2 == 0 else T1
            P.use("tensor", ("bank", bank))
            P.wait("tensor", S_GT, n_g)
            for cb in range(NCB):
                P.emit("tensor", lambda e, cb=cb, b=bank, tb4=tb4: e.matmul(
                    PS[b][:].bitcast(BF16)[:, cb * 128:(cb + 1) * 128],
                    GTL[:, cb * 512 + tb4 * 128: cb * 512 + tb4 * 128 + 128],
                    IDB[:], is_transpose=True, start=(cb == 0), stop=(cb == 7)), TE)
            t_gt = TE.n
            P.use("scalar", ("gsil", r))
            P.wait("scalar", TE, t_gt)
            if first_gs[0]:
                for tok in f1_hs_toks:
                    P.waitp("scalar", tok)
                P.wait("scalar", VE, ve_f2)
                first_gs[0] = False
            P.emit("scalar", lambda e, r=r, b=bank: e.activation(GSL(r), PS[b][:].bitcast(BF16)[:, 0:1024], AF.Silu), SE)
            se_gs = SE.n
            P.release(("bank", bank), SE)
            P.use("vector", ("ot", r))
            P.wait("vector", SE, se_gs)
            P.wait("vector", S_X2[r], n_x)
            P.emit("vector", lambda e, r=r: e.tensor_tensor(STM(r), STM(r), GSL(r), ALU.mult))
            P.emit("vector", lambda e, r=r: e.tensor_tensor(OTB(r), STM(r), XT2(r), ALU.add), VE)
            ve_ot = VE.n
            P.release(("stm", r), VE)
            P.release(("gsil", r), VE)
            P.release(("xt2", r), VE)
            P.wait("sync", VE, ve_ot)
            dma("sync", S_OT[r], out_d[tb * 128:(tb + 1) * 128, :], OTB(r))
            P.release(("ot", r), S_OT[r])
        P.release(("gtile",), TE)

    assert prep_state["next"] == N_JOBS, f"{prep_state['next']}/{N_JOBS} prep jobs"

    P.wait("sync", S_OT[0], S_OT[0].n)
    P.wait("sync", S_OT[1], S_OT[1].n)
    if DEBUG:
        P.wait("sync", OUT, OUT.n)

    with nc.Block() as block:
        @block.sync
        def _(e):
            for f in P.steps["sync"]:
                f(e)

        @block.gpsimd
        def _(e):
            for f in P.steps["gpsimd"]:
                f(e)

        @block.tensor
        def _(e):
            for f in P.steps["tensor"]:
                f(e)

        @block.scalar
        def _(e):
            for f in P.steps["scalar"]:
                f(e)

        @block.vector
        def _(e):
            for f in P.steps["vector"]:
                f(e)

    es.close()
    return nc


_NC_CACHE = None


def kernel(**inputs):
    global _NC_CACHE
    if _NC_CACHE is None:
        _NC_CACHE = build_nc()
    nc = _NC_CACHE

    x = np.ascontiguousarray(np.asarray(inputs["L0_attn_x"], dtype=np.float32))
    _ = inputs["L0_attn_time_decay"]
    basis = np.zeros((E, E * 128), dtype=ml_dtypes.bfloat16)
    for e_i in range(E):
        basis[e_i, e_i * 128:(e_i + 1) * 128] = 1.0
    common = {
        "Wk": np.ascontiguousarray(np.asarray(inputs["L0_attn_Wk"], np.float32)),
        "Wv": np.ascontiguousarray(np.asarray(inputs["L0_attn_Wv"], np.float32)),
        "Wr": np.ascontiguousarray(np.asarray(inputs["L0_attn_Wr"], np.float32)),
        "Wck": np.ascontiguousarray(np.asarray(inputs["L0_attn_Wck"], np.float32)),
        "Wcv": np.ascontiguousarray(np.asarray(inputs["L0_attn_Wcv"], np.float32)),
        "Wcr": np.ascontiguousarray(np.asarray(inputs["L0_attn_Wcr"], np.float32)),
        "ln1w": np.ascontiguousarray(np.asarray(inputs["L0_attn_ln1_w"], np.float32)),
        "ln1b": np.ascontiguousarray(np.asarray(inputs["L0_attn_ln1_b"], np.float32)),
        "ln2w": np.ascontiguousarray(np.asarray(inputs["L0_attn_ln2_w"], np.float32)),
        "ln2b": np.ascontiguousarray(np.asarray(inputs["L0_attn_ln2_b"], np.float32)),
        "tshift": np.ascontiguousarray(np.asarray(inputs["L0_attn_time_shift"], np.float32)),
        "tfirst": np.ascontiguousarray(np.asarray(inputs["L0_attn_time_first"], np.float32)),
        "gate_w": np.ascontiguousarray(np.asarray(inputs["L0_ffn_gate_w"], np.float32)),
        "gate_bias": np.ascontiguousarray(np.asarray(inputs["L0_ffn_gate_bias"], np.float32)),
        "w1": np.ascontiguousarray(np.asarray(inputs["L0_ffn_w1"], np.float32)),
        "w2": np.ascontiguousarray(np.asarray(inputs["L0_ffn_w2"], np.float32)),
        "w3": np.ascontiguousarray(np.asarray(inputs["L0_ffn_w3"], np.float32)),
        "ws1": np.ascontiguousarray(np.asarray(inputs["L0_ffn_ws1"], np.float32)),
        "ws2": np.ascontiguousarray(np.asarray(inputs["L0_ffn_ws2"], np.float32)),
        "ws3": np.ascontiguousarray(np.asarray(inputs["L0_ffn_ws3"], np.float32)),
        "identb": np.eye(128, dtype=ml_dtypes.bfloat16),
        "ident32": np.eye(128, dtype=np.float32),
        "basis": basis,
    }
    in_maps = [dict(common, x=np.ascontiguousarray(x[b])) for b in range(8)]
    res = run_bass_kernel_spmd(nc, in_maps, core_ids=list(range(8)))
    out = np.stack([np.asarray(res.results[b]["out"], np.float32) for b in range(8)], axis=0)
    return out


if __name__ == "__main__":
    build_nc()
    print("built ok")


# version 06
# speedup vs baseline: 1.0787x; 1.0787x over previous
import numpy as np
import ml_dtypes
from contextlib import ExitStack
import concourse.bass as bass
import concourse.mybir as mybir
from concourse.bass_utils import run_bass_kernel_spmd

F32 = mybir.dt.float32
BF16 = mybir.dt.bfloat16
AF = mybir.ActivationFunctionType
ALU = mybir.AluOpType
AX = mybir.AxisListType

T = 2048
C = 1024
NCB = 8
NTB = 16
NTILE = 4
TN = 512
E = 16
LN_EPS = 1e-5

DEBUG = False


class Sem:
    def __init__(self, h):
        self.h = h
        self.n = 0


class Plan:
    ENGINES = ("sync", "gpsimd", "tensor", "scalar", "vector")

    def __init__(self):
        self.steps = {e: [] for e in self.ENGINES}
        self.free = {}

    def wait(self, eng, sem, val):
        if val > sem.n:
            raise RuntimeError(f"wait for {val}, sem at {sem.n}")
        if val > 0:
            self.steps[eng].append(lambda e, s=sem.h, v=val: e.wait_ge(s, v))

    def emit(self, eng, build, sem=None, k=1):
        if sem is not None:
            sem.n += k
            self.steps[eng].append(lambda e, b=build, s=sem.h, kk=k: b(e).then_inc(s, kk))
            return sem.n
        self.steps[eng].append(build)
        return None

    def use(self, eng, key):
        for sem, val in self.free.get(key, ()):
            self.wait(eng, sem, val)

    def release(self, key, sem, val=None):
        self.free[key] = [(sem, sem.n if val is None else val)]

    def release_multi(self, key, toks):
        self.free[key] = list(toks)

    def drain(self, eng):
        self.steps[eng].append(lambda e: e.drain())

    def waitp(self, eng, tok):
        self.wait(eng, tok[0], tok[1])


def build_nc():
    nc = bass.Bass("TRN2", target_bir_lowering=False)

    def din(name, shape, dt=F32):
        return nc.dram_tensor(name, shape, dt, kind="ExternalInput").ap()

    x_d = din("x", [T, C])
    Wk_d = din("Wk", [C, C]); Wv_d = din("Wv", [C, C]); Wr_d = din("Wr", [C, C])
    Wck_d = din("Wck", [4096, C]); Wcv_d = din("Wcv", [C, 4096]); Wcr_d = din("Wcr", [C, C])
    ln1w_d = din("ln1w", [C]); ln1b_d = din("ln1b", [C])
    ln2w_d = din("ln2w", [C]); ln2b_d = din("ln2b", [C])
    ts_d = din("tshift", [1, 1, C]); u_d = din("tfirst", [C])
    gw_d = din("gate_w", [E, C]); gb_d = din("gate_bias", [E])
    w1_d = din("w1", [E, 512, C]); w2_d = din("w2", [E, C, 512]); w3_d = din("w3", [E, 512, C])
    ws1_d = din("ws1", [512, C]); ws2_d = din("ws2", [C, 512]); ws3_d = din("ws3", [512, C])
    idb_d = din("identb", [128, 128], BF16)
    id32_d = din("ident32", [128, 128], F32)
    bas_d = din("basis", [E, E * 128], BF16)

    out_d = nc.dram_tensor("out", [T, C], F32, kind="ExternalOutput").ap()

    def scrt(name, shape):
        return nc.dram_tensor(name, shape, BF16, kind="Internal").ap()

    wk_b = scrt("wk_b", [C, C]); wv_b = scrt("wv_b", [C, C]); wr_b = scrt("wr_b", [C, C])
    wck_b = scrt("wck_b", [4096, C]); wcr_b = scrt("wcr_b", [C, C]); wcv_b = scrt("wcv_b", [C, 4096])
    w1_b = scrt("w1_b", [E, 512, C]); w3_b = scrt("w3_b", [E, 512, C]); w2_b = scrt("w2_b", [E, C, 512])
    ws1_b = scrt("ws1_b", [512, C]); ws3_b = scrt("ws3_b", [512, C]); ws2_b = scrt("ws2_b", [C, 512])
    H_s = scrt("H_s", [NTILE, E, 128, 2048])
    g_s = scrt("g_s", [NTILE, 128, 4096])

    dbg = {}
    if DEBUG:
        dbg["t_p0"] = nc.dram_tensor("t_p0", [128, NCB * T], F32, kind="ExternalOutput").ap()
        dbg["t_xxx"] = nc.dram_tensor("t_xxx", [128, NCB * T], BF16, kind="ExternalOutput").ap()
        dbg["t_x1"] = nc.dram_tensor("t_x1", [128, NCB * T], F32, kind="ExternalOutput").ap()
        dbg["t_s"] = nc.dram_tensor("t_s", [128, NCB * T], F32, kind="ExternalOutput").ap()
        dbg["t_cmf"] = nc.dram_tensor("t_cmf", [16, E * 128], BF16, kind="ExternalOutput").ap()
        dbg["t_sco"] = nc.dram_tensor("t_sco", [128, NTB * E], F32, kind="ExternalOutput").ap()
        dbg["t_xnb"] = nc.dram_tensor("t_xnb", [128, NCB * T], BF16, kind="ExternalOutput").ap()
        dbg["t_gt"] = nc.dram_tensor("t_gt", [128, NCB * E], BF16, kind="ExternalOutput").ap()
        dbg["t_gbr"] = nc.dram_tensor("t_gbr", [128, E], F32, kind="ExternalOutput").ap()
        dbg["t_g"] = nc.dram_tensor("t_g", [NTILE, 128, 4096], BF16, kind="ExternalOutput").ap()
        dbg["t_kv"] = nc.dram_tensor("t_kv", [128, 5120], F32, kind="ExternalOutput").ap()

    es = ExitStack()
    sb = lambda name, shape, dt: es.enter_context(nc.sbuf_tensor(name, shape, dt))

    A = sb("A_stream", [128, NCB * T], F32)
    XNB = sb("A_xnb", [128, NCB * T], BF16)
    XX = sb("A_xxx", [128, NCB * T], BF16)
    AR = sb("AR", [128, 13376], F32)
    WT = sb("W_T", [128, 8 * 512], BF16)
    WT2 = sb("W_T2", [128, 4 * 1024], BF16)
    IDB = sb("identb_s", [128, 128], BF16)
    ID32 = sb("ident32_s", [128, 128], F32)
    ONECOL = sb("onecol", [128, 1], F32)
    ONEROW = sb("onerow", [1, 128], F32)
    ROWS = sb("rows", [1, 1024], F32)
    MU1C = sb("mu1c", [128, NTB], F32)
    RS1C = sb("rs1c", [128, NTB], F32)
    VEC = sb("veccols", [128, 10 * NCB], F32)
    CAR = sb("carries", [128, 2 * NCB], F32)
    H2P = sb("h2prev", [128, NCB], F32)
    GATET = sb("gateT", [128, NCB * E], BF16)
    GBR = sb("gbrow", [128, E], F32)
    GBS = sb("gbs", [1, E], F32)
    SCR = sb("scr", [128, 512], F32)

    V_U, V_TS, V_1PTS, V_NTS, V_1MTS, V_L1W, V_L1B, V_L2W, V_L2B, V_SP = range(10)

    def vcol(i, cb):
        return VEC[:, i * NCB + cb: i * NCB + cb + 1]

    def vcols(i):
        return VEC[:, i * NCB: (i + 1) * NCB]

    PS = [es.enter_context(nc.psum_tensor(f"ps{i}", [128, 512], F32)) for i in range(8)]
    MM0, MM1, MM2, MM3, T0, T1, ST, SP2 = range(8)

    mksem = lambda nm: Sem(es.enter_context(nc.semaphore(nm)))
    TE = mksem("TE"); SE = mksem("SE"); VE = mksem("VE"); OUT = mksem("OUT")
    S_CONST = mksem("S_CONST"); S_GW = mksem("S_GW"); S_MEM = mksem("S_MEM")
    S_XT = [mksem(f"S_XT{i}") for i in range(3)]
    S_WT = [mksem(f"S_WT{i}") for i in range(8)]
    S_WU = [mksem(f"S_WU{i}") for i in range(4)]
    S_CAST = {f: mksem(f"S_C_{f}") for f in
              ("wk", "wv", "wr", "wck", "wcr", "wcv", "w1", "w3", "w2", "ws")}
    S_W3 = mksem("S_W3"); S_HS = [mksem(f"S_HS{i}") for i in range(2)]
    S_H = mksem("S_H"); S_WS = mksem("S_WS"); S_GS = mksem("S_GS")
    S_GT = mksem("S_GT")
    S_X2 = [mksem(f"S_X2{i}") for i in range(2)]
    S_OT = [mksem(f"S_OT{i}") for i in range(2)]

    P = Plan()
    free = P.free

    A3 = A[:].rearrange("p (cb t) -> p cb t", t=T)
    XNB3 = XNB[:].rearrange("p (cb t) -> p cb t", t=T)
    XX3b = XX[:].rearrange("p (s n) -> p s n", n=2048)

    def a_sl(cb, a, b):
        return A[:, cb * T + a: cb * T + b]

    def xnb_sl(cb, a, b):
        return XNB[:, cb * T + a: cb * T + b]

    def xx_sl(cb, a, b):
        return XX[:, cb * T + a: cb * T + b]

    def dma(eng, sem, out, in_):
        return P.emit(eng, lambda e, o=out, i=in_: e.dma_start(out=o, in_=i), sem, 16)

    cast_done = {}

    def cast(fam, dst, src_):
        n = dma("gpsimd", S_CAST[fam], dst, src_)
        cast_done[fam] = (S_CAST[fam], n)

    cast("wk", wk_b, Wk_d)
    cast("wv", wv_b, Wv_d)
    cast("wr", wr_b, Wr_d)
    cast("wck", wck_b, Wck_d)
    cast("wcr", wcr_b, Wcr_d)
    cast("wcv", wcv_b, Wcv_d)
    cast("w1", w1_b.rearrange("e i c -> (e i) c"), w1_d.rearrange("e i c -> (e i) c"))
    cast("w3", w3_b.rearrange("e i c -> (e i) c"), w3_d.rearrange("e i c -> (e i) c"))
    cast("w2", w2_b.rearrange("e d i -> (e d) i"), w2_d.rearrange("e d i -> (e d) i"))
    dma("gpsimd", S_CAST["ws"], ws1_b, ws1_d)
    dma("gpsimd", S_CAST["ws"], ws3_b, ws3_d)
    n_ws = dma("gpsimd", S_CAST["ws"], ws2_b, ws2_d)
    cast_done["ws"] = (S_CAST["ws"], n_ws)

    wt_ptr = [0]
    wt2_ptr = [0]

    def t_load(src_rows):
        sl = wt_ptr[0] % 8
        wt_ptr[0] += 1
        dst = WT[:, sl * 512: sl * 512 + src_rows.shape[0]]
        P.use("sync", ("wt", sl))
        n = P.emit("sync", lambda e, o=dst, i=src_rows: e.dma_start(out=o, in_=i, transpose=True), S_WT[sl], 16)
        return dst, (S_WT[sl], n), ("wt", sl)

    def t_load2(src_rows):
        sl = wt2_ptr[0] % 4
        wt2_ptr[0] += 1
        dst = WT2[:, sl * 1024: sl * 1024 + src_rows.shape[0]]
        P.use("sync", ("wu", sl))
        n = P.emit("sync", lambda e, o=dst, i=src_rows: e.dma_start(out=o, in_=i, transpose=True), S_WU[sl], 16)
        return dst, (S_WU[sl], n), ("wu", sl)

    dma("sync", S_CONST, IDB[:], idb_d)
    dma("sync", S_CONST, ID32[:], id32_d)
    dma("sync", S_CONST, GBS[:], gb_d.rearrange("(a b) -> a b", a=1))
    VSTG = WT[:].bitcast(F32)[0:8, 0:768]
    vec_list = [(u_d, V_U), (ts_d.rearrange("x y c -> (x y c)"), V_TS), (ln1w_d, V_L1W),
                (ln1b_d, V_L1B), (ln2w_d, V_L2W), (ln2b_d, V_L2B)]
    for i, (src, _) in enumerate(vec_list):
        dma("sync", S_CONST, VSTG[:, i * 128:(i + 1) * 128], src.rearrange("(a b) -> a b", b=128))
    GWS = WT2[0:16, 0:1024]
    dma("gpsimd", S_GW, GWS, gw_d)
    n_in_const = S_CONST.n
    n_gw = S_GW.n

    P.emit("gpsimd", lambda e: e.memset(ONECOL[:], 1.0))
    P.emit("gpsimd", lambda e: e.memset(ONEROW[:], 1.0))
    P.emit("gpsimd", lambda e: e.memset(CAR[:, 0:NCB], 1.0))
    P.emit("gpsimd", lambda e: e.memset(CAR[:, NCB:2 * NCB], 0.0))
    P.emit("gpsimd", lambda e: e.memset(SCR[:, 0:128], 1.0), S_MEM, 1)
    n_memset = S_MEM.n

    P.wait("tensor", S_CONST, n_in_const)
    P.wait("tensor", S_MEM, n_memset)
    for i in range(len(vec_list)):
        P.emit("tensor", lambda e, i=i: e.matmul(
            PS[T0][:, i * 8:(i + 1) * 8], VSTG[:, i * 128:(i + 1) * 128], ID32[0:8, 0:8],
            is_transpose=True, start=(i == 0), stop=(i == len(vec_list) - 1)), TE)
    t_vec = TE.n
    P.wait("scalar", TE, t_vec)
    for i, (_, vidx) in enumerate(vec_list):
        P.emit("scalar", lambda e, i=i, v=vidx: e.copy(vcols(v), PS[T0][:, i * 8:(i + 1) * 8]), SE)
    se_vec = SE.n
    P.release(("bank", T0), SE)
    for _s in range(6):
        P.release(("wt", _s), TE, t_vec)
    P.wait("vector", SE, se_vec)
    P.emit("vector", lambda e: e.tensor_scalar(vcols(V_1PTS), vcols(V_TS), 1.0, None, ALU.add, ALU.bypass))
    P.emit("vector", lambda e: e.tensor_scalar(vcols(V_NTS), vcols(V_TS), -1.0, None, ALU.mult, ALU.bypass))
    P.emit("vector", lambda e: e.tensor_scalar(vcols(V_1MTS), vcols(V_TS), -1.0, 1.0, ALU.mult, ALU.add), VE)
    ve_vcols = VE.n

    P.wait("tensor", S_GW, n_gw)
    for kc in range(NCB):
        P.emit("tensor", lambda e, kc=kc: e.matmul(
            PS[T1][:].bitcast(BF16)[:, kc * 16:(kc + 1) * 16],
            GWS[:, kc * 128:(kc + 1) * 128], IDB[0:16, 0:16],
            is_transpose=True, start=(kc == 0), stop=(kc == 7)), TE)
    t_gate = TE.n
    P.wait("scalar", TE, t_gate)
    P.emit("scalar", lambda e: e.copy(GATET[:], PS[T1][:].bitcast(BF16)[:, 0:NCB * E]), SE)
    se_gate = SE.n
    P.release(("bank", T1), SE)
    P.release(("wu", 0), SE)
    P.emit("tensor", lambda e: e.matmul(PS[SP2][:, 0:16], ONEROW[:], GBS[:], start=True, stop=True), TE)
    t_gbr = TE.n
    P.wait("scalar", TE, t_gbr)
    P.emit("scalar", lambda e: e.copy(GBR[:], PS[SP2][:, 0:16]), SE)
    P.release(("bank", SP2), SE)

    XTM = lambda sl: AR[:, 10240 + sl * 1024: 10240 + (sl + 1) * 1024]
    SQD = AR[:, 9216:10240]
    se_tb = {}
    for tb in range(NTB):
        sl = tb % 3
        CSX = lambda i, sl=sl: AR[:, 13320 + sl * 16 + i: 13321 + sl * 16 + i]
        P.use("sync", ("xtm", sl))
        n_ld = dma("sync", S_XT[sl], XTM(sl), x_d[tb * 128:(tb + 1) * 128, :])
        P.wait("vector", S_XT[sl], n_ld)
        P.use("vector", ("csx", sl))
        P.emit("vector", lambda e, sl=sl, CSX=CSX: e.tensor_reduce(CSX(0), XTM(sl), AX.X, ALU.add), VE)
        P.drain("vector")
        P.wait("scalar", S_XT[sl], n_ld)
        P.use("scalar", ("sqd",))
        P.use("scalar", ("csxs", sl))
        P.emit("scalar", lambda e, sl=sl, CSX=CSX: e.activation(SQD, XTM(sl), AF.Square, accum_out=CSX(1)), SE)
        se_sq = SE.n
        P.wait("vector", SE, se_sq)
        P.emit("vector", lambda e, CSX=CSX: e.tensor_scalar(CSX(2), CSX(0), 1.0 / C, None, ALU.mult, ALU.bypass))
        P.emit("vector", lambda e, CSX=CSX: e.tensor_scalar(CSX(3), CSX(1), 1.0 / C, None, ALU.mult, ALU.bypass))
        P.drain("vector")
        P.emit("vector", lambda e, CSX=CSX: e.tensor_scalar(CSX(4), CSX(2), CSX(2), None, ALU.mult, ALU.bypass))
        P.drain("vector")
        P.emit("vector", lambda e, CSX=CSX: e.tensor_scalar(CSX(5), CSX(3), CSX(4), LN_EPS, ALU.subtract, ALU.add), VE)
        ve_var = VE.n
        P.wait("scalar", VE, ve_var)
        P.emit("scalar", lambda e, CSX=CSX: e.activation(CSX(6), CSX(5), AF.Ln))
        P.drain("scalar")
        P.emit("scalar", lambda e, CSX=CSX: e.activation(CSX(6), CSX(6), AF.Exp, scale=-0.5), SE)
        se_rstd = SE.n
        P.wait("vector", SE, se_rstd)
        P.emit("vector", lambda e, CSX=CSX, sl=sl: e.tensor_scalar(XTM(sl), XTM(sl), CSX(2), CSX(6), ALU.subtract, ALU.mult), VE)
        ve_xn = VE.n
        P.emit("vector", lambda e, CSX=CSX, sl=sl: e.tensor_reduce(CSX(7), XTM(sl), AX.X, ALU.add), VE)
        P.drain("vector")
        P.wait("scalar", VE, ve_xn)
        P.emit("scalar", lambda e, CSX=CSX, sl=sl: e.activation(SQD, XTM(sl), AF.Square, accum_out=CSX(8)), SE)
        se_sq1 = SE.n
        P.release(("sqd",), SE)
        P.wait("vector", SE, se_sq1)
        P.emit("vector", lambda e, CSX=CSX, tb=tb: e.tensor_scalar(MU1C[:, tb:tb + 1], CSX(7), 1.0 / C, None, ALU.mult, ALU.bypass))
        P.emit("vector", lambda e, CSX=CSX: e.tensor_scalar(CSX(9), CSX(8), 1.0 / C, None, ALU.mult, ALU.bypass))
        P.drain("vector")
        P.emit("vector", lambda e, CSX=CSX, tb=tb: e.tensor_scalar(CSX(10), MU1C[:, tb:tb + 1], MU1C[:, tb:tb + 1], None, ALU.mult, ALU.bypass))
        P.drain("vector")
        P.emit("vector", lambda e, CSX=CSX: e.tensor_scalar(CSX(9), CSX(9), CSX(10), LN_EPS, ALU.subtract, ALU.add), VE)
        ve_v1 = VE.n
        P.wait("scalar", VE, ve_v1)
        P.emit("scalar", lambda e, CSX=CSX: e.activation(CSX(9), CSX(9), AF.Ln))
        P.drain("scalar")
        P.emit("scalar", lambda e, CSX=CSX, tb=tb: e.activation(RS1C[:, tb:tb + 1], CSX(9), AF.Exp, scale=-0.5), SE)
        prev_se = SE.n
        P.release(("csx", sl), SE)
        P.release(("csxs", sl), VE, ve_v1)
        toks = [(VE, ve_v1), (SE, prev_se)]
        for g in range(2):
            bank = T0 if g == 0 else T1
            P.use("tensor", ("bank", bank))
            P.wait("tensor", VE, ve_xn)
            for j in range(4):
                cb = g * 4 + j
                P.emit("tensor", lambda e, sl=sl, cb=cb, j=j, b=bank: e.matmul(
                    PS[b][:, j * 128:(j + 1) * 128], XTM(sl)[:, cb * 128:(cb + 1) * 128],
                    ID32[:], is_transpose=True, start=(j == 0), stop=(j == 3)), TE)
            t_tr = TE.n
            toks.append((TE, t_tr))
            P.wait("scalar", TE, t_tr)
            ps4 = PS[bank][:].rearrange("p (j t) -> p j t", t=128)
            P.emit("scalar", lambda e, d=A3[:, g * 4:g * 4 + 4, tb * 128:(tb + 1) * 128], s=ps4: e.copy(d, s), SE)
            P.emit("scalar", lambda e, d=XNB3[:, g * 4:g * 4 + 4, tb * 128:(tb + 1) * 128], s=ps4: e.copy(d, s), SE)
            P.release(("bank", bank), SE)
        P.release_multi(("xtm", sl), toks)
        se_tb[tb] = SE.n
    se_p0 = SE.n

    if DEBUG:
        P.wait("sync", SE, se_p0)
        n_dbg0 = dma("sync", OUT, dbg["t_p0"], A[:])

    HB = lambda cb, a, b: AR[:, 3104 + cb * 513 + a: 3104 + cb * 513 + b]
    rows_mu = ROWS[0:1, 0:512]
    rows_rs = ROWS[0:1, 512:1024]
    P.wait("vector", VE, ve_vcols)
    for cb in range(NCB):
        P.emit("vector", lambda e, cb=cb: e.tensor_copy(HB(cb, 0, 1), vcol(V_TS, cb)), VE)
    ve_xxx_tile = {}
    for tau in range(NTILE):
        t0 = tau * TN
        P.use("tensor", ("bank", ST))
        P.use("tensor", ("bank", SP2))
        P.wait("tensor", SE, se_tb[tau * 4 + 3])
        for j in range(4):
            P.emit("tensor", lambda e, j=j, tau=tau: e.matmul(
                PS[ST][0:1, j * 128:(j + 1) * 128], MU1C[:, tau * 4 + j:tau * 4 + j + 1],
                ID32[:], is_transpose=True, start=(j == 0), stop=(j == 3)), TE)
        for j in range(4):
            P.emit("tensor", lambda e, j=j, tau=tau: e.matmul(
                PS[SP2][0:1, j * 128:(j + 1) * 128], RS1C[:, tau * 4 + j:tau * 4 + j + 1],
                ID32[:], is_transpose=True, start=(j == 0), stop=(j == 3)), TE)
        t_rows = TE.n
        P.use("scalar", ("rows",))
        P.wait("scalar", TE, t_rows)
        P.emit("scalar", lambda e: e.copy(rows_mu, PS[ST][0:1, 0:512]))
        P.emit("scalar", lambda e: e.copy(rows_rs, PS[SP2][0:1, 0:512]), SE)
        se_rows = SE.n
        P.release(("bank", ST), SE)
        P.release(("bank", SP2), SE)
        P.use("tensor", ("bank", MM2))
        P.use("tensor", ("bank", MM3))
        P.wait("tensor", SE, se_rows)
        P.emit("tensor", lambda e: e.matmul(PS[MM2][:], ONEROW[:], rows_mu, start=True, stop=True), TE)
        P.emit("tensor", lambda e: e.matmul(PS[MM3][:], ONEROW[:], rows_rs, start=True, stop=True), TE)
        t_bc = TE.n
        P.release(("rows",), TE)
        P.wait("vector", TE, t_bc)
        for cb in range(NCB):
            tmp = AR[:, 7208 + (cb % 2) * 512: 7208 + (cb % 2) * 512 + 512]
            P.emit("vector", lambda e, cb=cb, t0=t0: e.tensor_tensor(HB(cb, 1, 513), a_sl(cb, t0, t0 + TN), PS[MM2][:], ALU.subtract))
            P.emit("vector", lambda e, cb=cb: e.tensor_tensor(HB(cb, 1, 513), HB(cb, 1, 513), PS[MM3][:], ALU.mult))
            P.emit("vector", lambda e, cb=cb: e.tensor_scalar(HB(cb, 1, 513), HB(cb, 1, 513), vcol(V_L1W, cb), vcol(V_L1B, cb), ALU.mult, ALU.add))
            P.emit("vector", lambda e, cb=cb, tmp=tmp: e.tensor_scalar(tmp, HB(cb, 1, 513), vcol(V_1PTS, cb), None, ALU.mult, ALU.bypass))
            P.emit("vector", lambda e, cb=cb, tmp=tmp, t0=t0: e.scalar_tensor_tensor(
                xx_sl(cb, t0, t0 + TN), HB(cb, 0, 512), vcol(V_NTS, cb), tmp, ALU.mult, ALU.add), VE)
        ve_xxx_tile[tau] = VE.n
        P.release(("bank", MM2), VE)
        P.release(("bank", MM3), VE)
        if tau < NTILE - 1:
            for cb in range(NCB):
                P.emit("vector", lambda e, cb=cb: e.tensor_copy(HB(cb, 0, 1), HB(cb, 512, 513)), VE)
    ve_p1 = VE.n

    if DEBUG:
        P.wait("sync", VE, ve_p1)
        dma("sync", OUT, dbg["t_xxx"], XX[:])

    kv_k = lambda j, a, b: AR[:, j * 2560 + a: j * 2560 + b]
    kv_v = lambda j, a, b: AR[:, j * 2560 + 1024 + a: j * 2560 + 1024 + b]
    kv_r = lambda j: AR[:, j * 2560 + 2048: j * 2560 + 2560].bitcast(BF16)
    SBASE = lambda p: 10240 + p * 770
    S_EK = lambda p: AR[:, SBASE(p): SBASE(p) + 128]
    S_EU = lambda p: AR[:, SBASE(p) + 128: SBASE(p) + 256]
    S_T1 = lambda p: AR[:, SBASE(p) + 256: SBASE(p) + 384]
    S_T2 = lambda p: AR[:, SBASE(p) + 384: SBASE(p) + 512]
    S_DB = lambda p: AR[:, SBASE(p) + 512: SBASE(p) + 641]
    S_AB = lambda p: AR[:, SBASE(p) + 641: SBASE(p) + 770]
    S_ONE = SCR[:, 0:128]

    prod_done = {}

    def emit_prod_half(quad, h):
        obs = [quad * 4 + i for i in range(4)]
        for kind, fam_b, fam in (("k", wk_b, "wk"), ("v", wv_b, "wv"), ("r", wr_b, "wr")):
            if h == 0 and quad == 0 and kind == "k":
                pass
            toks = []
            for kc in range(NCB):
                P.waitp("sync", cast_done[fam])
                tile, retok, wtok = t_load(fam_b[quad * 512:(quad + 1) * 512, kc * 128:(kc + 1) * 128])
                first_mm_te = None
                for ob_l in range(4):
                    for tt_l in range(2):
                        bank = ob_l * 2 + tt_l
                        tt = 2 * h + tt_l
                        if kc == 0:
                            P.use("tensor", ("bank", bank))
                            if kind == "k" and tt_l == 0:
                                P.use("tensor", ("kvr", ob_l))
                        if ob_l == 0 and tt_l == 0:
                            P.waitp("tensor", retok)
                            P.wait("tensor", VE, ve_p1)
                        P.emit("tensor", lambda e, b=bank, kc=kc, t=tile, tt=tt, ob_l=ob_l: e.matmul(
                            PS[b][:], t[:, ob_l * 128:(ob_l + 1) * 128], xx_sl(kc, tt * TN, (tt + 1) * TN),
                            start=(kc == 0), stop=(kc == 7)), TE)
                P.release(wtok, TE)
            for ob_l in range(4):
                for tt_l in range(2):
                    bank = ob_l * 2 + tt_l
                    j = ob_l
                    P.wait("scalar", TE, TE.n)
                    if kind == "k":
                        dst = kv_k(j, tt_l * TN, (tt_l + 1) * TN)
                    elif kind == "v":
                        dst = kv_v(j, tt_l * TN, (tt_l + 1) * TN)
                    else:
                        dst = kv_r(j)[:, tt_l * TN:(tt_l + 1) * TN]
                    P.emit("scalar", lambda e, d=dst, b=bank: e.copy(d, PS[b][:]), SE)
                    P.release(("bank", bank), SE)
        for cb in obs:
            prod_done[(cb, h)] = SE.n

    def emit_scan_quad(cbs, h):
        TC = 128
        for p, cb in enumerate(cbs):
            P.wait("vector", SE, prod_done[(cb, h)])
        for q in range(8):
            a = q * TC
            ga = h * 1024 + a
            se_ek = [0] * 4; ve_m = [0] * 4; se_rm = [0] * 4; ve_den = [0] * 4; se_sig = [0] * 4
            for p, cb in enumerate(cbs):
                j = cb % 4
                kq = kv_k(j, a, a + TC)
                P.wait("scalar", S_MEM, n_memset)
                P.emit("scalar", lambda e, k=kq, p=p: e.activation(S_EK(p), k, AF.Exp))
                P.emit("scalar", lambda e, k=kq, p=p, cb=cb: e.activation(S_EU(p), k, AF.Exp, bias=vcol(V_U, cb)), SE)
                se_ek[p] = SE.n
            for p, cb in enumerate(cbs):
                dcar = CAR[:, cb:cb + 1]
                P.wait("vector", SE, se_ek[p])
                P.emit("vector", lambda e, d=dcar, p=p: e.tensor_copy(S_DB(p)[:, 0:1], d))
                P.emit("vector", lambda e, d=dcar, p=p: e.tensor_tensor_scan(S_DB(p)[:, 1:129], S_ONE, S_EK(p), d, ALU.mult, ALU.add))
                P.emit("vector", lambda e, p=p: e.tensor_tensor(S_T1(p), S_DB(p)[:, 0:128], S_EK(p), ALU.max), VE)
                ve_m[p] = VE.n
            for p in range(4):
                P.wait("scalar", VE, ve_m[p])
                P.emit("scalar", lambda e, p=p: e.activation(S_T2(p), S_T1(p), AF.Ln))
                P.emit("scalar", lambda e, p=p: e.activation(S_T1(p), S_T2(p), AF.Exp, scale=-1.0), SE)
                se_rm[p] = SE.n
            for p, cb in enumerate(cbs):
                j = cb % 4
                vq = kv_v(j, a, a + TC)
                acar = CAR[:, NCB + cb:NCB + cb + 1]
                dcar = CAR[:, cb:cb + 1]
                P.wait("vector", SE, se_rm[p])
                P.emit("vector", lambda e, p=p: e.tensor_tensor(S_T2(p), S_DB(p)[:, 0:128], S_T1(p), ALU.mult))
                P.emit("vector", lambda e, p=p: e.tensor_tensor(S_EK(p), S_EK(p), S_T1(p), ALU.mult))
                P.emit("vector", lambda e, v=vq, p=p: e.tensor_tensor(S_EK(p), S_EK(p), v, ALU.mult))
                P.emit("vector", lambda e, ac=acar, p=p: e.tensor_copy(S_AB(p)[:, 0:1], ac))
                P.emit("vector", lambda e, ac=acar, p=p: e.tensor_tensor_scan(S_AB(p)[:, 1:129], S_T2(p), S_EK(p), ac, ALU.mult, ALU.add))
                P.emit("vector", lambda e, d=dcar, p=p: e.tensor_copy(d, S_DB(p)[:, 128:129]))
                P.emit("vector", lambda e, ac=acar, p=p: e.tensor_copy(ac, S_AB(p)[:, 128:129]))
                P.emit("vector", lambda e, p=p: e.tensor_tensor(S_T1(p), S_DB(p)[:, 0:128], S_AB(p)[:, 0:128], ALU.mult))
                P.emit("vector", lambda e, v=vq, p=p: e.tensor_tensor(S_T2(p), S_EU(p), v, ALU.mult))
                P.emit("vector", lambda e, p=p: e.tensor_tensor(S_T1(p), S_T1(p), S_T2(p), ALU.add))
                P.emit("vector", lambda e, p=p: e.tensor_tensor(S_T2(p), S_DB(p)[:, 0:128], S_EU(p), ALU.add), VE)
                ve_den[p] = VE.n
            for p, cb in enumerate(cbs):
                j = cb % 4
                rq = kv_r(j)[:, a:a + TC]
                P.wait("scalar", VE, ve_den[p])
                P.emit("scalar", lambda e, p=p: e.activation(S_T2(p), S_T2(p), AF.Ln))
                P.emit("scalar", lambda e, p=p: e.activation(S_T2(p), S_T2(p), AF.Exp, scale=-1.0))
                P.emit("scalar", lambda e, r=rq, p=p: e.activation(S_EU(p), r, AF.Exp, scale=-1.0))
                P.emit("scalar", lambda e, p=p: e.add(S_EU(p), S_EU(p), 1.0))
                P.emit("scalar", lambda e, p=p: e.activation(S_EU(p), S_EU(p), AF.Ln))
                P.emit("scalar", lambda e, p=p: e.activation(S_EU(p), S_EU(p), AF.Exp, scale=-1.0), SE)
                se_sig[p] = SE.n
            for p, cb in enumerate(cbs):
                P.wait("vector", SE, se_sig[p])
                P.emit("vector", lambda e, p=p: e.tensor_tensor(S_T1(p), S_T1(p), S_T2(p), ALU.mult))
                P.emit("vector", lambda e, p=p: e.tensor_tensor(S_T1(p), S_T1(p), S_EU(p), ALU.mult))
                P.emit("vector", lambda e, cb=cb, ga=ga, p=p: e.tensor_tensor(a_sl(cb, ga, ga + TC), S_T1(p), a_sl(cb, ga, ga + TC), ALU.add), VE)
        for cb in cbs:
            P.release(("kvr", cb % 4), VE)

    emit_prod_half(0, 0)
    emit_scan_quad([0, 1, 2, 3], 0)
    emit_prod_half(0, 1)
    emit_scan_quad([0, 1, 2, 3], 1)
    emit_prod_half(1, 0)
    emit_scan_quad([4, 5, 6, 7], 0)
    emit_prod_half(1, 1)
    emit_scan_quad([4, 5, 6, 7], 1)
    ve_scan = VE.n

    if DEBUG:
        P.wait("sync", VE, ve_scan)
        dma("sync", OUT, dbg["t_x1"], A[:])

    KCSQ = AR[:, 0:8192].bitcast(BF16)
    H2 = lambda cb, a, b: AR[:, 8192 + cb * 512 + a: 8192 + cb * 512 + b]
    RCT = AR[:, 12288:12800]
    VVT = AR[:, 12800:13312]
    TROW = SCR[0:1, 0:512]

    P.use("tensor", ("bank", ST))
    P.wait("tensor", VE, ve_scan)
    if DEBUG:
        P.wait("tensor", OUT, OUT.n)
    for cb in range(NCB):
        P.emit("tensor", lambda e, cb=cb: e.matmul(PS[ST][0:1, 0:1], ONECOL[:], A3[:, cb, 2047:2048],
                                                   start=(cb == 0), stop=(cb == 7)), TE)
    P.wait("scalar", VE, ve_scan)
    P.emit("scalar", lambda e: e.activation(H2P[:], A3[:, 0:8, 2047:2048].rearrange("p a b -> p (a b)"), AF.Square), SE)
    se_wsq = SE.n
    P.use("tensor", ("bank", SP2))
    P.wait("tensor", SE, se_wsq)
    for cb in range(NCB):
        P.emit("tensor", lambda e, cb=cb: e.matmul(PS[SP2][0:1, 0:1], ONECOL[:], H2P[:, cb:cb + 1],
                                                   start=(cb == 0), stop=(cb == 7)), TE)
    t_wssq = TE.n
    P.use("scalar", ("rows",))
    P.wait("scalar", TE, t_wssq)
    P.emit("scalar", lambda e: e.copy(ROWS[0:1, 0:1], PS[ST][0:1, 0:1]))
    P.emit("scalar", lambda e: e.copy(ROWS[0:1, 512:513], PS[SP2][0:1, 0:1]), SE)
    se_wrow = SE.n
    P.release(("bank", ST), SE)
    P.release(("bank", SP2), SE)
    P.use("vector", ("scr_p3",))
    P.wait("vector", SE, se_wrow)
    P.emit("vector", lambda e: e.tensor_scalar(rows_mu[:, 0:1], rows_mu[:, 0:1], 1.0 / C, None, ALU.mult, ALU.bypass))
    P.emit("vector", lambda e: e.tensor_scalar(rows_rs[:, 0:1], rows_rs[:, 0:1], 1.0 / C, None, ALU.mult, ALU.bypass))
    P.drain("vector")
    P.emit("vector", lambda e: e.tensor_scalar(TROW[:, 0:1], rows_mu[:, 0:1], rows_mu[:, 0:1], None, ALU.mult, ALU.bypass))
    P.drain("vector")
    P.emit("vector", lambda e: e.tensor_scalar(rows_rs[:, 0:1], rows_rs[:, 0:1], TROW[:, 0:1], LN_EPS, ALU.subtract, ALU.add), VE)
    ve_wv = VE.n
    P.wait("scalar", VE, ve_wv)
    P.emit("scalar", lambda e: e.activation(rows_rs[:, 0:1], rows_rs[:, 0:1], AF.Ln))
    P.drain("scalar")
    P.emit("scalar", lambda e: e.activation(rows_rs[:, 0:1], rows_rs[:, 0:1], AF.Exp, scale=-0.5), SE)
    se_wst = SE.n
    P.use("tensor", ("bank", SP2))
    P.wait("tensor", SE, se_wst)
    P.emit("tensor", lambda e: e.matmul(PS[SP2][:, 0:1], ONEROW[:], rows_mu[:, 0:1], start=True, stop=True))
    P.emit("tensor", lambda e: e.matmul(PS[SP2][:, 1:2], ONEROW[:], rows_rs[:, 0:1], start=True, stop=True), TE)
    t_wbc = TE.n
    P.release(("rows",), TE)
    P.wait("vector", TE, t_wbc)
    for cb in range(NCB):
        P.emit("vector", lambda e, cb=cb: e.tensor_tensor(H2P[:, cb:cb + 1], A3[:, cb, 2047:2048], PS[SP2][:, 0:1], ALU.subtract))
        P.drain("vector")
        P.emit("vector", lambda e, cb=cb: e.tensor_tensor(H2P[:, cb:cb + 1], H2P[:, cb:cb + 1], PS[SP2][:, 1:2], ALU.mult))
        P.drain("vector")
        P.emit("vector", lambda e, cb=cb: e.tensor_scalar(H2P[:, cb:cb + 1], H2P[:, cb:cb + 1], vcol(V_L2W, cb), vcol(V_L2B, cb), ALU.mult, ALU.add), VE)
        P.drain("vector")
    ve_h2p = VE.n
    P.release(("bank", SP2), VE)

    prev_tau_ve = ve_h2p
    ve_xxx2_tau = {}
    for tau in range(NTILE):
        t0 = tau * TN
        P.use("tensor", ("bank", ST))
        for cb in range(NCB):
            P.emit("tensor", lambda e, cb=cb, t0=t0: e.matmul(PS[ST][0:1, 0:512], ONECOL[:], a_sl(cb, t0, t0 + TN),
                                                              start=(cb == 0), stop=(cb == 7)), TE)
        t_sum = TE.n
        P.use("scalar", ("rows",))
        P.wait("scalar", TE, t_sum)
        P.emit("scalar", lambda e: e.copy(rows_mu, PS[ST][0:1, 0:512]), SE)
        P.release(("bank", ST), SE)
        P.use("tensor", ("bank", SP2))
        for cb in range(NCB):
            sq = cb % 2
            P.use("scalar", ("h2sq", sq))
            P.wait("scalar", VE, prev_tau_ve)
            P.emit("scalar", lambda e, cb=cb, t0=t0, s=sq: e.activation(H2(s, 0, TN), a_sl(cb, t0, t0 + TN), AF.Square), SE)
            se_q = SE.n
            P.wait("tensor", SE, se_q)
            P.emit("tensor", lambda e, cb=cb, s=sq: e.matmul(PS[SP2][0:1, 0:512], ONECOL[:], H2(s, 0, TN),
                                                             start=(cb == 0), stop=(cb == 7)), TE)
            P.release(("h2sq", sq), TE)
        t_ssq = TE.n
        P.wait("scalar", TE, t_ssq)
        P.emit("scalar", lambda e: e.copy(rows_rs, PS[SP2][0:1, 0:512]), SE)
        se_ssq = SE.n
        P.release(("bank", SP2), SE)
        P.use("vector", ("scr_p3",))
        P.wait("vector", SE, se_ssq)
        P.emit("vector", lambda e: e.tensor_scalar(rows_mu, rows_mu, 1.0 / C, None, ALU.mult, ALU.bypass))
        P.emit("vector", lambda e: e.tensor_scalar(rows_rs, rows_rs, 1.0 / C, None, ALU.mult, ALU.bypass))
        P.emit("vector", lambda e: e.tensor_tensor(TROW, rows_mu, rows_mu, ALU.mult))
        P.emit("vector", lambda e: e.tensor_tensor(rows_rs, rows_rs, TROW, ALU.subtract))
        P.emit("vector", lambda e: e.tensor_scalar(rows_rs, rows_rs, LN_EPS, None, ALU.add, ALU.bypass), VE)
        ve_v2 = VE.n
        P.release(("scr_p3",), VE)
        P.wait("scalar", VE, ve_v2)
        P.emit("scalar", lambda e: e.activation(rows_rs, rows_rs, AF.Ln))
        P.emit("scalar", lambda e: e.activation(rows_rs, rows_rs, AF.Exp, scale=-0.5), SE)
        se_st2 = SE.n
        P.use("tensor", ("bank", MM2))
        P.use("tensor", ("bank", MM3))
        P.wait("tensor", SE, se_st2)
        P.emit("tensor", lambda e: e.matmul(PS[MM2][:], ONEROW[:], rows_mu, start=True, stop=True), TE)
        P.emit("tensor", lambda e: e.matmul(PS[MM3][:], ONEROW[:], rows_rs, start=True, stop=True), TE)
        t_bc2 = TE.n
        P.release(("rows",), TE)
        P.wait("vector", TE, t_bc2)
        for cb in range(NCB):
            P.emit("vector", lambda e, cb=cb, t0=t0: e.tensor_tensor(H2(cb, 0, TN), a_sl(cb, t0, t0 + TN), PS[MM2][:], ALU.subtract))
            P.emit("vector", lambda e, cb=cb: e.tensor_tensor(H2(cb, 0, TN), H2(cb, 0, TN), PS[MM3][:], ALU.mult))
            P.emit("vector", lambda e, cb=cb: e.tensor_scalar(H2(cb, 0, TN), H2(cb, 0, TN), vcol(V_L2W, cb), vcol(V_L2B, cb), ALU.mult, ALU.add))
            P.emit("vector", lambda e, cb=cb: e.tensor_scalar(VVT, H2(cb, 0, TN), vcol(V_1MTS, cb), None, ALU.mult, ALU.bypass))
            P.emit("vector", lambda e, cb=cb, t0=t0: e.scalar_tensor_tensor(
                xx_sl(cb, t0, t0 + 1), H2P[:, cb:cb + 1], vcol(V_TS, cb), VVT[:, 0:1], ALU.mult, ALU.add))
            P.emit("vector", lambda e, cb=cb, t0=t0: e.scalar_tensor_tensor(
                xx_sl(cb, t0 + 1, t0 + TN), H2(cb, 0, TN - 1), vcol(V_TS, cb), VVT[:, 1:TN], ALU.mult, ALU.add), VE)
        ve_xxx2_tau[tau] = VE.n
        P.release(("bank", MM2), VE)
        P.release(("bank", MM3), VE)
        for cb in range(NCB):
            P.emit("vector", lambda e, cb=cb: e.tensor_copy(H2P[:, cb:cb + 1], H2(cb, TN - 1, TN)), VE)
        prev_tau_ve = VE.n

    RCF = lambda ob: AR[:, 8192 + ob * 512: 8192 + (ob + 1) * 512]
    prev_vv_ve = prev_tau_ve
    for tau in range(NTILE):
        t0 = tau * TN
        ve_xxx2 = ve_xxx2_tau[tau]
        for hbg in range(4):
            for kc in range(NCB):
                P.waitp("sync", cast_done["wck"])
                tile, retok, wtok = t_load2(wck_b[hbg * 1024:(hbg + 1) * 1024, kc * 128:(kc + 1) * 128])
                for hb_l in range(8):
                    if kc == 0:
                        P.use("tensor", ("bank", hb_l))
                    if hb_l == 0:
                        P.waitp("tensor", retok)
                        P.wait("tensor", VE, ve_xxx2)
                    P.emit("tensor", lambda e, b=hb_l, t=tile, hb_l=hb_l, t0=t0, kc=kc: e.matmul(
                        PS[b][:], t[:, hb_l * 128:(hb_l + 1) * 128], xx_sl(kc, t0, t0 + TN),
                        start=(kc == 0), stop=(kc == 7)), TE)
                P.release(wtok, TE)
            t_kc = TE.n
            for hb_l in range(8):
                hb = hbg * 8 + hb_l
                P.use("scalar", ("scanbuf",))
                P.wait("scalar", TE, t_kc)
                P.emit("scalar", lambda e, b=hb_l: e.activation(SCR[:], PS[b][:], AF.Relu), SE)
                P.release(("bank", hb_l), SE)
                P.emit("scalar", lambda e, hb=hb: e.activation(KCSQ[:, hb * 512:(hb + 1) * 512], SCR[:], AF.Square), SE)
                P.release(("scanbuf",), SE)
        se_kcsq = SE.n
        for kc in range(NCB):
            P.waitp("sync", cast_done["wcr"])
            tile, retok, wtok = t_load2(wcr_b[0:1024, kc * 128:(kc + 1) * 128])
            for ob in range(8):
                if kc == 0:
                    P.use("tensor", ("bank", ob))
                if ob == 0:
                    P.waitp("tensor", retok)
                P.emit("tensor", lambda e, b=ob, t=tile, ob=ob, t0=t0, kc=kc: e.matmul(
                    PS[b][:], t[:, ob * 128:(ob + 1) * 128], xx_sl(kc, t0, t0 + TN),
                    start=(kc == 0), stop=(kc == 7)), TE)
            P.release(wtok, TE)
        t_rc = TE.n
        for ob in range(8):
            P.wait("scalar", TE, t_rc)
            if ob == 0:
                P.wait("scalar", VE, prev_vv_ve)
            P.emit("scalar", lambda e, ob=ob, b=ob: e.activation(RCF(ob), PS[b][:], AF.Exp, scale=-1.0), SE)
            P.release(("bank", ob), SE)
            P.emit("scalar", lambda e, ob=ob: e.add(RCF(ob), RCF(ob), 1.0))
            P.emit("scalar", lambda e, ob=ob: e.activation(RCF(ob), RCF(ob), AF.Ln))
            P.emit("scalar", lambda e, ob=ob: e.activation(RCF(ob), RCF(ob), AF.Exp, scale=-1.0), SE)
        se_rc = SE.n
        for hb in range(32):
            P.waitp("sync", cast_done["wcv"])
            tile, retok, wtok = t_load2(wcv_b[0:1024, hb * 128:(hb + 1) * 128])
            for cb in range(8):
                if hb == 0:
                    P.use("tensor", ("bank", cb))
                if cb == 0:
                    P.waitp("tensor", retok)
                    if hb == 0:
                        P.wait("tensor", SE, se_kcsq)
                P.emit("tensor", lambda e, b=cb, t=tile, cb=cb, hb=hb: e.matmul(
                    PS[b][:], t[:, cb * 128:(cb + 1) * 128], KCSQ[:, hb * 512:(hb + 1) * 512],
                    start=(hb == 0), stop=(hb == 31)), TE)
            P.release(wtok, TE)
        t_vv = TE.n
        for cb in range(8):
            P.wait("vector", TE, t_vv)
            P.wait("vector", SE, se_rc)
            P.emit("vector", lambda e, cb=cb: e.tensor_tensor(VVT, RCF(cb), PS[cb][:], ALU.mult), VE)
            P.emit("vector", lambda e, cb=cb, t0=t0: e.tensor_tensor(a_sl(cb, t0, t0 + TN), VVT, a_sl(cb, t0, t0 + TN), ALU.add), VE)
            P.release(("bank", cb), VE)
        prev_vv_ve = VE.n
    ve_p3 = VE.n
    te_p3 = TE.n
    se_p3 = SE.n

    if DEBUG:
        P.wait("sync", VE, ve_p3)
        dma("sync", OUT, dbg["t_s"], A[:])

    GPRE = AR[:, 0:2048].bitcast(BF16)
    WS1T = AR[:, 2048:4096].bitcast(BF16)
    WS3T = AR[:, 4096:6144].bitcast(BF16)
    WS2T = AR[:, 6144:8192].bitcast(BF16)
    ZMID = AR[:, 8192:9216].bitcast(BF16)
    SILU = AR[:, 9216:9728]
    HST = lambda r: AR[:, 9728 + r * 1024: 10752 + r * 1024].bitcast(BF16)
    SCO = AR[:, 9728:9984]
    CMF = AR[0:16, 11776:12800].bitcast(BF16)

    for tb in range(NTB):
        bank = MM0 if tb % 2 == 0 else MM1
        P.use("tensor", ("bank", bank))
        if tb == 0:
            P.wait("tensor", SE, se_gate)
        for kc in range(NCB):
            P.emit("tensor", lambda e, b=bank, kc=kc, tb=tb: e.matmul(
                PS[b][:, 0:16], XNB3[:, kc, tb * 128:(tb + 1) * 128],
                GATET[:, kc * 16:(kc + 1) * 16], start=(kc == 0), stop=(kc == 7)), TE)
        t_sc = TE.n
        P.wait("scalar", TE, t_sc)
        if tb == 0:
            P.wait("scalar", VE, ve_p3)
        stb = SCO[:, tb * 16:(tb + 1) * 16]
        P.emit("scalar", lambda e, b=bank, stb=stb: e.activation(stb, PS[b][:, 0:16], AF.Exp, scale=-1.0), SE)
        se_s1 = SE.n
        P.release(("bank", bank), SE)
        P.drain("scalar")
        P.emit("scalar", lambda e, stb=stb: e.add(stb, stb, 1.0))
        P.drain("scalar")
        P.emit("scalar", lambda e, stb=stb: e.activation(stb, stb, AF.Ln))
        P.drain("scalar")
        P.emit("scalar", lambda e, stb=stb: e.activation(stb, stb, AF.Exp, scale=-1.0), SE)
        if DEBUG:
            P.emit("scalar", lambda e, tb=tb, stb=stb: e.copy(WT2[:].bitcast(F32)[:, tb * 16:(tb + 1) * 16], stb), SE)
        se_sco = SE.n
        sc = lambda x, y: SCR[:, x:y]
        P.use("vector", ("scr_p4",))
        P.wait("vector", SE, se_sco)
        P.emit("vector", lambda e, stb=stb: e.tensor_tensor(sc(96, 112), stb, GBR[:], ALU.add))
        P.drain("vector")
        P.emit("vector", lambda e: e.tensor_reduce(sc(0, 4), sc(96, 112).rearrange("p (a b) -> p a b", b=4), AX.X, ALU.max))
        P.drain("vector")
        for G in range(4):
            P.emit("vector", lambda e, G=G: e.tensor_scalar(sc(4, 8), sc(0, 4), sc(G, G + 1), 0.0, ALU.is_gt, ALU.add, accum_out=sc(8 + G, 9 + G)))
        P.drain("vector")
        for G in range(4):
            P.emit("vector", lambda e, G=G: e.tensor_scalar(sc(12 + G, 13 + G), sc(8 + G, 9 + G), 2.0, None, ALU.is_lt, ALU.bypass))
        P.drain("vector")
        for G in range(4):
            P.emit("vector", lambda e, G=G: e.tensor_scalar(sc(16 + G, 17 + G), sc(12 + G, 13 + G), 4.0, None, ALU.mult, ALU.bypass))
        P.drain("vector")
        for G in range(4):
            P.emit("vector", lambda e, G=G: e.tensor_scalar(sc(96 + 4 * G, 100 + 4 * G), sc(96 + 4 * G, 100 + 4 * G), sc(16 + G, 17 + G), -4.0, ALU.add, ALU.add))
        P.drain("vector")
        for ee in range(E):
            P.emit("vector", lambda e, ee=ee: e.tensor_scalar(sc(32, 48), sc(96, 112), sc(96 + ee, 97 + ee), 0.0, ALU.is_gt, ALU.add, accum_out=sc(48 + ee, 49 + ee)))
        P.drain("vector")
        for ee in range(E):
            P.emit("vector", lambda e, ee=ee: e.tensor_scalar(sc(64 + ee, 65 + ee), sc(48 + ee, 49 + ee), 4.0, None, ALU.is_lt, ALU.bypass))
        P.drain("vector")
        for ee in range(E):
            P.emit("vector", lambda e, ee=ee, stb=stb: e.tensor_tensor(
                stb[:, ee:ee + 1], sc(64 + ee, 65 + ee), stb[:, ee:ee + 1], ALU.mult), VE)
        P.drain("vector")
        ve_comb = VE.n
        P.release(("scr_p4",), VE)
        grp = tb % 4
        bank2 = T0 if (tb // 4) % 2 == 0 else T1
        if grp == 0:
            P.use("tensor", ("bank", bank2))
        P.wait("tensor", VE, ve_comb)
        P.emit("tensor", lambda e, stb=stb, b=bank2, grp=grp: e.matmul(
            PS[b][0:16, grp * 128:(grp + 1) * 128], stb, ID32[:],
            is_transpose=True, start=(grp == 0), stop=(grp == 3)), TE)
        if grp == 3:
            t_cmt = TE.n
            P.wait("scalar", TE, t_cmt)
            P.emit("scalar", lambda e, tb=tb, b=bank2: e.copy(CMF[:, (tb - 3) * 128:(tb + 1) * 128], PS[b][0:16, 0:512]), SE)
            P.release(("bank", bank2), SE)
    se_cmf = SE.n

    if DEBUG:
        P.wait("sync", SE, se_cmf)
        dma("sync", OUT, dbg["t_cmf"], CMF[:])
        dma("sync", OUT, dbg["t_sco"], WT2[:].bitcast(F32)[:, 0:NTB * E])
        dma("sync", OUT, dbg["t_xnb"], XNB[:])
        dma("sync", OUT, dbg["t_gt"], GATET[:])
        dma("sync", OUT, dbg["t_gbr"], GBR[:])

    W3SL = AR[:, 0:2048].bitcast(BF16)
    BAS = AR[0:16, 2048:3072].bitcast(BF16)
    P.wait("sync", TE, te_p3)
    P.wait("sync", VE, ve_p3)
    n_bas = dma("sync", S_CONST, BAS, bas_d)
    hst_r = [0]
    ve_he = 0
    w1_toks = None
    w3_toks = None

    def load_w1(e_i):
        toks = []
        for kc in range(NCB):
            P.waitp("sync", cast_done["w1"])
            tile, retok, wtok = t_load(w1_b[e_i][0:512, kc * 128:(kc + 1) * 128])
            toks.append((tile, retok, wtok))
        return toks

    def load_w3(e_i):
        toks = []
        for kc in range(NCB):
            P.waitp("sync", cast_done["w3"])
            P.use("sync", ("w3sl", kc))
            if e_i == 0 and kc == 0:
                P.wait("sync", TE, te_p3)
                P.wait("sync", VE, ve_p3)
            dstv = W3SL[:, kc * 512:(kc + 1) * 512]
            n3 = P.emit("sync", lambda e, o=dstv, i=w3_b[e_i][0:512, kc * 128:(kc + 1) * 128]: e.dma_start(
                out=o, in_=i, transpose=True), S_W3, 16)
            toks.append((dstv, (S_W3, n3), kc))
        return toks

    w1_toks = load_w1(0)
    w3_toks = load_w3(0)
    for e_i in range(E):
        for tau in range(NTILE):
            t0 = tau * TN
            P.use("tensor", ("bank", MM0))
            if e_i == 0 and tau == 0:
                P.wait("tensor", SE, se_cmf)
                P.wait("tensor", S_CONST, n_bas)
                if DEBUG:
                    P.wait("tensor", OUT, OUT.n)
            P.emit("tensor", lambda e, e_i=e_i, tau=tau: e.matmul(
                PS[MM0][:], BAS[:, e_i * 128:(e_i + 1) * 128], CMF[:, tau * 512:(tau + 1) * 512],
                start=True, stop=True), TE)
            t_cb = TE.n
            P.use("scalar", ("combB",))
            P.wait("scalar", TE, t_cb)
            P.emit("scalar", lambda e: e.copy(SCR[:], PS[MM0][:]), SE)
            se_cb = SE.n
            P.release(("bank", MM0), SE)
            for ib in range(4):
                P.use("tensor", ("bank", ib))
            for kc in range(NCB):
                if kc == 0:
                    P.waitp("tensor", w1_toks[kc][1])
                for ib in range(4):
                    P.emit("tensor", lambda e, b=ib, t=w1_toks[kc][0], ib=ib, t0=t0, kc=kc: e.matmul(
                        PS[b][:], t[:, ib * 128:(ib + 1) * 128], xnb_sl(kc, t0, t0 + TN),
                        start=(kc == 0), stop=(kc == 7)), TE)
            t_h1 = TE.n
            if tau == NTILE - 1:
                for _, _, wtok in w1_toks:
                    P.release(wtok, TE)
            for ib in range(4):
                P.use("tensor", ("bank", 4 + ib))
            for kc in range(NCB):
                if kc == 0:
                    P.waitp("tensor", w3_toks[kc][1])
                for ib in range(4):
                    P.emit("tensor", lambda e, b=4 + ib, t=w3_toks[kc][0], ib=ib, t0=t0, kc=kc: e.matmul(
                        PS[b][:], t[:, ib * 128:(ib + 1) * 128], xnb_sl(kc, t0, t0 + TN),
                        start=(kc == 0), stop=(kc == 7)), TE)
            t_h3 = TE.n
            if tau == NTILE - 1:
                for _, _, kcsl in w3_toks:
                    P.release(("w3sl", kcsl), TE)
            if tau == NTILE - 1 and e_i + 1 < E:
                nxt1 = load_w1(e_i + 1)
                nxt3 = load_w3(e_i + 1)
            r = hst_r[0] % 2
            for ib in range(4):
                P.use("scalar", ("silu",))
                P.wait("scalar", TE, t_h1 if ib == 0 else TE.n)
                if e_i == 0 and tau == 0 and ib == 0:
                    P.wait("scalar", VE, ve_p3)
                P.emit("scalar", lambda e, b=ib: e.activation(SILU, PS[b][:], AF.Silu), SE)
                se_si = SE.n
                P.release(("bank", ib), SE)
                if ib == 0:
                    P.use("vector", ("hst", r))
                P.wait("vector", SE, se_si)
                P.wait("vector", SE, se_cb)
                P.wait("vector", TE, t_h3)
                P.emit("vector", lambda e, b=4 + ib: e.tensor_tensor(SILU, SILU, PS[b][:], ALU.mult), VE)
                P.release(("bank", 4 + ib), VE)
                P.emit("vector", lambda e, r=r, ib=ib: e.tensor_tensor(HST(r)[:, ib * 512:(ib + 1) * 512], SILU, SCR[:], ALU.mult), VE)
                ve_he = VE.n
                P.release(("silu",), VE)
                if ib == 3:
                    P.release(("combB",), VE)
            P.wait("sync", VE, ve_he)
            dma("sync", S_HS[r], H_s[tau][e_i], HST(r))
            P.release(("hst", r), S_HS[r])
            hst_r[0] += 1
        if e_i + 1 < E:
            w1_toks = nxt1
            w3_toks = nxt3
    f1_hs_toks = [(S_HS[0], S_HS[0].n), (S_HS[1], S_HS[1].n)]

    te_f1 = TE.n
    P.wait("sync", TE, te_f1)
    P.wait("sync", VE, ve_he)
    P.waitp("sync", cast_done["ws"])
    for kc in range(NCB):
        P.emit("sync", lambda e, kc=kc: e.dma_start(
            out=WS1T[:, kc * 512:(kc + 1) * 512], in_=ws1_b[0:512, kc * 128:(kc + 1) * 128], transpose=True), S_WS, 16)
    for kc in range(NCB):
        P.emit("sync", lambda e, kc=kc: e.dma_start(
            out=WS3T[:, kc * 512:(kc + 1) * 512], in_=ws3_b[0:512, kc * 128:(kc + 1) * 128], transpose=True), S_WS, 16)
    for ikc in range(4):
        P.emit("sync", lambda e, ikc=ikc: e.dma_start(
            out=WS2T[:, ikc * 1024:(ikc + 1) * 1024], in_=ws2_b[0:1024, ikc * 128:(ikc + 1) * 128], transpose=True), S_WS, 16)
    re_wsz = S_WS.n

    YB = list(range(8))
    for tau in range(NTILE):
        t0 = tau * TN
        for ib in range(4):
            b1 = 0 if ib % 2 == 0 else 1
            b3 = 2 if ib % 2 == 0 else 3
            P.use("tensor", ("bank", b1))
            P.use("tensor", ("bank", b3))
            P.wait("tensor", S_WS, re_wsz)
            for kc in range(NCB):
                P.emit("tensor", lambda e, b=b1, ib=ib, kc=kc, t0=t0: e.matmul(
                    PS[b][:], WS1T[:, kc * 512 + ib * 128: kc * 512 + (ib + 1) * 128], xnb_sl(kc, t0, t0 + TN),
                    start=(kc == 0), stop=(kc == 7)), TE)
            for kc in range(NCB):
                P.emit("tensor", lambda e, b=b3, ib=ib, kc=kc, t0=t0: e.matmul(
                    PS[b][:], WS3T[:, kc * 512 + ib * 128: kc * 512 + (ib + 1) * 128], xnb_sl(kc, t0, t0 + TN),
                    start=(kc == 0), stop=(kc == 7)), TE)
            t_z = TE.n
            P.use("scalar", ("silu",))
            P.wait("scalar", TE, t_z)
            if tau == 0 and ib == 0:
                P.wait("scalar", VE, ve_he)
            P.emit("scalar", lambda e, b=b1: e.activation(SILU, PS[b][:], AF.Silu), SE)
            se_zs = SE.n
            P.release(("bank", b1), SE)
            P.use("vector", ("zmid", ib))
            P.wait("vector", SE, se_zs)
            P.emit("vector", lambda e, ib=ib, b=b3: e.tensor_tensor(ZMID[:, ib * 512:(ib + 1) * 512], SILU, PS[b][:], ALU.mult), VE)
            P.release(("bank", b3), VE)
            P.release(("silu",), VE)
        ve_zmid = VE.n
        h_re = {}
        for b in YB:
            P.use("tensor", ("bank", b))
        for e_i in range(E):
            esub = e_i % 8
            P.use("sync", ("xxh", esub))
            if tau == 0 and e_i == 0:
                P.wait("sync", TE, te_p3)
                for tok in f1_hs_toks:
                    P.waitp("sync", tok)
            h_re[e_i] = (S_H, dma("sync", S_H, XX3b[:, esub, :], H_s[tau][e_i]))
            P.waitp("sync", cast_done["w2"])
            w2t = []
            for ikc in range(4):
                w2t.append(t_load2(w2_b[e_i][0:1024, ikc * 128:(ikc + 1) * 128]))
            P.waitp("tensor", h_re[e_i])
            for ikc in range(4):
                P.waitp("tensor", w2t[ikc][1])
                for db in range(8):
                    first = (e_i == 0 and ikc == 0)
                    P.emit("tensor", lambda e, b=YB[db], t=w2t[ikc][0], db=db, esub=esub, ikc=ikc, first=first: e.matmul(
                        PS[b][:], t[:, db * 128:(db + 1) * 128],
                        XX3b[:, esub, ikc * 512:(ikc + 1) * 512],
                        start=first, stop=False), TE)
                P.release(w2t[ikc][2], TE)
            if e_i >= 8:
                P.release(("xxh", esub), TE)
            else:
                P.release(("xxh", esub), TE)
        P.wait("tensor", VE, ve_zmid)
        for ikc in range(4):
            for db in range(8):
                P.emit("tensor", lambda e, b=YB[db], db=db, ikc=ikc: e.matmul(
                    PS[b][:], WS2T[:, ikc * 1024 + db * 128: ikc * 1024 + (db + 1) * 128],
                    ZMID[:, ikc * 512:(ikc + 1) * 512], start=False, stop=(ikc == 3)), TE)
        t_y = TE.n
        for esub in range(8):
            P.release(("xxh", esub), TE, t_y)
        for ib in range(4):
            P.release(("zmid", ib), TE, t_y)
        for db in range(8):
            P.use("scalar", ("gpre", db))
            P.wait("scalar", TE, t_y)
            P.emit("scalar", lambda e, db=db, b=YB[db]: e.copy(GPRE[:, db * 512:(db + 1) * 512], PS[b][:]), SE)
            P.release(("bank", YB[db]), SE)
        se_g = SE.n
        P.wait("sync", SE, se_g)
        if DEBUG:
            dma("sync", OUT, dbg["t_g"][tau], GPRE[:])
        n_gs = dma("sync", S_GS, g_s[tau], GPRE[:])
        for db in range(8):
            P.release(("gpre", db), S_GS, S_GS.n)
    spl_g = S_GS.n
    te_f2 = TE.n
    ve_f2 = VE.n

    STM = lambda r: AR[:, r * 1024: (r + 1) * 1024]
    XT2 = lambda r: AR[:, 2048 + r * 1024: 2048 + (r + 1) * 1024]
    OTB = lambda r: AR[:, 4096 + r * 1024: 4096 + (r + 1) * 1024]
    GTL = AR[:, 6144:8192].bitcast(BF16)
    GSL = lambda r: AR[:, 8192 + r * 1024: 8192 + (r + 1) * 1024]

    first_stm = [True]
    first_xt2 = [True]
    first_gs = [True]
    for tau in range(NTILE):
        P.use("sync", ("gtile",))
        if tau == 0:
            P.wait("sync", S_GS, spl_g)
            P.wait("sync", TE, te_f2)
        n_g = dma("sync", S_GT, GTL[:], g_s[tau])
        for tb4 in range(4):
            tb = tau * 4 + tb4
            r = tb % 2
            P.use("sync", ("xt2", r))
            if first_xt2[0]:
                P.wait("sync", TE, te_f2)
                first_xt2[0] = False
            n_x = dma("sync", S_X2[r], XT2(r), x_d[tb * 128:(tb + 1) * 128, :])
            P.use("scalar", ("stm", r))
            for g in range(2):
                bank = T0 if g == 0 else T1
                P.use("tensor", ("bank", bank))
                for j in range(4):
                    cb = g * 4 + j
                    P.emit("tensor", lambda e, cb=cb, j=j, b=bank, tb=tb: e.matmul(
                        PS[b][:, j * 128:(j + 1) * 128], A3[:, cb, tb * 128:(tb + 1) * 128],
                        ID32[:], is_transpose=True, start=(j == 0), stop=(j == 3)), TE)
                t_tr = TE.n
                P.wait("scalar", TE, t_tr)
                if first_stm[0]:
                    P.wait("scalar", S_GS, spl_g)
                    first_stm[0] = False
                P.emit("scalar", lambda e, r=r, g=g, b=bank: e.copy(STM(r)[:, g * 512:(g + 1) * 512], PS[b][:]), SE)
                P.release(("bank", bank), SE)
            bank = T0 if tb % 2 == 0 else T1
            P.use("tensor", ("bank", bank))
            P.wait("tensor", S_GT, n_g)
            for cb in range(NCB):
                P.emit("tensor", lambda e, cb=cb, b=bank, tb4=tb4: e.matmul(
                    PS[b][:].bitcast(BF16)[:, cb * 128:(cb + 1) * 128],
                    GTL[:, cb * 512 + tb4 * 128: cb * 512 + tb4 * 128 + 128],
                    IDB[:], is_transpose=True, start=(cb == 0), stop=(cb == 7)), TE)
            t_gt = TE.n
            P.use("scalar", ("gsil", r))
            P.wait("scalar", TE, t_gt)
            if first_gs[0]:
                for tok in f1_hs_toks:
                    P.waitp("scalar", tok)
                P.wait("scalar", VE, ve_f2)
                first_gs[0] = False
            P.emit("scalar", lambda e, r=r, b=bank: e.activation(GSL(r), PS[b][:].bitcast(BF16)[:, 0:1024], AF.Silu), SE)
            se_gs = SE.n
            P.release(("bank", bank), SE)
            P.use("vector", ("ot", r))
            P.wait("vector", SE, se_gs)
            P.wait("vector", S_X2[r], n_x)
            P.emit("vector", lambda e, r=r: e.tensor_tensor(STM(r), STM(r), GSL(r), ALU.mult))
            P.emit("vector", lambda e, r=r: e.tensor_tensor(OTB(r), STM(r), XT2(r), ALU.add), VE)
            ve_ot = VE.n
            P.release(("stm", r), VE)
            P.release(("gsil", r), VE)
            P.release(("xt2", r), VE)
            P.wait("sync", VE, ve_ot)
            dma("sync", S_OT[r], out_d[tb * 128:(tb + 1) * 128, :], OTB(r))
            P.release(("ot", r), S_OT[r])
        P.release(("gtile",), TE)

    P.wait("sync", S_OT[0], S_OT[0].n)
    P.wait("sync", S_OT[1], S_OT[1].n)
    if DEBUG:
        P.wait("sync", OUT, OUT.n)

    with nc.Block() as block:
        @block.sync
        def _(e):
            for f in P.steps["sync"]:
                f(e)

        @block.gpsimd
        def _(e):
            for f in P.steps["gpsimd"]:
                f(e)

        @block.tensor
        def _(e):
            for f in P.steps["tensor"]:
                f(e)

        @block.scalar
        def _(e):
            for f in P.steps["scalar"]:
                f(e)

        @block.vector
        def _(e):
            for f in P.steps["vector"]:
                f(e)

    es.close()
    return nc


_NC_CACHE = None


def kernel(**inputs):
    global _NC_CACHE
    if _NC_CACHE is None:
        _NC_CACHE = build_nc()
    nc = _NC_CACHE

    x = np.ascontiguousarray(np.asarray(inputs["L0_attn_x"], dtype=np.float32))
    _ = inputs["L0_attn_time_decay"]
    basis = np.zeros((E, E * 128), dtype=ml_dtypes.bfloat16)
    for e_i in range(E):
        basis[e_i, e_i * 128:(e_i + 1) * 128] = 1.0
    common = {
        "Wk": np.ascontiguousarray(np.asarray(inputs["L0_attn_Wk"], np.float32)),
        "Wv": np.ascontiguousarray(np.asarray(inputs["L0_attn_Wv"], np.float32)),
        "Wr": np.ascontiguousarray(np.asarray(inputs["L0_attn_Wr"], np.float32)),
        "Wck": np.ascontiguousarray(np.asarray(inputs["L0_attn_Wck"], np.float32)),
        "Wcv": np.ascontiguousarray(np.asarray(inputs["L0_attn_Wcv"], np.float32)),
        "Wcr": np.ascontiguousarray(np.asarray(inputs["L0_attn_Wcr"], np.float32)),
        "ln1w": np.ascontiguousarray(np.asarray(inputs["L0_attn_ln1_w"], np.float32)),
        "ln1b": np.ascontiguousarray(np.asarray(inputs["L0_attn_ln1_b"], np.float32)),
        "ln2w": np.ascontiguousarray(np.asarray(inputs["L0_attn_ln2_w"], np.float32)),
        "ln2b": np.ascontiguousarray(np.asarray(inputs["L0_attn_ln2_b"], np.float32)),
        "tshift": np.ascontiguousarray(np.asarray(inputs["L0_attn_time_shift"], np.float32)),
        "tfirst": np.ascontiguousarray(np.asarray(inputs["L0_attn_time_first"], np.float32)),
        "gate_w": np.ascontiguousarray(np.asarray(inputs["L0_ffn_gate_w"], np.float32)),
        "gate_bias": np.ascontiguousarray(np.asarray(inputs["L0_ffn_gate_bias"], np.float32)),
        "w1": np.ascontiguousarray(np.asarray(inputs["L0_ffn_w1"], np.float32)),
        "w2": np.ascontiguousarray(np.asarray(inputs["L0_ffn_w2"], np.float32)),
        "w3": np.ascontiguousarray(np.asarray(inputs["L0_ffn_w3"], np.float32)),
        "ws1": np.ascontiguousarray(np.asarray(inputs["L0_ffn_ws1"], np.float32)),
        "ws2": np.ascontiguousarray(np.asarray(inputs["L0_ffn_ws2"], np.float32)),
        "ws3": np.ascontiguousarray(np.asarray(inputs["L0_ffn_ws3"], np.float32)),
        "identb": np.eye(128, dtype=ml_dtypes.bfloat16),
        "ident32": np.eye(128, dtype=np.float32),
        "basis": basis,
    }
    in_maps = [dict(common, x=np.ascontiguousarray(x[b])) for b in range(8)]
    res = run_bass_kernel_spmd(nc, in_maps, core_ids=list(range(8)))
    out = np.stack([np.asarray(res.results[b]["out"], np.float32) for b in range(8)], axis=0)
    return out


if __name__ == "__main__":
    build_nc()
    print("built ok")


# version 07
# speedup vs baseline: 1.1764x; 1.0906x over previous
import numpy as np
import ml_dtypes
from contextlib import ExitStack
import concourse.bass as bass
import concourse.mybir as mybir
from concourse.bass_utils import run_bass_kernel_spmd

F32 = mybir.dt.float32
BF16 = mybir.dt.bfloat16
AF = mybir.ActivationFunctionType
ALU = mybir.AluOpType
AX = mybir.AxisListType

T = 2048
C = 1024
NCB = 8
NTB = 16
NTILE = 4
TN = 512
E = 16
LN_EPS = 1e-5

DEBUG = False


class Sem:
    def __init__(self, h):
        self.h = h
        self.n = 0


class Plan:
    ENGINES = ("sync", "gpsimd", "tensor", "scalar", "vector")

    def __init__(self):
        self.steps = {e: [] for e in self.ENGINES}
        self.free = {}

    def wait(self, eng, sem, val):
        if val > sem.n:
            raise RuntimeError(f"wait for {val}, sem at {sem.n}")
        if val > 0:
            self.steps[eng].append(lambda e, s=sem.h, v=val: e.wait_ge(s, v))

    def emit(self, eng, build, sem=None, k=1):
        if sem is not None:
            sem.n += k
            self.steps[eng].append(lambda e, b=build, s=sem.h, kk=k: b(e).then_inc(s, kk))
            return sem.n
        self.steps[eng].append(build)
        return None

    def use(self, eng, key):
        for sem, val in self.free.get(key, ()):
            self.wait(eng, sem, val)

    def release(self, key, sem, val=None):
        self.free[key] = [(sem, sem.n if val is None else val)]

    def release_multi(self, key, toks):
        self.free[key] = list(toks)

    def drain(self, eng):
        self.steps[eng].append(lambda e: e.drain())

    def waitp(self, eng, tok):
        self.wait(eng, tok[0], tok[1])


def build_nc():
    nc = bass.Bass("TRN2", target_bir_lowering=False)

    def din(name, shape, dt=F32):
        return nc.dram_tensor(name, shape, dt, kind="ExternalInput").ap()

    x_d = din("x", [T, C])
    Wk_d = din("Wk", [C, C]); Wv_d = din("Wv", [C, C]); Wr_d = din("Wr", [C, C])
    Wck_d = din("Wck", [4096, C]); Wcv_d = din("Wcv", [C, 4096]); Wcr_d = din("Wcr", [C, C])
    ln1w_d = din("ln1w", [C]); ln1b_d = din("ln1b", [C])
    ln2w_d = din("ln2w", [C]); ln2b_d = din("ln2b", [C])
    ts_d = din("tshift", [1, 1, C]); u_d = din("tfirst", [C])
    gw_d = din("gate_w", [E, C]); gb_d = din("gate_bias", [E])
    w1_d = din("w1", [E, 512, C]); w2_d = din("w2", [E, C, 512]); w3_d = din("w3", [E, 512, C])
    ws1_d = din("ws1", [512, C]); ws2_d = din("ws2", [C, 512]); ws3_d = din("ws3", [512, C])
    idb_d = din("identb", [128, 128], BF16)
    id32_d = din("ident32", [128, 128], F32)
    bas_d = din("basis", [E, E * 128], BF16)

    out_d = nc.dram_tensor("out", [T, C], F32, kind="ExternalOutput").ap()

    def scrt(name, shape):
        return nc.dram_tensor(name, shape, BF16, kind="Internal").ap()

    wk_b = scrt("wk_b", [C, C]); wv_b = scrt("wv_b", [C, C]); wr_b = scrt("wr_b", [C, C])
    wck_b = scrt("wck_b", [4096, C]); wcr_b = scrt("wcr_b", [C, C]); wcv_b = scrt("wcv_b", [C, 4096])
    w1_b = scrt("w1_b", [E, 512, C]); w3_b = scrt("w3_b", [E, 512, C]); w2_b = scrt("w2_b", [E, C, 512])
    ws1_b = scrt("ws1_b", [512, C]); ws3_b = scrt("ws3_b", [512, C]); ws2_b = scrt("ws2_b", [C, 512])
    H_s = scrt("H_s", [NTILE, E, 128, 2048])
    g_s = scrt("g_s", [NTILE, 128, 4096])

    dbg = {}
    if DEBUG:
        dbg["t_p0"] = nc.dram_tensor("t_p0", [128, NCB * T], F32, kind="ExternalOutput").ap()
        dbg["t_xxx"] = nc.dram_tensor("t_xxx", [128, NCB * T], BF16, kind="ExternalOutput").ap()
        dbg["t_x1"] = nc.dram_tensor("t_x1", [128, NCB * T], F32, kind="ExternalOutput").ap()
        dbg["t_s"] = nc.dram_tensor("t_s", [128, NCB * T], F32, kind="ExternalOutput").ap()
        dbg["t_cmf"] = nc.dram_tensor("t_cmf", [16, E * 128], BF16, kind="ExternalOutput").ap()
        dbg["t_sco"] = nc.dram_tensor("t_sco", [128, NTB * E], F32, kind="ExternalOutput").ap()
        dbg["t_xnb"] = nc.dram_tensor("t_xnb", [128, NCB * T], BF16, kind="ExternalOutput").ap()
        dbg["t_gt"] = nc.dram_tensor("t_gt", [128, NCB * E], BF16, kind="ExternalOutput").ap()
        dbg["t_gbr"] = nc.dram_tensor("t_gbr", [128, E], F32, kind="ExternalOutput").ap()
        dbg["t_g"] = nc.dram_tensor("t_g", [NTILE, 128, 4096], BF16, kind="ExternalOutput").ap()
        dbg["t_kv"] = nc.dram_tensor("t_kv", [128, 5120], F32, kind="ExternalOutput").ap()

    es = ExitStack()
    sb = lambda name, shape, dt: es.enter_context(nc.sbuf_tensor(name, shape, dt))

    A = sb("A_stream", [128, NCB * T], F32)
    XNB = sb("A_xnb", [128, NCB * T], BF16)
    XX = sb("A_xxx", [128, NCB * T], BF16)
    AR = sb("AR", [128, 13376], F32)
    WT = sb("W_T", [128, 8 * 512], BF16)
    WT2 = sb("W_T2", [128, 4 * 1024], BF16)
    IDB = sb("identb_s", [128, 128], BF16)
    ID32 = sb("ident32_s", [128, 128], F32)
    ONECOL = sb("onecol", [128, 1], F32)
    ONEROW = sb("onerow", [1, 128], F32)
    ROWS = sb("rows", [1, 1024], F32)
    MU1C = sb("mu1c", [128, NTB], F32)
    RS1C = sb("rs1c", [128, NTB], F32)
    VEC = sb("veccols", [128, 10 * NCB], F32)
    CAR = sb("carries", [128, 2 * NCB], F32)
    H2P = sb("h2prev", [128, NCB], F32)
    GATET = sb("gateT", [128, NCB * E], BF16)
    GBR = sb("gbrow", [128, E], F32)
    GBS = sb("gbs", [1, E], F32)
    SCR = sb("scr", [128, 512], F32)

    V_U, V_TS, V_1PTS, V_NTS, V_1MTS, V_L1W, V_L1B, V_L2W, V_L2B, V_SP = range(10)

    def vcol(i, cb):
        return VEC[:, i * NCB + cb: i * NCB + cb + 1]

    def vcols(i):
        return VEC[:, i * NCB: (i + 1) * NCB]

    PS = [es.enter_context(nc.psum_tensor(f"ps{i}", [128, 512], F32)) for i in range(8)]
    MM0, MM1, MM2, MM3, T0, T1, ST, SP2 = range(8)

    mksem = lambda nm: Sem(es.enter_context(nc.semaphore(nm)))
    TE = mksem("TE"); SE = mksem("SE"); VE = mksem("VE"); OUT = mksem("OUT")
    S_CONST = mksem("S_CONST"); S_GW = mksem("S_GW"); S_MEM = mksem("S_MEM")
    S_XT = [mksem(f"S_XT{i}") for i in range(3)]
    S_WT = [mksem(f"S_WT{i}") for i in range(8)]
    S_WU = [mksem(f"S_WU{i}") for i in range(4)]
    S_CAST = {f: mksem(f"S_C_{f}") for f in
              ("wk", "wv", "wr", "wck", "wcr", "wcv", "w1", "w3", "w2", "ws")}
    S_W3 = mksem("S_W3"); S_HS = [mksem(f"S_HS{i}") for i in range(2)]
    S_H = mksem("S_H"); S_WS = mksem("S_WS"); S_GS = mksem("S_GS")
    S_GT = mksem("S_GT")
    S_X2 = [mksem(f"S_X2{i}") for i in range(2)]
    S_OT = [mksem(f"S_OT{i}") for i in range(2)]

    P = Plan()
    free = P.free

    A3 = A[:].rearrange("p (cb t) -> p cb t", t=T)
    XNB3 = XNB[:].rearrange("p (cb t) -> p cb t", t=T)
    XX3b = XX[:].rearrange("p (s n) -> p s n", n=2048)

    def a_sl(cb, a, b):
        return A[:, cb * T + a: cb * T + b]

    def xnb_sl(cb, a, b):
        return XNB[:, cb * T + a: cb * T + b]

    def xx_sl(cb, a, b):
        return XX[:, cb * T + a: cb * T + b]

    def dma(eng, sem, out, in_):
        return P.emit(eng, lambda e, o=out, i=in_: e.dma_start(out=o, in_=i), sem, 16)

    cast_done = {}

    def cast(fam, dst, src_):
        n = dma("gpsimd", S_CAST[fam], dst, src_)
        cast_done[fam] = (S_CAST[fam], n)

    cast("wk", wk_b, Wk_d)
    cast("wv", wv_b, Wv_d)
    cast("wr", wr_b, Wr_d)
    cast("wck", wck_b, Wck_d)
    cast("wcr", wcr_b, Wcr_d)
    cast("wcv", wcv_b, Wcv_d)
    cast("w1", w1_b.rearrange("e i c -> (e i) c"), w1_d.rearrange("e i c -> (e i) c"))
    cast("w3", w3_b.rearrange("e i c -> (e i) c"), w3_d.rearrange("e i c -> (e i) c"))
    cast("w2", w2_b.rearrange("e d i -> (e d) i"), w2_d.rearrange("e d i -> (e d) i"))
    dma("gpsimd", S_CAST["ws"], ws1_b, ws1_d)
    dma("gpsimd", S_CAST["ws"], ws3_b, ws3_d)
    n_ws = dma("gpsimd", S_CAST["ws"], ws2_b, ws2_d)
    cast_done["ws"] = (S_CAST["ws"], n_ws)

    wt_ptr = [0]
    wt2_ptr = [0]

    def t_load(src_rows):
        sl = wt_ptr[0] % 8
        wt_ptr[0] += 1
        dst = WT[:, sl * 512: sl * 512 + src_rows.shape[0]]
        P.use("sync", ("wt", sl))
        n = P.emit("sync", lambda e, o=dst, i=src_rows: e.dma_start(out=o, in_=i, transpose=True), S_WT[sl], 16)
        return dst, (S_WT[sl], n), ("wt", sl)

    def t_load2(src_rows):
        sl = wt2_ptr[0] % 4
        wt2_ptr[0] += 1
        dst = WT2[:, sl * 1024: sl * 1024 + src_rows.shape[0]]
        P.use("sync", ("wu", sl))
        n = P.emit("sync", lambda e, o=dst, i=src_rows: e.dma_start(out=o, in_=i, transpose=True), S_WU[sl], 16)
        return dst, (S_WU[sl], n), ("wu", sl)

    dma("sync", S_CONST, IDB[:], idb_d)
    dma("sync", S_CONST, ID32[:], id32_d)
    dma("sync", S_CONST, GBS[:], gb_d.rearrange("(a b) -> a b", a=1))
    VSTG = WT[:].bitcast(F32)[0:8, 0:768]
    vec_list = [(u_d, V_U), (ts_d.rearrange("x y c -> (x y c)"), V_TS), (ln1w_d, V_L1W),
                (ln1b_d, V_L1B), (ln2w_d, V_L2W), (ln2b_d, V_L2B)]
    for i, (src, _) in enumerate(vec_list):
        dma("sync", S_CONST, VSTG[:, i * 128:(i + 1) * 128], src.rearrange("(a b) -> a b", b=128))
    GWS = WT2[0:16, 0:1024]
    dma("gpsimd", S_GW, GWS, gw_d)
    n_in_const = S_CONST.n
    n_gw = S_GW.n

    P.emit("gpsimd", lambda e: e.memset(ONECOL[:], 1.0))
    P.emit("gpsimd", lambda e: e.memset(ONEROW[:], 1.0))
    P.emit("gpsimd", lambda e: e.memset(CAR[:, 0:NCB], 1.0))
    P.emit("gpsimd", lambda e: e.memset(CAR[:, NCB:2 * NCB], 0.0))
    P.emit("gpsimd", lambda e: e.memset(SCR[:, 0:128], 1.0), S_MEM, 1)
    n_memset = S_MEM.n

    P.wait("tensor", S_CONST, n_in_const)
    P.wait("tensor", S_MEM, n_memset)
    for i in range(len(vec_list)):
        P.emit("tensor", lambda e, i=i: e.matmul(
            PS[T0][:, i * 8:(i + 1) * 8], VSTG[:, i * 128:(i + 1) * 128], ID32[0:8, 0:8],
            is_transpose=True, start=(i == 0), stop=(i == len(vec_list) - 1)), TE)
    t_vec = TE.n
    P.wait("scalar", TE, t_vec)
    for i, (_, vidx) in enumerate(vec_list):
        P.emit("scalar", lambda e, i=i, v=vidx: e.copy(vcols(v), PS[T0][:, i * 8:(i + 1) * 8]), SE)
    se_vec = SE.n
    P.release(("bank", T0), SE)
    for _s in range(6):
        P.release(("wt", _s), TE, t_vec)
    P.wait("vector", SE, se_vec)
    P.emit("vector", lambda e: e.tensor_scalar(vcols(V_1PTS), vcols(V_TS), 1.0, None, ALU.add, ALU.bypass))
    P.emit("vector", lambda e: e.tensor_scalar(vcols(V_NTS), vcols(V_TS), -1.0, None, ALU.mult, ALU.bypass))
    P.emit("vector", lambda e: e.tensor_scalar(vcols(V_1MTS), vcols(V_TS), -1.0, 1.0, ALU.mult, ALU.add), VE)
    ve_vcols = VE.n

    P.wait("tensor", S_GW, n_gw)
    for kc in range(NCB):
        P.emit("tensor", lambda e, kc=kc: e.matmul(
            PS[T1][:].bitcast(BF16)[:, kc * 16:(kc + 1) * 16],
            GWS[:, kc * 128:(kc + 1) * 128], IDB[0:16, 0:16],
            is_transpose=True, start=(kc == 0), stop=(kc == 7)), TE)
    t_gate = TE.n
    P.wait("scalar", TE, t_gate)
    P.emit("scalar", lambda e: e.copy(GATET[:], PS[T1][:].bitcast(BF16)[:, 0:NCB * E]), SE)
    se_gate = SE.n
    P.release(("bank", T1), SE)
    P.release(("wu", 0), SE)
    P.emit("tensor", lambda e: e.matmul(PS[SP2][:, 0:16], ONEROW[:], GBS[:], start=True, stop=True), TE)
    t_gbr = TE.n
    P.wait("scalar", TE, t_gbr)
    P.emit("scalar", lambda e: e.copy(GBR[:], PS[SP2][:, 0:16]), SE)
    P.release(("bank", SP2), SE)

    XTM = lambda sl: AR[:, 10240 + sl * 1024: 10240 + (sl + 1) * 1024]
    SQD = AR[:, 9216:10240]
    se_tb = {}
    for tb in range(NTB):
        sl = tb % 3
        CSX = lambda i, sl=sl: AR[:, 13320 + sl * 16 + i: 13321 + sl * 16 + i]
        P.use("sync", ("xtm", sl))
        n_ld = dma("sync", S_XT[sl], XTM(sl), x_d[tb * 128:(tb + 1) * 128, :])
        P.wait("vector", S_XT[sl], n_ld)
        P.use("vector", ("csx", sl))
        P.emit("vector", lambda e, sl=sl, CSX=CSX: e.tensor_reduce(CSX(0), XTM(sl), AX.X, ALU.add), VE)
        P.drain("vector")
        P.wait("scalar", S_XT[sl], n_ld)
        P.use("scalar", ("sqd",))
        P.use("scalar", ("csxs", sl))
        P.emit("scalar", lambda e, sl=sl, CSX=CSX: e.activation(SQD, XTM(sl), AF.Square, accum_out=CSX(1)), SE)
        se_sq = SE.n
        P.wait("vector", SE, se_sq)
        P.emit("vector", lambda e, CSX=CSX: e.tensor_scalar(CSX(2), CSX(0), 1.0 / C, None, ALU.mult, ALU.bypass))
        P.emit("vector", lambda e, CSX=CSX: e.tensor_scalar(CSX(3), CSX(1), 1.0 / C, None, ALU.mult, ALU.bypass))
        P.drain("vector")
        P.emit("vector", lambda e, CSX=CSX: e.tensor_scalar(CSX(4), CSX(2), CSX(2), None, ALU.mult, ALU.bypass))
        P.drain("vector")
        P.emit("vector", lambda e, CSX=CSX: e.tensor_scalar(CSX(5), CSX(3), CSX(4), LN_EPS, ALU.subtract, ALU.add), VE)
        ve_var = VE.n
        P.wait("scalar", VE, ve_var)
        P.emit("scalar", lambda e, CSX=CSX: e.activation(CSX(6), CSX(5), AF.Ln))
        P.drain("scalar")
        P.emit("scalar", lambda e, CSX=CSX: e.activation(CSX(6), CSX(6), AF.Exp, scale=-0.5), SE)
        se_rstd = SE.n
        P.wait("vector", SE, se_rstd)
        P.emit("vector", lambda e, CSX=CSX, sl=sl: e.tensor_scalar(XTM(sl), XTM(sl), CSX(2), CSX(6), ALU.subtract, ALU.mult), VE)
        ve_xn = VE.n
        P.emit("vector", lambda e, CSX=CSX, sl=sl: e.tensor_reduce(CSX(7), XTM(sl), AX.X, ALU.add), VE)
        P.drain("vector")
        P.wait("scalar", VE, ve_xn)
        P.emit("scalar", lambda e, CSX=CSX, sl=sl: e.activation(SQD, XTM(sl), AF.Square, accum_out=CSX(8)), SE)
        se_sq1 = SE.n
        P.release(("sqd",), SE)
        P.wait("vector", SE, se_sq1)
        P.emit("vector", lambda e, CSX=CSX, tb=tb: e.tensor_scalar(MU1C[:, tb:tb + 1], CSX(7), 1.0 / C, None, ALU.mult, ALU.bypass))
        P.emit("vector", lambda e, CSX=CSX: e.tensor_scalar(CSX(9), CSX(8), 1.0 / C, None, ALU.mult, ALU.bypass))
        P.drain("vector")
        P.emit("vector", lambda e, CSX=CSX, tb=tb: e.tensor_scalar(CSX(10), MU1C[:, tb:tb + 1], MU1C[:, tb:tb + 1], None, ALU.mult, ALU.bypass))
        P.drain("vector")
        P.emit("vector", lambda e, CSX=CSX: e.tensor_scalar(CSX(9), CSX(9), CSX(10), LN_EPS, ALU.subtract, ALU.add), VE)
        ve_v1 = VE.n
        P.wait("scalar", VE, ve_v1)
        P.emit("scalar", lambda e, CSX=CSX: e.activation(CSX(9), CSX(9), AF.Ln))
        P.drain("scalar")
        P.emit("scalar", lambda e, CSX=CSX, tb=tb: e.activation(RS1C[:, tb:tb + 1], CSX(9), AF.Exp, scale=-0.5), SE)
        prev_se = SE.n
        P.release(("csx", sl), SE)
        P.release(("csxs", sl), VE, ve_v1)
        toks = [(VE, ve_v1), (SE, prev_se)]
        for g in range(2):
            bank = T0 if g == 0 else T1
            P.use("tensor", ("bank", bank))
            P.wait("tensor", VE, ve_xn)
            for j in range(4):
                cb = g * 4 + j
                P.emit("tensor", lambda e, sl=sl, cb=cb, j=j, b=bank: e.matmul(
                    PS[b][:, j * 128:(j + 1) * 128], XTM(sl)[:, cb * 128:(cb + 1) * 128],
                    ID32[:], is_transpose=True, start=(j == 0), stop=(j == 3)), TE)
            t_tr = TE.n
            toks.append((TE, t_tr))
            P.wait("scalar", TE, t_tr)
            ps4 = PS[bank][:].rearrange("p (j t) -> p j t", t=128)
            P.emit("scalar", lambda e, d=A3[:, g * 4:g * 4 + 4, tb * 128:(tb + 1) * 128], s=ps4: e.copy(d, s), SE)
            P.emit("scalar", lambda e, d=XNB3[:, g * 4:g * 4 + 4, tb * 128:(tb + 1) * 128], s=ps4: e.copy(d, s), SE)
            P.release(("bank", bank), SE)
        P.release_multi(("xtm", sl), toks)
        se_tb[tb] = SE.n
    se_p0 = SE.n

    if DEBUG:
        P.wait("sync", SE, se_p0)
        n_dbg0 = dma("sync", OUT, dbg["t_p0"], A[:])

    HB = lambda cb, a, b: AR[:, 3104 + cb * 513 + a: 3104 + cb * 513 + b]
    rows_mu = ROWS[0:1, 0:512]
    rows_rs = ROWS[0:1, 512:1024]
    P.wait("vector", VE, ve_vcols)
    for cb in range(NCB):
        P.emit("vector", lambda e, cb=cb: e.tensor_copy(HB(cb, 0, 1), vcol(V_TS, cb)), VE)
    ve_xxx_tile = {}
    for tau in range(NTILE):
        t0 = tau * TN
        P.use("tensor", ("bank", ST))
        P.use("tensor", ("bank", SP2))
        P.wait("tensor", SE, se_tb[tau * 4 + 3])
        for j in range(4):
            P.emit("tensor", lambda e, j=j, tau=tau: e.matmul(
                PS[ST][0:1, j * 128:(j + 1) * 128], MU1C[:, tau * 4 + j:tau * 4 + j + 1],
                ID32[:], is_transpose=True, start=(j == 0), stop=(j == 3)), TE)
        for j in range(4):
            P.emit("tensor", lambda e, j=j, tau=tau: e.matmul(
                PS[SP2][0:1, j * 128:(j + 1) * 128], RS1C[:, tau * 4 + j:tau * 4 + j + 1],
                ID32[:], is_transpose=True, start=(j == 0), stop=(j == 3)), TE)
        t_rows = TE.n
        P.use("scalar", ("rows",))
        P.wait("scalar", TE, t_rows)
        P.emit("scalar", lambda e: e.copy(rows_mu, PS[ST][0:1, 0:512]))
        P.emit("scalar", lambda e: e.copy(rows_rs, PS[SP2][0:1, 0:512]), SE)
        se_rows = SE.n
        P.release(("bank", ST), SE)
        P.release(("bank", SP2), SE)
        P.use("tensor", ("bank", MM2))
        P.use("tensor", ("bank", MM3))
        P.wait("tensor", SE, se_rows)
        P.emit("tensor", lambda e: e.matmul(PS[MM2][:], ONEROW[:], rows_mu, start=True, stop=True), TE)
        P.emit("tensor", lambda e: e.matmul(PS[MM3][:], ONEROW[:], rows_rs, start=True, stop=True), TE)
        t_bc = TE.n
        P.release(("rows",), TE)
        P.wait("vector", TE, t_bc)
        for cb in range(NCB):
            tmp = AR[:, 7208 + (cb % 2) * 512: 7208 + (cb % 2) * 512 + 512]
            P.emit("vector", lambda e, cb=cb, t0=t0: e.tensor_tensor(HB(cb, 1, 513), a_sl(cb, t0, t0 + TN), PS[MM2][:], ALU.subtract))
            P.emit("vector", lambda e, cb=cb: e.tensor_tensor(HB(cb, 1, 513), HB(cb, 1, 513), PS[MM3][:], ALU.mult))
            P.emit("vector", lambda e, cb=cb: e.tensor_scalar(HB(cb, 1, 513), HB(cb, 1, 513), vcol(V_L1W, cb), vcol(V_L1B, cb), ALU.mult, ALU.add))
            P.emit("vector", lambda e, cb=cb, tmp=tmp: e.tensor_scalar(tmp, HB(cb, 1, 513), vcol(V_1PTS, cb), None, ALU.mult, ALU.bypass))
            P.emit("vector", lambda e, cb=cb, tmp=tmp, t0=t0: e.scalar_tensor_tensor(
                xx_sl(cb, t0, t0 + TN), HB(cb, 0, 512), vcol(V_NTS, cb), tmp, ALU.mult, ALU.add), VE)
        ve_xxx_tile[tau] = VE.n
        P.release(("bank", MM2), VE)
        P.release(("bank", MM3), VE)
        if tau < NTILE - 1:
            for cb in range(NCB):
                P.emit("vector", lambda e, cb=cb: e.tensor_copy(HB(cb, 0, 1), HB(cb, 512, 513)), VE)
    ve_p1 = VE.n

    if DEBUG:
        P.wait("sync", VE, ve_p1)
        dma("sync", OUT, dbg["t_xxx"], XX[:])

    kv_k = lambda j, a, b: AR[:, j * 2560 + a: j * 2560 + b]
    kv_v = lambda j, a, b: AR[:, j * 2560 + 1024 + a: j * 2560 + 1024 + b]
    kv_r = lambda j: AR[:, j * 2560 + 2048: j * 2560 + 2560].bitcast(BF16)
    SBASE = lambda p: 10240 + p * 770
    S_EK = lambda p: AR[:, SBASE(p): SBASE(p) + 128]
    S_EU = lambda p: AR[:, SBASE(p) + 128: SBASE(p) + 256]
    S_T1 = lambda p: AR[:, SBASE(p) + 256: SBASE(p) + 384]
    S_T2 = lambda p: AR[:, SBASE(p) + 384: SBASE(p) + 512]
    S_DB = lambda p: AR[:, SBASE(p) + 512: SBASE(p) + 641]
    S_AB = lambda p: AR[:, SBASE(p) + 641: SBASE(p) + 770]
    S_ONE = SCR[:, 0:128]

    prod_done = {}

    def emit_prod_half(quad, h):
        obs = [quad * 4 + i for i in range(4)]
        for kind, fam_b, fam in (("k", wk_b, "wk"), ("v", wv_b, "wv"), ("r", wr_b, "wr")):
            if h == 0 and quad == 0 and kind == "k":
                pass
            toks = []
            for kc in range(NCB):
                P.waitp("sync", cast_done[fam])
                tile, retok, wtok = t_load(fam_b[quad * 512:(quad + 1) * 512, kc * 128:(kc + 1) * 128])
                first_mm_te = None
                for ob_l in range(4):
                    for tt_l in range(2):
                        bank = ob_l * 2 + tt_l
                        tt = 2 * h + tt_l
                        if kc == 0:
                            P.use("tensor", ("bank", bank))
                            if kind == "k" and tt_l == 0:
                                P.use("tensor", ("kvr", ob_l))
                        if ob_l == 0 and tt_l == 0:
                            P.waitp("tensor", retok)
                            P.wait("tensor", VE, ve_p1)
                        P.emit("tensor", lambda e, b=bank, kc=kc, t=tile, tt=tt, ob_l=ob_l: e.matmul(
                            PS[b][:], t[:, ob_l * 128:(ob_l + 1) * 128], xx_sl(kc, tt * TN, (tt + 1) * TN),
                            start=(kc == 0), stop=(kc == 7)), TE)
                P.release(wtok, TE)
            for ob_l in range(4):
                for tt_l in range(2):
                    bank = ob_l * 2 + tt_l
                    j = ob_l
                    P.wait("scalar", TE, TE.n)
                    if kind == "k":
                        dst = kv_k(j, tt_l * TN, (tt_l + 1) * TN)
                    elif kind == "v":
                        dst = kv_v(j, tt_l * TN, (tt_l + 1) * TN)
                    else:
                        dst = kv_r(j)[:, tt_l * TN:(tt_l + 1) * TN]
                    P.emit("scalar", lambda e, d=dst, b=bank: e.copy(d, PS[b][:]), SE)
                    P.release(("bank", bank), SE)
        for cb in obs:
            prod_done[(cb, h)] = SE.n

    def emit_scan_quad(cbs, h):
        TC = 128
        for p, cb in enumerate(cbs):
            P.wait("vector", SE, prod_done[(cb, h)])
        for q in range(8):
            a = q * TC
            ga = h * 1024 + a
            se_ek = [0] * 4; ve_m = [0] * 4; se_rm = [0] * 4; ve_den = [0] * 4; se_sig = [0] * 4
            for p, cb in enumerate(cbs):
                j = cb % 4
                kq = kv_k(j, a, a + TC)
                P.wait("scalar", S_MEM, n_memset)
                P.emit("scalar", lambda e, k=kq, p=p: e.activation(S_EK(p), k, AF.Exp))
                P.emit("scalar", lambda e, k=kq, p=p, cb=cb: e.activation(S_EU(p), k, AF.Exp, bias=vcol(V_U, cb)), SE)
                se_ek[p] = SE.n
            for p, cb in enumerate(cbs):
                dcar = CAR[:, cb:cb + 1]
                P.wait("vector", SE, se_ek[p])
                P.emit("vector", lambda e, d=dcar, p=p: e.tensor_copy(S_DB(p)[:, 0:1], d))
                P.emit("vector", lambda e, d=dcar, p=p: e.tensor_tensor_scan(S_DB(p)[:, 1:129], S_ONE, S_EK(p), d, ALU.mult, ALU.add))
                P.emit("vector", lambda e, p=p: e.tensor_tensor(S_T1(p), S_DB(p)[:, 0:128], S_EK(p), ALU.max), VE)
                ve_m[p] = VE.n
            for p in range(4):
                P.wait("scalar", VE, ve_m[p])
                P.emit("scalar", lambda e, p=p: e.activation(S_T2(p), S_T1(p), AF.Ln))
                P.emit("scalar", lambda e, p=p: e.activation(S_T1(p), S_T2(p), AF.Exp, scale=-1.0), SE)
                se_rm[p] = SE.n
            for p, cb in enumerate(cbs):
                j = cb % 4
                vq = kv_v(j, a, a + TC)
                acar = CAR[:, NCB + cb:NCB + cb + 1]
                dcar = CAR[:, cb:cb + 1]
                P.wait("vector", SE, se_rm[p])
                P.emit("vector", lambda e, p=p: e.tensor_tensor(S_T2(p), S_DB(p)[:, 0:128], S_T1(p), ALU.mult))
                P.emit("vector", lambda e, p=p: e.tensor_tensor(S_EK(p), S_EK(p), S_T1(p), ALU.mult))
                P.emit("vector", lambda e, v=vq, p=p: e.tensor_tensor(S_EK(p), S_EK(p), v, ALU.mult))
                P.emit("vector", lambda e, ac=acar, p=p: e.tensor_copy(S_AB(p)[:, 0:1], ac))
                P.emit("vector", lambda e, ac=acar, p=p: e.tensor_tensor_scan(S_AB(p)[:, 1:129], S_T2(p), S_EK(p), ac, ALU.mult, ALU.add))
                P.emit("vector", lambda e, d=dcar, p=p: e.tensor_copy(d, S_DB(p)[:, 128:129]))
                P.emit("vector", lambda e, ac=acar, p=p: e.tensor_copy(ac, S_AB(p)[:, 128:129]))
                P.emit("vector", lambda e, p=p: e.tensor_tensor(S_T1(p), S_DB(p)[:, 0:128], S_AB(p)[:, 0:128], ALU.mult))
                P.emit("vector", lambda e, v=vq, p=p: e.tensor_tensor(S_T2(p), S_EU(p), v, ALU.mult))
                P.emit("vector", lambda e, p=p: e.tensor_tensor(S_T1(p), S_T1(p), S_T2(p), ALU.add))
                P.emit("vector", lambda e, p=p: e.tensor_tensor(S_T2(p), S_DB(p)[:, 0:128], S_EU(p), ALU.add), VE)
                ve_den[p] = VE.n
            for p, cb in enumerate(cbs):
                j = cb % 4
                rq = kv_r(j)[:, a:a + TC]
                P.wait("scalar", VE, ve_den[p])
                P.emit("scalar", lambda e, p=p: e.activation(S_T2(p), S_T2(p), AF.Ln))
                P.emit("scalar", lambda e, p=p: e.activation(S_T2(p), S_T2(p), AF.Exp, scale=-1.0))
                P.emit("scalar", lambda e, r=rq, p=p: e.activation(S_EU(p), r, AF.Exp, scale=-1.0))
                P.emit("scalar", lambda e, p=p: e.add(S_EU(p), S_EU(p), 1.0))
                P.emit("scalar", lambda e, p=p: e.activation(S_EU(p), S_EU(p), AF.Ln))
                P.emit("scalar", lambda e, p=p: e.activation(S_EU(p), S_EU(p), AF.Exp, scale=-1.0), SE)
                se_sig[p] = SE.n
            for p, cb in enumerate(cbs):
                P.wait("vector", SE, se_sig[p])
                P.emit("vector", lambda e, p=p: e.tensor_tensor(S_T1(p), S_T1(p), S_T2(p), ALU.mult))
                P.emit("vector", lambda e, p=p: e.tensor_tensor(S_T1(p), S_T1(p), S_EU(p), ALU.mult))
                P.emit("vector", lambda e, cb=cb, ga=ga, p=p: e.tensor_tensor(a_sl(cb, ga, ga + TC), S_T1(p), a_sl(cb, ga, ga + TC), ALU.add), VE)
        for cb in cbs:
            P.release(("kvr", cb % 4), VE)

    emit_prod_half(0, 0)
    emit_scan_quad([0, 1, 2, 3], 0)
    emit_prod_half(0, 1)
    emit_scan_quad([0, 1, 2, 3], 1)
    emit_prod_half(1, 0)
    emit_scan_quad([4, 5, 6, 7], 0)
    emit_prod_half(1, 1)
    emit_scan_quad([4, 5, 6, 7], 1)
    ve_scan = VE.n

    if DEBUG:
        P.wait("sync", VE, ve_scan)
        dma("sync", OUT, dbg["t_x1"], A[:])

    KCSQ = AR[:, 0:8192].bitcast(BF16)
    H2 = lambda cb, a, b: AR[:, 8192 + cb * 512 + a: 8192 + cb * 512 + b]
    RCT = AR[:, 12288:12800]
    VVT = AR[:, 12800:13312]
    TROW = SCR[0:1, 0:512]

    P.use("tensor", ("bank", ST))
    P.wait("tensor", VE, ve_scan)
    if DEBUG:
        P.wait("tensor", OUT, OUT.n)
    for cb in range(NCB):
        P.emit("tensor", lambda e, cb=cb: e.matmul(PS[ST][0:1, 0:1], ONECOL[:], A3[:, cb, 2047:2048],
                                                   start=(cb == 0), stop=(cb == 7)), TE)
    P.wait("scalar", VE, ve_scan)
    P.emit("scalar", lambda e: e.activation(H2P[:], A3[:, 0:8, 2047:2048].rearrange("p a b -> p (a b)"), AF.Square), SE)
    se_wsq = SE.n
    P.use("tensor", ("bank", SP2))
    P.wait("tensor", SE, se_wsq)
    for cb in range(NCB):
        P.emit("tensor", lambda e, cb=cb: e.matmul(PS[SP2][0:1, 0:1], ONECOL[:], H2P[:, cb:cb + 1],
                                                   start=(cb == 0), stop=(cb == 7)), TE)
    t_wssq = TE.n
    P.use("scalar", ("rows",))
    P.wait("scalar", TE, t_wssq)
    P.emit("scalar", lambda e: e.copy(ROWS[0:1, 0:1], PS[ST][0:1, 0:1]))
    P.emit("scalar", lambda e: e.copy(ROWS[0:1, 512:513], PS[SP2][0:1, 0:1]), SE)
    se_wrow = SE.n
    P.release(("bank", ST), SE)
    P.release(("bank", SP2), SE)
    P.use("vector", ("scr_p3",))
    P.wait("vector", SE, se_wrow)
    P.emit("vector", lambda e: e.tensor_scalar(rows_mu[:, 0:1], rows_mu[:, 0:1], 1.0 / C, None, ALU.mult, ALU.bypass))
    P.emit("vector", lambda e: e.tensor_scalar(rows_rs[:, 0:1], rows_rs[:, 0:1], 1.0 / C, None, ALU.mult, ALU.bypass))
    P.drain("vector")
    P.emit("vector", lambda e: e.tensor_scalar(TROW[:, 0:1], rows_mu[:, 0:1], rows_mu[:, 0:1], None, ALU.mult, ALU.bypass))
    P.drain("vector")
    P.emit("vector", lambda e: e.tensor_scalar(rows_rs[:, 0:1], rows_rs[:, 0:1], TROW[:, 0:1], LN_EPS, ALU.subtract, ALU.add), VE)
    ve_wv = VE.n
    P.wait("scalar", VE, ve_wv)
    P.emit("scalar", lambda e: e.activation(rows_rs[:, 0:1], rows_rs[:, 0:1], AF.Ln))
    P.drain("scalar")
    P.emit("scalar", lambda e: e.activation(rows_rs[:, 0:1], rows_rs[:, 0:1], AF.Exp, scale=-0.5), SE)
    se_wst = SE.n
    P.use("tensor", ("bank", SP2))
    P.wait("tensor", SE, se_wst)
    P.emit("tensor", lambda e: e.matmul(PS[SP2][:, 0:1], ONEROW[:], rows_mu[:, 0:1], start=True, stop=True))
    P.emit("tensor", lambda e: e.matmul(PS[SP2][:, 1:2], ONEROW[:], rows_rs[:, 0:1], start=True, stop=True), TE)
    t_wbc = TE.n
    P.release(("rows",), TE)
    P.wait("vector", TE, t_wbc)
    for cb in range(NCB):
        P.emit("vector", lambda e, cb=cb: e.tensor_tensor(H2P[:, cb:cb + 1], A3[:, cb, 2047:2048], PS[SP2][:, 0:1], ALU.subtract))
        P.drain("vector")
        P.emit("vector", lambda e, cb=cb: e.tensor_tensor(H2P[:, cb:cb + 1], H2P[:, cb:cb + 1], PS[SP2][:, 1:2], ALU.mult))
        P.drain("vector")
        P.emit("vector", lambda e, cb=cb: e.tensor_scalar(H2P[:, cb:cb + 1], H2P[:, cb:cb + 1], vcol(V_L2W, cb), vcol(V_L2B, cb), ALU.mult, ALU.add), VE)
        P.drain("vector")
    ve_h2p = VE.n
    P.release(("bank", SP2), VE)

    prev_tau_ve = ve_h2p
    ve_xxx2_tau = {}
    for tau in range(NTILE):
        t0 = tau * TN
        P.use("tensor", ("bank", ST))
        for cb in range(NCB):
            P.emit("tensor", lambda e, cb=cb, t0=t0: e.matmul(PS[ST][0:1, 0:512], ONECOL[:], a_sl(cb, t0, t0 + TN),
                                                              start=(cb == 0), stop=(cb == 7)), TE)
        t_sum = TE.n
        P.use("scalar", ("rows",))
        P.wait("scalar", TE, t_sum)
        P.emit("scalar", lambda e: e.copy(rows_mu, PS[ST][0:1, 0:512]), SE)
        P.release(("bank", ST), SE)
        P.use("tensor", ("bank", SP2))
        for cb in range(NCB):
            sq = cb % 2
            P.use("scalar", ("h2sq", sq))
            P.wait("scalar", VE, prev_tau_ve)
            P.emit("scalar", lambda e, cb=cb, t0=t0, s=sq: e.activation(H2(s, 0, TN), a_sl(cb, t0, t0 + TN), AF.Square), SE)
            se_q = SE.n
            P.wait("tensor", SE, se_q)
            P.emit("tensor", lambda e, cb=cb, s=sq: e.matmul(PS[SP2][0:1, 0:512], ONECOL[:], H2(s, 0, TN),
                                                             start=(cb == 0), stop=(cb == 7)), TE)
            P.release(("h2sq", sq), TE)
        t_ssq = TE.n
        P.wait("scalar", TE, t_ssq)
        P.emit("scalar", lambda e: e.copy(rows_rs, PS[SP2][0:1, 0:512]), SE)
        se_ssq = SE.n
        P.release(("bank", SP2), SE)
        P.use("vector", ("scr_p3",))
        P.wait("vector", SE, se_ssq)
        P.emit("vector", lambda e: e.tensor_scalar(rows_mu, rows_mu, 1.0 / C, None, ALU.mult, ALU.bypass))
        P.emit("vector", lambda e: e.tensor_scalar(rows_rs, rows_rs, 1.0 / C, None, ALU.mult, ALU.bypass))
        P.emit("vector", lambda e: e.tensor_tensor(TROW, rows_mu, rows_mu, ALU.mult))
        P.emit("vector", lambda e: e.tensor_tensor(rows_rs, rows_rs, TROW, ALU.subtract))
        P.emit("vector", lambda e: e.tensor_scalar(rows_rs, rows_rs, LN_EPS, None, ALU.add, ALU.bypass), VE)
        ve_v2 = VE.n
        P.release(("scr_p3",), VE)
        P.wait("scalar", VE, ve_v2)
        P.emit("scalar", lambda e: e.activation(rows_rs, rows_rs, AF.Ln))
        P.emit("scalar", lambda e: e.activation(rows_rs, rows_rs, AF.Exp, scale=-0.5), SE)
        se_st2 = SE.n
        P.use("tensor", ("bank", MM2))
        P.use("tensor", ("bank", MM3))
        P.wait("tensor", SE, se_st2)
        P.emit("tensor", lambda e: e.matmul(PS[MM2][:], ONEROW[:], rows_mu, start=True, stop=True), TE)
        P.emit("tensor", lambda e: e.matmul(PS[MM3][:], ONEROW[:], rows_rs, start=True, stop=True), TE)
        t_bc2 = TE.n
        P.release(("rows",), TE)
        P.wait("vector", TE, t_bc2)
        for cb in range(NCB):
            P.emit("vector", lambda e, cb=cb, t0=t0: e.tensor_tensor(H2(cb, 0, TN), a_sl(cb, t0, t0 + TN), PS[MM2][:], ALU.subtract))
            P.emit("vector", lambda e, cb=cb: e.tensor_tensor(H2(cb, 0, TN), H2(cb, 0, TN), PS[MM3][:], ALU.mult))
            P.emit("vector", lambda e, cb=cb: e.tensor_scalar(H2(cb, 0, TN), H2(cb, 0, TN), vcol(V_L2W, cb), vcol(V_L2B, cb), ALU.mult, ALU.add))
            P.emit("vector", lambda e, cb=cb: e.tensor_scalar(VVT, H2(cb, 0, TN), vcol(V_1MTS, cb), None, ALU.mult, ALU.bypass))
            P.emit("vector", lambda e, cb=cb, t0=t0: e.scalar_tensor_tensor(
                xx_sl(cb, t0, t0 + 1), H2P[:, cb:cb + 1], vcol(V_TS, cb), VVT[:, 0:1], ALU.mult, ALU.add))
            P.emit("vector", lambda e, cb=cb, t0=t0: e.scalar_tensor_tensor(
                xx_sl(cb, t0 + 1, t0 + TN), H2(cb, 0, TN - 1), vcol(V_TS, cb), VVT[:, 1:TN], ALU.mult, ALU.add), VE)
        ve_xxx2_tau[tau] = VE.n
        P.release(("bank", MM2), VE)
        P.release(("bank", MM3), VE)
        for cb in range(NCB):
            P.emit("vector", lambda e, cb=cb: e.tensor_copy(H2P[:, cb:cb + 1], H2(cb, TN - 1, TN)), VE)
        prev_tau_ve = VE.n

    RCF = lambda ob: AR[:, 8192 + ob * 512: 8192 + (ob + 1) * 512]
    prev_vv_ve = prev_tau_ve
    for tau in range(NTILE):
        t0 = tau * TN
        ve_xxx2 = ve_xxx2_tau[tau]
        for hbg in range(4):
            for kc in range(NCB):
                P.waitp("sync", cast_done["wck"])
                tile, retok, wtok = t_load2(wck_b[hbg * 1024:(hbg + 1) * 1024, kc * 128:(kc + 1) * 128])
                for hb_l in range(8):
                    if kc == 0:
                        P.use("tensor", ("bank", hb_l))
                    if hb_l == 0:
                        P.waitp("tensor", retok)
                        P.wait("tensor", VE, ve_xxx2)
                    P.emit("tensor", lambda e, b=hb_l, t=tile, hb_l=hb_l, t0=t0, kc=kc: e.matmul(
                        PS[b][:], t[:, hb_l * 128:(hb_l + 1) * 128], xx_sl(kc, t0, t0 + TN),
                        start=(kc == 0), stop=(kc == 7)), TE)
                P.release(wtok, TE)
            t_kc = TE.n
            for hb_l in range(8):
                hb = hbg * 8 + hb_l
                P.use("scalar", ("scanbuf",))
                P.wait("scalar", TE, t_kc)
                P.emit("scalar", lambda e, b=hb_l: e.activation(SCR[:], PS[b][:], AF.Relu), SE)
                P.release(("bank", hb_l), SE)
                P.emit("scalar", lambda e, hb=hb: e.activation(KCSQ[:, hb * 512:(hb + 1) * 512], SCR[:], AF.Square), SE)
                P.release(("scanbuf",), SE)
        se_kcsq = SE.n
        for kc in range(NCB):
            P.waitp("sync", cast_done["wcr"])
            tile, retok, wtok = t_load2(wcr_b[0:1024, kc * 128:(kc + 1) * 128])
            for ob in range(8):
                if kc == 0:
                    P.use("tensor", ("bank", ob))
                if ob == 0:
                    P.waitp("tensor", retok)
                P.emit("tensor", lambda e, b=ob, t=tile, ob=ob, t0=t0, kc=kc: e.matmul(
                    PS[b][:], t[:, ob * 128:(ob + 1) * 128], xx_sl(kc, t0, t0 + TN),
                    start=(kc == 0), stop=(kc == 7)), TE)
            P.release(wtok, TE)
        t_rc = TE.n
        for ob in range(8):
            P.wait("scalar", TE, t_rc)
            if ob == 0:
                P.wait("scalar", VE, prev_vv_ve)
            P.emit("scalar", lambda e, ob=ob, b=ob: e.activation(RCF(ob), PS[b][:], AF.Exp, scale=-1.0), SE)
            P.release(("bank", ob), SE)
            P.emit("scalar", lambda e, ob=ob: e.add(RCF(ob), RCF(ob), 1.0))
            P.emit("scalar", lambda e, ob=ob: e.activation(RCF(ob), RCF(ob), AF.Ln))
            P.emit("scalar", lambda e, ob=ob: e.activation(RCF(ob), RCF(ob), AF.Exp, scale=-1.0), SE)
        se_rc = SE.n
        for hb in range(32):
            P.waitp("sync", cast_done["wcv"])
            tile, retok, wtok = t_load2(wcv_b[0:1024, hb * 128:(hb + 1) * 128])
            for cb in range(8):
                if hb == 0:
                    P.use("tensor", ("bank", cb))
                if cb == 0:
                    P.waitp("tensor", retok)
                    if hb == 0:
                        P.wait("tensor", SE, se_kcsq)
                P.emit("tensor", lambda e, b=cb, t=tile, cb=cb, hb=hb: e.matmul(
                    PS[b][:], t[:, cb * 128:(cb + 1) * 128], KCSQ[:, hb * 512:(hb + 1) * 512],
                    start=(hb == 0), stop=(hb == 31)), TE)
            P.release(wtok, TE)
        t_vv = TE.n
        for cb in range(8):
            P.wait("vector", TE, t_vv)
            P.wait("vector", SE, se_rc)
            P.emit("vector", lambda e, cb=cb: e.tensor_tensor(VVT, RCF(cb), PS[cb][:], ALU.mult), VE)
            P.emit("vector", lambda e, cb=cb, t0=t0: e.tensor_tensor(a_sl(cb, t0, t0 + TN), VVT, a_sl(cb, t0, t0 + TN), ALU.add), VE)
            P.release(("bank", cb), VE)
        prev_vv_ve = VE.n
    ve_p3 = VE.n
    te_p3 = TE.n
    se_p3 = SE.n

    if DEBUG:
        P.wait("sync", VE, ve_p3)
        dma("sync", OUT, dbg["t_s"], A[:])

    GPRE = AR[:, 0:2048].bitcast(BF16)
    WS1T = AR[:, 2048:4096].bitcast(BF16)
    WS3T = AR[:, 4096:6144].bitcast(BF16)
    WS2T = AR[:, 6144:8192].bitcast(BF16)
    ZMID = AR[:, 8192:9216].bitcast(BF16)
    SILU = AR[:, 9216:9728]
    HST = lambda r: AR[:, 9728 + r * 1024: 10752 + r * 1024].bitcast(BF16)
    SCO = AR[:, 9728:9984]
    CMF = AR[0:16, 11776:12800].bitcast(BF16)

    for tb in range(NTB):
        bank = MM0 if tb % 2 == 0 else MM1
        P.use("tensor", ("bank", bank))
        if tb == 0:
            P.wait("tensor", SE, se_gate)
        for kc in range(NCB):
            P.emit("tensor", lambda e, b=bank, kc=kc, tb=tb: e.matmul(
                PS[b][:, 0:16], XNB3[:, kc, tb * 128:(tb + 1) * 128],
                GATET[:, kc * 16:(kc + 1) * 16], start=(kc == 0), stop=(kc == 7)), TE)
        t_sc = TE.n
        P.wait("scalar", TE, t_sc)
        if tb == 0:
            P.wait("scalar", VE, ve_p3)
        stb = SCO[:, tb * 16:(tb + 1) * 16]
        P.emit("scalar", lambda e, b=bank, stb=stb: e.activation(stb, PS[b][:, 0:16], AF.Exp, scale=-1.0), SE)
        se_s1 = SE.n
        P.release(("bank", bank), SE)
        P.drain("scalar")
        P.emit("scalar", lambda e, stb=stb: e.add(stb, stb, 1.0))
        P.drain("scalar")
        P.emit("scalar", lambda e, stb=stb: e.activation(stb, stb, AF.Ln))
        P.drain("scalar")
        P.emit("scalar", lambda e, stb=stb: e.activation(stb, stb, AF.Exp, scale=-1.0), SE)
        if DEBUG:
            P.emit("scalar", lambda e, tb=tb, stb=stb: e.copy(WT2[:].bitcast(F32)[:, tb * 16:(tb + 1) * 16], stb), SE)
        se_sco = SE.n
        sc = lambda x, y: SCR[:, x:y]
        P.use("vector", ("scr_p4",))
        P.wait("vector", SE, se_sco)
        P.emit("vector", lambda e, stb=stb: e.tensor_tensor(sc(96, 112), stb, GBR[:], ALU.add))
        P.drain("vector")
        P.emit("vector", lambda e: e.tensor_reduce(sc(0, 4), sc(96, 112).rearrange("p (a b) -> p a b", b=4), AX.X, ALU.max))
        P.drain("vector")
        for G in range(4):
            P.emit("vector", lambda e, G=G: e.tensor_scalar(sc(4, 8), sc(0, 4), sc(G, G + 1), 0.0, ALU.is_gt, ALU.add, accum_out=sc(8 + G, 9 + G)))
        P.drain("vector")
        for G in range(4):
            P.emit("vector", lambda e, G=G: e.tensor_scalar(sc(12 + G, 13 + G), sc(8 + G, 9 + G), 2.0, None, ALU.is_lt, ALU.bypass))
        P.drain("vector")
        for G in range(4):
            P.emit("vector", lambda e, G=G: e.tensor_scalar(sc(16 + G, 17 + G), sc(12 + G, 13 + G), 4.0, None, ALU.mult, ALU.bypass))
        P.drain("vector")
        for G in range(4):
            P.emit("vector", lambda e, G=G: e.tensor_scalar(sc(96 + 4 * G, 100 + 4 * G), sc(96 + 4 * G, 100 + 4 * G), sc(16 + G, 17 + G), -4.0, ALU.add, ALU.add))
        P.drain("vector")
        for ee in range(E):
            P.emit("vector", lambda e, ee=ee: e.tensor_scalar(sc(32, 48), sc(96, 112), sc(96 + ee, 97 + ee), 0.0, ALU.is_gt, ALU.add, accum_out=sc(48 + ee, 49 + ee)))
        P.drain("vector")
        for ee in range(E):
            P.emit("vector", lambda e, ee=ee: e.tensor_scalar(sc(64 + ee, 65 + ee), sc(48 + ee, 49 + ee), 4.0, None, ALU.is_lt, ALU.bypass))
        P.drain("vector")
        for ee in range(E):
            P.emit("vector", lambda e, ee=ee, stb=stb: e.tensor_tensor(
                stb[:, ee:ee + 1], sc(64 + ee, 65 + ee), stb[:, ee:ee + 1], ALU.mult), VE)
        P.drain("vector")
        ve_comb = VE.n
        P.release(("scr_p4",), VE)
        grp = tb % 4
        bank2 = T0 if (tb // 4) % 2 == 0 else T1
        if grp == 0:
            P.use("tensor", ("bank", bank2))
        P.wait("tensor", VE, ve_comb)
        P.emit("tensor", lambda e, stb=stb, b=bank2, grp=grp: e.matmul(
            PS[b][0:16, grp * 128:(grp + 1) * 128], stb, ID32[:],
            is_transpose=True, start=(grp == 0), stop=(grp == 3)), TE)
        if grp == 3:
            t_cmt = TE.n
            P.wait("scalar", TE, t_cmt)
            P.emit("scalar", lambda e, tb=tb, b=bank2: e.copy(CMF[:, (tb - 3) * 128:(tb + 1) * 128], PS[b][0:16, 0:512]), SE)
            P.release(("bank", bank2), SE)
    se_cmf = SE.n

    if DEBUG:
        P.wait("sync", SE, se_cmf)
        dma("sync", OUT, dbg["t_cmf"], CMF[:])
        dma("sync", OUT, dbg["t_sco"], WT2[:].bitcast(F32)[:, 0:NTB * E])
        dma("sync", OUT, dbg["t_xnb"], XNB[:])
        dma("sync", OUT, dbg["t_gt"], GATET[:])
        dma("sync", OUT, dbg["t_gbr"], GBR[:])

    W3SL = AR[:, 0:2048].bitcast(BF16)
    BAS = AR[0:16, 2048:3072].bitcast(BF16)
    P.wait("sync", TE, te_p3)
    P.wait("sync", VE, ve_p3)
    n_bas = dma("sync", S_CONST, BAS, bas_d)
    hst_r = [0]
    ve_he = 0
    w1_toks = None
    w3_toks = None

    def load_w1(e_i):
        toks = []
        for kc in range(NCB):
            P.waitp("sync", cast_done["w1"])
            tile, retok, wtok = t_load(w1_b[e_i][0:512, kc * 128:(kc + 1) * 128])
            toks.append((tile, retok, wtok))
        return toks

    def load_w3(e_i):
        toks = []
        for kc in range(NCB):
            P.waitp("sync", cast_done["w3"])
            P.use("sync", ("w3sl", kc))
            if e_i == 0 and kc == 0:
                P.wait("sync", TE, te_p3)
                P.wait("sync", VE, ve_p3)
            dstv = W3SL[:, kc * 512:(kc + 1) * 512]
            n3 = P.emit("sync", lambda e, o=dstv, i=w3_b[e_i][0:512, kc * 128:(kc + 1) * 128]: e.dma_start(
                out=o, in_=i, transpose=True), S_W3, 16)
            toks.append((dstv, (S_W3, n3), kc))
        return toks

    COMBE = AR[:, 3072:4096].bitcast(BF16)
    w1_toks = load_w1(0)
    w3_toks = load_w3(0)
    for e_i in range(E):
        cb_se = 0
        for tau in range(NTILE):
            bank = MM0 if tau % 2 == 0 else MM1
            P.use("tensor", ("bank", bank))
            if e_i == 0 and tau == 0:
                P.wait("tensor", SE, se_cmf)
                P.wait("tensor", S_CONST, n_bas)
                if DEBUG:
                    P.wait("tensor", OUT, OUT.n)
            P.emit("tensor", lambda e, e_i=e_i, tau=tau, b=bank: e.matmul(
                PS[b][:], BAS[:, e_i * 128:(e_i + 1) * 128], CMF[:, tau * 512:(tau + 1) * 512],
                start=True, stop=True), TE)
            t_cb = TE.n
            P.use("scalar", ("combe", tau))
            P.wait("scalar", TE, t_cb)
            if e_i == 0 and tau == 0:
                P.wait("scalar", VE, ve_p3)
            P.emit("scalar", lambda e, tau=tau, b=bank: e.copy(COMBE[:, tau * 512:(tau + 1) * 512], PS[b][:]), SE)
            cb_se = SE.n
            P.release(("bank", bank), SE)
        for tau in range(NTILE):
            t0 = tau * TN
            t_h1_ib = [0] * 4
            t_h3_ib = [0] * 4
            for ib in range(4):
                P.use("tensor", ("bank", ib))
                for kc in range(NCB):
                    if ib == 0 and kc == 0:
                        P.waitp("tensor", w1_toks[0][1])
                        for kc2 in range(1, NCB):
                            P.waitp("tensor", w1_toks[kc2][1])
                    P.emit("tensor", lambda e, b=ib, t=w1_toks[kc][0], ib=ib, t0=t0, kc=kc: e.matmul(
                        PS[b][:], t[:, ib * 128:(ib + 1) * 128], xnb_sl(kc, t0, t0 + TN),
                        start=(kc == 0), stop=(kc == 7)), TE)
                t_h1_ib[ib] = TE.n
            for ib in range(4):
                P.use("tensor", ("bank", 4 + ib))
                for kc in range(NCB):
                    if ib == 0 and kc == 0:
                        for kc2 in range(NCB):
                            P.waitp("tensor", w3_toks[kc2][1])
                    P.emit("tensor", lambda e, b=4 + ib, t=w3_toks[kc][0], ib=ib, t0=t0, kc=kc: e.matmul(
                        PS[b][:], t[:, ib * 128:(ib + 1) * 128], xnb_sl(kc, t0, t0 + TN),
                        start=(kc == 0), stop=(kc == 7)), TE)
                t_h3_ib[ib] = TE.n
            if tau == NTILE - 1:
                for _, _, wtok in w1_toks:
                    P.release(wtok, TE)
                for _, _, kcsl in w3_toks:
                    P.release(("w3sl", kcsl), TE)
                if e_i + 1 < E:
                    nxt1 = load_w1(e_i + 1)
                    nxt3 = load_w3(e_i + 1)
            r = hst_r[0] % 2
            for ib in range(4):
                P.use("scalar", ("silu",))
                P.wait("scalar", TE, t_h1_ib[ib])
                if e_i == 0 and tau == 0 and ib == 0:
                    P.wait("scalar", VE, ve_p3)
                P.emit("scalar", lambda e, b=ib: e.activation(SILU, PS[b][:], AF.Silu), SE)
                se_si = SE.n
                P.release(("bank", ib), SE)
                if ib == 0:
                    P.use("vector", ("hst", r))
                P.wait("vector", SE, se_si)
                if ib == 0:
                    P.wait("vector", SE, cb_se)
                P.wait("vector", TE, t_h3_ib[ib])
                P.emit("vector", lambda e, b=4 + ib: e.tensor_tensor(SILU, SILU, PS[b][:], ALU.mult), VE)
                P.release(("bank", 4 + ib), VE)
                P.emit("vector", lambda e, r=r, ib=ib, tau=tau: e.tensor_tensor(
                    HST(r)[:, ib * 512:(ib + 1) * 512], SILU, COMBE[:, tau * 512:(tau + 1) * 512], ALU.mult), VE)
                ve_he = VE.n
                P.release(("silu",), VE)
                if ib == 3:
                    P.release(("combe", tau), VE)
            P.wait("sync", VE, ve_he)
            dma("sync", S_HS[r], H_s[tau][e_i], HST(r))
            P.release(("hst", r), S_HS[r])
            hst_r[0] += 1
        if e_i + 1 < E:
            w1_toks = nxt1
            w3_toks = nxt3
    f1_hs_toks = [(S_HS[0], S_HS[0].n), (S_HS[1], S_HS[1].n)]

    te_f1 = TE.n
    P.wait("sync", TE, te_f1)
    P.wait("sync", VE, ve_he)
    P.waitp("sync", cast_done["ws"])
    for kc in range(NCB):
        P.emit("sync", lambda e, kc=kc: e.dma_start(
            out=WS1T[:, kc * 512:(kc + 1) * 512], in_=ws1_b[0:512, kc * 128:(kc + 1) * 128], transpose=True), S_WS, 16)
    for kc in range(NCB):
        P.emit("sync", lambda e, kc=kc: e.dma_start(
            out=WS3T[:, kc * 512:(kc + 1) * 512], in_=ws3_b[0:512, kc * 128:(kc + 1) * 128], transpose=True), S_WS, 16)
    for ikc in range(4):
        P.emit("sync", lambda e, ikc=ikc: e.dma_start(
            out=WS2T[:, ikc * 1024:(ikc + 1) * 1024], in_=ws2_b[0:1024, ikc * 128:(ikc + 1) * 128], transpose=True), S_WS, 16)
    re_wsz = S_WS.n

    YB = list(range(8))
    for tau in range(NTILE):
        t0 = tau * TN
        for ib in range(4):
            b1 = 0 if ib % 2 == 0 else 1
            b3 = 2 if ib % 2 == 0 else 3
            P.use("tensor", ("bank", b1))
            P.use("tensor", ("bank", b3))
            P.wait("tensor", S_WS, re_wsz)
            for kc in range(NCB):
                P.emit("tensor", lambda e, b=b1, ib=ib, kc=kc, t0=t0: e.matmul(
                    PS[b][:], WS1T[:, kc * 512 + ib * 128: kc * 512 + (ib + 1) * 128], xnb_sl(kc, t0, t0 + TN),
                    start=(kc == 0), stop=(kc == 7)), TE)
            for kc in range(NCB):
                P.emit("tensor", lambda e, b=b3, ib=ib, kc=kc, t0=t0: e.matmul(
                    PS[b][:], WS3T[:, kc * 512 + ib * 128: kc * 512 + (ib + 1) * 128], xnb_sl(kc, t0, t0 + TN),
                    start=(kc == 0), stop=(kc == 7)), TE)
            t_z = TE.n
            P.use("scalar", ("silu",))
            P.wait("scalar", TE, t_z)
            if tau == 0 and ib == 0:
                P.wait("scalar", VE, ve_he)
            P.emit("scalar", lambda e, b=b1: e.activation(SILU, PS[b][:], AF.Silu), SE)
            se_zs = SE.n
            P.release(("bank", b1), SE)
            P.use("vector", ("zmid", ib))
            P.wait("vector", SE, se_zs)
            P.emit("vector", lambda e, ib=ib, b=b3: e.tensor_tensor(ZMID[:, ib * 512:(ib + 1) * 512], SILU, PS[b][:], ALU.mult), VE)
            P.release(("bank", b3), VE)
            P.release(("silu",), VE)
        ve_zmid = VE.n
        h_re = {}
        for b in YB:
            P.use("tensor", ("bank", b))
        for e_i in range(E):
            esub = e_i % 8
            P.use("sync", ("xxh", esub))
            if tau == 0 and e_i == 0:
                P.wait("sync", TE, te_p3)
                for tok in f1_hs_toks:
                    P.waitp("sync", tok)
            h_re[e_i] = (S_H, dma("sync", S_H, XX3b[:, esub, :], H_s[tau][e_i]))
            P.waitp("sync", cast_done["w2"])
            w2t = []
            for ikc in range(4):
                w2t.append(t_load2(w2_b[e_i][0:1024, ikc * 128:(ikc + 1) * 128]))
            P.waitp("tensor", h_re[e_i])
            for ikc in range(4):
                P.waitp("tensor", w2t[ikc][1])
                for db in range(8):
                    first = (e_i == 0 and ikc == 0)
                    P.emit("tensor", lambda e, b=YB[db], t=w2t[ikc][0], db=db, esub=esub, ikc=ikc, first=first: e.matmul(
                        PS[b][:], t[:, db * 128:(db + 1) * 128],
                        XX3b[:, esub, ikc * 512:(ikc + 1) * 512],
                        start=first, stop=False), TE)
                P.release(w2t[ikc][2], TE)
            if e_i >= 8:
                P.release(("xxh", esub), TE)
            else:
                P.release(("xxh", esub), TE)
        P.wait("tensor", VE, ve_zmid)
        for ikc in range(4):
            for db in range(8):
                P.emit("tensor", lambda e, b=YB[db], db=db, ikc=ikc: e.matmul(
                    PS[b][:], WS2T[:, ikc * 1024 + db * 128: ikc * 1024 + (db + 1) * 128],
                    ZMID[:, ikc * 512:(ikc + 1) * 512], start=False, stop=(ikc == 3)), TE)
        t_y = TE.n
        for esub in range(8):
            P.release(("xxh", esub), TE, t_y)
        for ib in range(4):
            P.release(("zmid", ib), TE, t_y)
        for db in range(8):
            P.use("scalar", ("gpre", db))
            P.wait("scalar", TE, t_y)
            P.emit("scalar", lambda e, db=db, b=YB[db]: e.copy(GPRE[:, db * 512:(db + 1) * 512], PS[b][:]), SE)
            P.release(("bank", YB[db]), SE)
        se_g = SE.n
        P.wait("sync", SE, se_g)
        if DEBUG:
            dma("sync", OUT, dbg["t_g"][tau], GPRE[:])
        n_gs = dma("sync", S_GS, g_s[tau], GPRE[:])
        for db in range(8):
            P.release(("gpre", db), S_GS, S_GS.n)
    spl_g = S_GS.n
    te_f2 = TE.n
    ve_f2 = VE.n

    STM = lambda r: AR[:, r * 1024: (r + 1) * 1024]
    XT2 = lambda r: AR[:, 2048 + r * 1024: 2048 + (r + 1) * 1024]
    OTB = lambda r: AR[:, 4096 + r * 1024: 4096 + (r + 1) * 1024]
    GTL = AR[:, 6144:8192].bitcast(BF16)
    GSL = lambda r: AR[:, 8192 + r * 1024: 8192 + (r + 1) * 1024]

    first_stm = [True]
    first_xt2 = [True]
    first_gs = [True]
    for tau in range(NTILE):
        P.use("sync", ("gtile",))
        if tau == 0:
            P.wait("sync", S_GS, spl_g)
            P.wait("sync", TE, te_f2)
        n_g = dma("sync", S_GT, GTL[:], g_s[tau])
        for tb4 in range(4):
            tb = tau * 4 + tb4
            r = tb % 2
            P.use("sync", ("xt2", r))
            if first_xt2[0]:
                P.wait("sync", TE, te_f2)
                first_xt2[0] = False
            n_x = dma("sync", S_X2[r], XT2(r), x_d[tb * 128:(tb + 1) * 128, :])
            P.use("scalar", ("stm", r))
            for g in range(2):
                bank = T0 if g == 0 else T1
                P.use("tensor", ("bank", bank))
                for j in range(4):
                    cb = g * 4 + j
                    P.emit("tensor", lambda e, cb=cb, j=j, b=bank, tb=tb: e.matmul(
                        PS[b][:, j * 128:(j + 1) * 128], A3[:, cb, tb * 128:(tb + 1) * 128],
                        ID32[:], is_transpose=True, start=(j == 0), stop=(j == 3)), TE)
                t_tr = TE.n
                P.wait("scalar", TE, t_tr)
                if first_stm[0]:
                    P.wait("scalar", S_GS, spl_g)
                    first_stm[0] = False
                P.emit("scalar", lambda e, r=r, g=g, b=bank: e.copy(STM(r)[:, g * 512:(g + 1) * 512], PS[b][:]), SE)
                P.release(("bank", bank), SE)
            bank = T0 if tb % 2 == 0 else T1
            P.use("tensor", ("bank", bank))
            P.wait("tensor", S_GT, n_g)
            for cb in range(NCB):
                P.emit("tensor", lambda e, cb=cb, b=bank, tb4=tb4: e.matmul(
                    PS[b][:].bitcast(BF16)[:, cb * 128:(cb + 1) * 128],
                    GTL[:, cb * 512 + tb4 * 128: cb * 512 + tb4 * 128 + 128],
                    IDB[:], is_transpose=True, start=(cb == 0), stop=(cb == 7)), TE)
            t_gt = TE.n
            P.use("scalar", ("gsil", r))
            P.wait("scalar", TE, t_gt)
            if first_gs[0]:
                for tok in f1_hs_toks:
                    P.waitp("scalar", tok)
                P.wait("scalar", VE, ve_f2)
                first_gs[0] = False
            P.emit("scalar", lambda e, r=r, b=bank: e.activation(GSL(r), PS[b][:].bitcast(BF16)[:, 0:1024], AF.Silu), SE)
            se_gs = SE.n
            P.release(("bank", bank), SE)
            P.use("vector", ("ot", r))
            P.wait("vector", SE, se_gs)
            P.wait("vector", S_X2[r], n_x)
            P.emit("vector", lambda e, r=r: e.tensor_tensor(STM(r), STM(r), GSL(r), ALU.mult))
            P.emit("vector", lambda e, r=r: e.tensor_tensor(OTB(r), STM(r), XT2(r), ALU.add), VE)
            ve_ot = VE.n
            P.release(("stm", r), VE)
            P.release(("gsil", r), VE)
            P.release(("xt2", r), VE)
            P.wait("sync", VE, ve_ot)
            dma("sync", S_OT[r], out_d[tb * 128:(tb + 1) * 128, :], OTB(r))
            P.release(("ot", r), S_OT[r])
        P.release(("gtile",), TE)

    P.wait("sync", S_OT[0], S_OT[0].n)
    P.wait("sync", S_OT[1], S_OT[1].n)
    if DEBUG:
        P.wait("sync", OUT, OUT.n)

    with nc.Block() as block:
        @block.sync
        def _(e):
            for f in P.steps["sync"]:
                f(e)

        @block.gpsimd
        def _(e):
            for f in P.steps["gpsimd"]:
                f(e)

        @block.tensor
        def _(e):
            for f in P.steps["tensor"]:
                f(e)

        @block.scalar
        def _(e):
            for f in P.steps["scalar"]:
                f(e)

        @block.vector
        def _(e):
            for f in P.steps["vector"]:
                f(e)

    es.close()
    return nc


_NC_CACHE = None


def kernel(**inputs):
    global _NC_CACHE
    if _NC_CACHE is None:
        _NC_CACHE = build_nc()
    nc = _NC_CACHE

    x = np.ascontiguousarray(np.asarray(inputs["L0_attn_x"], dtype=np.float32))
    _ = inputs["L0_attn_time_decay"]
    basis = np.zeros((E, E * 128), dtype=ml_dtypes.bfloat16)
    for e_i in range(E):
        basis[e_i, e_i * 128:(e_i + 1) * 128] = 1.0
    common = {
        "Wk": np.ascontiguousarray(np.asarray(inputs["L0_attn_Wk"], np.float32)),
        "Wv": np.ascontiguousarray(np.asarray(inputs["L0_attn_Wv"], np.float32)),
        "Wr": np.ascontiguousarray(np.asarray(inputs["L0_attn_Wr"], np.float32)),
        "Wck": np.ascontiguousarray(np.asarray(inputs["L0_attn_Wck"], np.float32)),
        "Wcv": np.ascontiguousarray(np.asarray(inputs["L0_attn_Wcv"], np.float32)),
        "Wcr": np.ascontiguousarray(np.asarray(inputs["L0_attn_Wcr"], np.float32)),
        "ln1w": np.ascontiguousarray(np.asarray(inputs["L0_attn_ln1_w"], np.float32)),
        "ln1b": np.ascontiguousarray(np.asarray(inputs["L0_attn_ln1_b"], np.float32)),
        "ln2w": np.ascontiguousarray(np.asarray(inputs["L0_attn_ln2_w"], np.float32)),
        "ln2b": np.ascontiguousarray(np.asarray(inputs["L0_attn_ln2_b"], np.float32)),
        "tshift": np.ascontiguousarray(np.asarray(inputs["L0_attn_time_shift"], np.float32)),
        "tfirst": np.ascontiguousarray(np.asarray(inputs["L0_attn_time_first"], np.float32)),
        "gate_w": np.ascontiguousarray(np.asarray(inputs["L0_ffn_gate_w"], np.float32)),
        "gate_bias": np.ascontiguousarray(np.asarray(inputs["L0_ffn_gate_bias"], np.float32)),
        "w1": np.ascontiguousarray(np.asarray(inputs["L0_ffn_w1"], np.float32)),
        "w2": np.ascontiguousarray(np.asarray(inputs["L0_ffn_w2"], np.float32)),
        "w3": np.ascontiguousarray(np.asarray(inputs["L0_ffn_w3"], np.float32)),
        "ws1": np.ascontiguousarray(np.asarray(inputs["L0_ffn_ws1"], np.float32)),
        "ws2": np.ascontiguousarray(np.asarray(inputs["L0_ffn_ws2"], np.float32)),
        "ws3": np.ascontiguousarray(np.asarray(inputs["L0_ffn_ws3"], np.float32)),
        "identb": np.eye(128, dtype=ml_dtypes.bfloat16),
        "ident32": np.eye(128, dtype=np.float32),
        "basis": basis,
    }
    in_maps = [dict(common, x=np.ascontiguousarray(x[b])) for b in range(8)]
    res = run_bass_kernel_spmd(nc, in_maps, core_ids=list(range(8)))
    out = np.stack([np.asarray(res.results[b]["out"], np.float32) for b in range(8)], axis=0)
    return out


if __name__ == "__main__":
    build_nc()
    print("built ok")


# version 09
# speedup vs baseline: 1.2665x; 1.0766x over previous
import numpy as np
import ml_dtypes
from contextlib import ExitStack
import concourse.bass as bass
import concourse.mybir as mybir
from concourse.bass_utils import run_bass_kernel_spmd

F32 = mybir.dt.float32
BF16 = mybir.dt.bfloat16
AF = mybir.ActivationFunctionType
ALU = mybir.AluOpType
AX = mybir.AxisListType

T = 2048
C = 1024
NCB = 8
NTB = 16
NTILE = 4
TN = 512
E = 16
LN_EPS = 1e-5

DEBUG = False


class Sem:
    def __init__(self, h):
        self.h = h
        self.n = 0


class Plan:
    ENGINES = ("sync", "gpsimd", "tensor", "scalar", "vector")

    def __init__(self):
        self.steps = {e: [] for e in self.ENGINES}
        self.free = {}

    def wait(self, eng, sem, val):
        if val > sem.n:
            raise RuntimeError(f"wait for {val}, sem at {sem.n}")
        if val > 0:
            self.steps[eng].append(lambda e, s=sem.h, v=val: e.wait_ge(s, v))

    def emit(self, eng, build, sem=None, k=1):
        if sem is not None:
            sem.n += k
            self.steps[eng].append(lambda e, b=build, s=sem.h, kk=k: b(e).then_inc(s, kk))
            return sem.n
        self.steps[eng].append(build)
        return None

    def use(self, eng, key):
        for sem, val in self.free.get(key, ()):
            self.wait(eng, sem, val)

    def release(self, key, sem, val=None):
        self.free[key] = [(sem, sem.n if val is None else val)]

    def release_multi(self, key, toks):
        self.free[key] = list(toks)

    def drain(self, eng):
        self.steps[eng].append(lambda e: e.drain())

    def waitp(self, eng, tok):
        self.wait(eng, tok[0], tok[1])


def build_nc():
    nc = bass.Bass("TRN2", target_bir_lowering=False)

    def din(name, shape, dt=F32):
        return nc.dram_tensor(name, shape, dt, kind="ExternalInput").ap()

    x_d = din("x", [T, C])
    Wk_d = din("Wk", [C, C]); Wv_d = din("Wv", [C, C]); Wr_d = din("Wr", [C, C])
    Wck_d = din("Wck", [4096, C]); Wcv_d = din("Wcv", [C, 4096]); Wcr_d = din("Wcr", [C, C])
    ln1w_d = din("ln1w", [C]); ln1b_d = din("ln1b", [C])
    ln2w_d = din("ln2w", [C]); ln2b_d = din("ln2b", [C])
    ts_d = din("tshift", [1, 1, C]); u_d = din("tfirst", [C])
    gw_d = din("gate_w", [E, C]); gb_d = din("gate_bias", [E])
    w1_d = din("w1", [E, 512, C]); w2_d = din("w2", [E, C, 512]); w3_d = din("w3", [E, 512, C])
    ws1_d = din("ws1", [512, C]); ws2_d = din("ws2", [C, 512]); ws3_d = din("ws3", [512, C])
    idb_d = din("identb", [128, 128], BF16)
    id32_d = din("ident32", [128, 128], F32)
    bas_d = din("basis", [E, E * 128], BF16)

    out_d = nc.dram_tensor("out", [T, C], F32, kind="ExternalOutput").ap()

    def scrt(name, shape):
        return nc.dram_tensor(name, shape, BF16, kind="Internal").ap()

    wk_b = scrt("wk_b", [C, C]); wv_b = scrt("wv_b", [C, C]); wr_b = scrt("wr_b", [C, C])
    wck_b = scrt("wck_b", [4096, C]); wcr_b = scrt("wcr_b", [C, C]); wcv_b = scrt("wcv_b", [C, 4096])
    w1_b = scrt("w1_b", [E, 512, C]); w3_b = scrt("w3_b", [E, 512, C]); w2_b = scrt("w2_b", [E, C, 512])
    ws1_b = scrt("ws1_b", [512, C]); ws3_b = scrt("ws3_b", [512, C]); ws2_b = scrt("ws2_b", [C, 512])
    H_s = scrt("H_s", [NTILE, E, 128, 2048])
    g_s = scrt("g_s", [NTILE, 128, 4096])

    dbg = {}
    if DEBUG:
        dbg["t_p0"] = nc.dram_tensor("t_p0", [128, NCB * T], F32, kind="ExternalOutput").ap()
        dbg["t_xxx"] = nc.dram_tensor("t_xxx", [128, NCB * T], BF16, kind="ExternalOutput").ap()
        dbg["t_x1"] = nc.dram_tensor("t_x1", [128, NCB * T], F32, kind="ExternalOutput").ap()
        dbg["t_s"] = nc.dram_tensor("t_s", [128, NCB * T], F32, kind="ExternalOutput").ap()
        dbg["t_cmf"] = nc.dram_tensor("t_cmf", [16, E * 128], BF16, kind="ExternalOutput").ap()
        dbg["t_xnb"] = nc.dram_tensor("t_xnb", [128, NCB * T], BF16, kind="ExternalOutput").ap()
        dbg["t_gt"] = nc.dram_tensor("t_gt", [128, NCB * E], BF16, kind="ExternalOutput").ap()
        dbg["t_gbr"] = nc.dram_tensor("t_gbr", [128, E], F32, kind="ExternalOutput").ap()
        dbg["t_g"] = nc.dram_tensor("t_g", [NTILE, 128, 4096], BF16, kind="ExternalOutput").ap()
        dbg["t_kv"] = nc.dram_tensor("t_kv", [128, 5120], F32, kind="ExternalOutput").ap()

    es = ExitStack()
    sb = lambda name, shape, dt: es.enter_context(nc.sbuf_tensor(name, shape, dt))

    A = sb("A_stream", [128, NCB * T], F32)
    XNB = sb("A_xnb", [128, NCB * T], BF16)
    XX = sb("A_xxx", [128, NCB * T], BF16)
    AR = sb("AR", [128, 13376], F32)
    WT = sb("W_T", [128, 8 * 512], BF16)
    WT2 = sb("W_T2", [128, 4 * 1024], BF16)
    IDB = sb("identb_s", [128, 128], BF16)
    ID32 = sb("ident32_s", [128, 128], F32)
    ONECOL = sb("onecol", [128, 1], F32)
    ONEROW = sb("onerow", [1, 128], F32)
    ROWS = sb("rows", [1, 1024], F32)
    MU1C = sb("mu1c", [128, NTB], F32)
    RS1C = sb("rs1c", [128, NTB], F32)
    VEC = sb("veccols", [128, 10 * NCB], F32)
    CAR = sb("carries", [128, 2 * NCB], F32)
    H2P = sb("h2prev", [128, NCB], F32)
    GATET = sb("gateT", [128, NCB * E], BF16)
    GBR = sb("gbrow", [128, E], F32)
    GBS = sb("gbs", [1, E], F32)
    SCR = sb("scr", [128, 512], F32)

    V_U, V_TS, V_1PTS, V_NTS, V_1MTS, V_L1W, V_L1B, V_L2W, V_L2B, V_SP = range(10)

    def vcol(i, cb):
        return VEC[:, i * NCB + cb: i * NCB + cb + 1]

    def vcols(i):
        return VEC[:, i * NCB: (i + 1) * NCB]

    PS = [es.enter_context(nc.psum_tensor(f"ps{i}", [128, 512], F32)) for i in range(8)]
    MM0, MM1, MM2, MM3, T0, T1, ST, SP2 = range(8)

    mksem = lambda nm: Sem(es.enter_context(nc.semaphore(nm)))
    TE = mksem("TE"); SE = mksem("SE"); VE = mksem("VE"); OUT = mksem("OUT")
    S_CONST = mksem("S_CONST"); S_GW = mksem("S_GW"); S_MEM = mksem("S_MEM")
    S_XT = [mksem(f"S_XT{i}") for i in range(3)]
    S_WT = [mksem(f"S_WT{i}") for i in range(8)]
    S_WU = [mksem(f"S_WU{i}") for i in range(4)]
    S_CAST = {f: mksem(f"S_C_{f}") for f in
              ("wk", "wv", "wr", "wck", "wcr", "wcv", "w1", "w3", "w2", "ws")}
    S_W3 = mksem("S_W3"); S_HS = [mksem(f"S_HS{i}") for i in range(2)]
    S_H = mksem("S_H"); S_WS = mksem("S_WS"); S_GS = mksem("S_GS")
    S_GT = mksem("S_GT")
    S_X2 = [mksem(f"S_X2{i}") for i in range(2)]
    S_OT = [mksem(f"S_OT{i}") for i in range(2)]

    P = Plan()
    free = P.free

    A3 = A[:].rearrange("p (cb t) -> p cb t", t=T)
    XNB3 = XNB[:].rearrange("p (cb t) -> p cb t", t=T)
    XX3b = XX[:].rearrange("p (s n) -> p s n", n=2048)

    def a_sl(cb, a, b):
        return A[:, cb * T + a: cb * T + b]

    def xnb_sl(cb, a, b):
        return XNB[:, cb * T + a: cb * T + b]

    def xx_sl(cb, a, b):
        return XX[:, cb * T + a: cb * T + b]

    def dma(eng, sem, out, in_):
        return P.emit(eng, lambda e, o=out, i=in_: e.dma_start(out=o, in_=i), sem, 16)

    P.emit("gpsimd", lambda e: e.memset(ONECOL[:], 1.0))
    P.emit("gpsimd", lambda e: e.memset(ONEROW[:], 1.0))
    P.emit("gpsimd", lambda e: e.memset(CAR[:, 0:NCB], 1.0))
    P.emit("gpsimd", lambda e: e.memset(CAR[:, NCB:2 * NCB], 0.0))
    P.emit("gpsimd", lambda e: e.memset(SCR[:, 0:128], 1.0), S_MEM, 1)
    n_memset = S_MEM.n

    cast_done = {}

    def cast(fam, dst, src_):
        n = dma("gpsimd", S_CAST[fam], dst, src_)
        cast_done[fam] = (S_CAST[fam], n)

    cast("wk", wk_b, Wk_d)
    cast("wv", wv_b, Wv_d)
    cast("wr", wr_b, Wr_d)
    cast("wck", wck_b, Wck_d)
    cast("wcr", wcr_b, Wcr_d)
    cast("wcv", wcv_b, Wcv_d)
    cast("w1", w1_b.rearrange("e i c -> (e i) c"), w1_d.rearrange("e i c -> (e i) c"))
    cast("w3", w3_b.rearrange("e i c -> (e i) c"), w3_d.rearrange("e i c -> (e i) c"))
    cast("w2", w2_b.rearrange("e d i -> (e d) i"), w2_d.rearrange("e d i -> (e d) i"))
    dma("gpsimd", S_CAST["ws"], ws1_b, ws1_d)
    dma("gpsimd", S_CAST["ws"], ws3_b, ws3_d)
    n_ws = dma("gpsimd", S_CAST["ws"], ws2_b, ws2_d)
    cast_done["ws"] = (S_CAST["ws"], n_ws)

    wt_ptr = [0]
    wt2_ptr = [0]

    def t_load(src_rows):
        sl = wt_ptr[0] % 8
        wt_ptr[0] += 1
        dst = WT[:, sl * 512: sl * 512 + src_rows.shape[0]]
        P.use("sync", ("wt", sl))
        n = P.emit("sync", lambda e, o=dst, i=src_rows: e.dma_start(out=o, in_=i, transpose=True), S_WT[sl], 16)
        return dst, (S_WT[sl], n), ("wt", sl)

    def t_load2(src_rows):
        sl = wt2_ptr[0] % 4
        wt2_ptr[0] += 1
        dst = WT2[:, sl * 1024: sl * 1024 + src_rows.shape[0]]
        P.use("sync", ("wu", sl))
        n = P.emit("sync", lambda e, o=dst, i=src_rows: e.dma_start(out=o, in_=i, transpose=True), S_WU[sl], 16)
        return dst, (S_WU[sl], n), ("wu", sl)

    dma("sync", S_CONST, IDB[:], idb_d)
    dma("sync", S_CONST, ID32[:], id32_d)
    dma("sync", S_CONST, GBS[:], gb_d.rearrange("(a b) -> a b", a=1))
    VSTG = WT[:].bitcast(F32)[0:8, 0:768]
    vec_list = [(u_d, V_U), (ts_d.rearrange("x y c -> (x y c)"), V_TS), (ln1w_d, V_L1W),
                (ln1b_d, V_L1B), (ln2w_d, V_L2W), (ln2b_d, V_L2B)]
    for i, (src, _) in enumerate(vec_list):
        dma("sync", S_CONST, VSTG[:, i * 128:(i + 1) * 128], src.rearrange("(a b) -> a b", b=128))
    GWS = WT2[0:16, 0:1024]
    dma("gpsimd", S_GW, GWS, gw_d)
    n_in_const = S_CONST.n
    n_gw = S_GW.n


    P.wait("tensor", S_CONST, n_in_const)
    P.wait("tensor", S_MEM, n_memset)
    for i in range(len(vec_list)):
        P.emit("tensor", lambda e, i=i: e.matmul(
            PS[T0][:, i * 8:(i + 1) * 8], VSTG[:, i * 128:(i + 1) * 128], ID32[0:8, 0:8],
            is_transpose=True, start=(i == 0), stop=(i == len(vec_list) - 1)), TE)
    t_vec = TE.n
    P.wait("scalar", TE, t_vec)
    for i, (_, vidx) in enumerate(vec_list):
        P.emit("scalar", lambda e, i=i, v=vidx: e.copy(vcols(v), PS[T0][:, i * 8:(i + 1) * 8]), SE)
    se_vec = SE.n
    P.release(("bank", T0), SE)
    for _s in range(6):
        P.release(("wt", _s), TE, t_vec)
    P.wait("vector", SE, se_vec)
    P.emit("vector", lambda e: e.tensor_scalar(vcols(V_1PTS), vcols(V_TS), 1.0, None, ALU.add, ALU.bypass))
    P.emit("vector", lambda e: e.tensor_scalar(vcols(V_NTS), vcols(V_TS), -1.0, None, ALU.mult, ALU.bypass))
    P.emit("vector", lambda e: e.tensor_scalar(vcols(V_1MTS), vcols(V_TS), -1.0, 1.0, ALU.mult, ALU.add), VE)
    ve_vcols = VE.n

    P.wait("tensor", S_GW, n_gw)
    for kc in range(NCB):
        P.emit("tensor", lambda e, kc=kc: e.matmul(
            PS[T1][:].bitcast(BF16)[:, kc * 16:(kc + 1) * 16],
            GWS[:, kc * 128:(kc + 1) * 128], IDB[0:16, 0:16],
            is_transpose=True, start=(kc == 0), stop=(kc == 7)), TE)
    t_gate = TE.n
    P.wait("scalar", TE, t_gate)
    P.emit("scalar", lambda e: e.copy(GATET[:], PS[T1][:].bitcast(BF16)[:, 0:NCB * E]), SE)
    se_gate = SE.n
    P.release(("bank", T1), SE)
    P.release(("wu", 0), SE)
    P.emit("tensor", lambda e: e.matmul(PS[SP2][:, 0:16], ONEROW[:], GBS[:], start=True, stop=True), TE)
    t_gbr = TE.n
    P.wait("scalar", TE, t_gbr)
    P.emit("scalar", lambda e: e.copy(GBR[:], PS[SP2][:, 0:16]), SE)
    P.release(("bank", SP2), SE)

    XTM = lambda sl: AR[:, 10240 + sl * 1024: 10240 + (sl + 1) * 1024]
    SQD = AR[:, 9216:10240]
    se_tb = {}
    for tb in range(NTB):
        sl = tb % 3
        CSX = lambda i, sl=sl: AR[:, 13320 + sl * 16 + i: 13321 + sl * 16 + i]
        P.use("sync", ("xtm", sl))
        n_ld = dma("sync", S_XT[sl], XTM(sl), x_d[tb * 128:(tb + 1) * 128, :])
        P.wait("vector", S_XT[sl], n_ld)
        P.use("vector", ("csx", sl))
        P.emit("vector", lambda e, sl=sl, CSX=CSX: e.tensor_reduce(CSX(0), XTM(sl), AX.X, ALU.add), VE)
        P.drain("vector")
        P.wait("scalar", S_XT[sl], n_ld)
        P.use("scalar", ("sqd",))
        P.use("scalar", ("csxs", sl))
        P.emit("scalar", lambda e, sl=sl, CSX=CSX: e.activation(SQD, XTM(sl), AF.Square, accum_out=CSX(1)), SE)
        se_sq = SE.n
        P.wait("vector", SE, se_sq)
        P.emit("vector", lambda e, CSX=CSX: e.tensor_scalar(CSX(2), CSX(0), 1.0 / C, None, ALU.mult, ALU.bypass))
        P.emit("vector", lambda e, CSX=CSX: e.tensor_scalar(CSX(3), CSX(1), 1.0 / C, None, ALU.mult, ALU.bypass))
        P.drain("vector")
        P.emit("vector", lambda e, CSX=CSX: e.tensor_scalar(CSX(4), CSX(2), CSX(2), None, ALU.mult, ALU.bypass))
        P.drain("vector")
        P.emit("vector", lambda e, CSX=CSX: e.tensor_scalar(CSX(5), CSX(3), CSX(4), LN_EPS, ALU.subtract, ALU.add), VE)
        ve_var = VE.n
        P.wait("scalar", VE, ve_var)
        P.emit("scalar", lambda e, CSX=CSX: e.activation(CSX(6), CSX(5), AF.Ln))
        P.drain("scalar")
        P.emit("scalar", lambda e, CSX=CSX: e.activation(CSX(6), CSX(6), AF.Exp, scale=-0.5), SE)
        se_rstd = SE.n
        P.wait("vector", SE, se_rstd)
        P.emit("vector", lambda e, CSX=CSX, sl=sl: e.tensor_scalar(XTM(sl), XTM(sl), CSX(2), CSX(6), ALU.subtract, ALU.mult), VE)
        ve_xn = VE.n
        P.emit("vector", lambda e, CSX=CSX, sl=sl: e.tensor_reduce(CSX(7), XTM(sl), AX.X, ALU.add), VE)
        P.drain("vector")
        P.wait("scalar", VE, ve_xn)
        P.emit("scalar", lambda e, CSX=CSX, sl=sl: e.activation(SQD, XTM(sl), AF.Square, accum_out=CSX(8)), SE)
        se_sq1 = SE.n
        P.release(("sqd",), SE)
        P.wait("vector", SE, se_sq1)
        P.emit("vector", lambda e, CSX=CSX, tb=tb: e.tensor_scalar(MU1C[:, tb:tb + 1], CSX(7), 1.0 / C, None, ALU.mult, ALU.bypass))
        P.emit("vector", lambda e, CSX=CSX: e.tensor_scalar(CSX(9), CSX(8), 1.0 / C, None, ALU.mult, ALU.bypass))
        P.drain("vector")
        P.emit("vector", lambda e, CSX=CSX, tb=tb: e.tensor_scalar(CSX(10), MU1C[:, tb:tb + 1], MU1C[:, tb:tb + 1], None, ALU.mult, ALU.bypass))
        P.drain("vector")
        P.emit("vector", lambda e, CSX=CSX: e.tensor_scalar(CSX(9), CSX(9), CSX(10), LN_EPS, ALU.subtract, ALU.add), VE)
        ve_v1 = VE.n
        P.wait("scalar", VE, ve_v1)
        P.emit("scalar", lambda e, CSX=CSX: e.activation(CSX(9), CSX(9), AF.Ln))
        P.drain("scalar")
        P.emit("scalar", lambda e, CSX=CSX, tb=tb: e.activation(RS1C[:, tb:tb + 1], CSX(9), AF.Exp, scale=-0.5), SE)
        prev_se = SE.n
        P.release(("csx", sl), SE)
        P.release(("csxs", sl), VE, ve_v1)
        toks = [(VE, ve_v1), (SE, prev_se)]
        for g in range(2):
            bank = T0 if g == 0 else T1
            P.use("tensor", ("bank", bank))
            P.wait("tensor", VE, ve_xn)
            for j in range(4):
                cb = g * 4 + j
                P.emit("tensor", lambda e, sl=sl, cb=cb, j=j, b=bank: e.matmul(
                    PS[b][:, j * 128:(j + 1) * 128], XTM(sl)[:, cb * 128:(cb + 1) * 128],
                    ID32[:], is_transpose=True, start=(j == 0), stop=(j == 3)), TE)
            t_tr = TE.n
            toks.append((TE, t_tr))
            P.wait("scalar", TE, t_tr)
            ps4 = PS[bank][:].rearrange("p (j t) -> p j t", t=128)
            P.emit("scalar", lambda e, d=A3[:, g * 4:g * 4 + 4, tb * 128:(tb + 1) * 128], s=ps4: e.copy(d, s), SE)
            P.emit("scalar", lambda e, d=XNB3[:, g * 4:g * 4 + 4, tb * 128:(tb + 1) * 128], s=ps4: e.copy(d, s), SE)
            P.release(("bank", bank), SE)
        P.release_multi(("xtm", sl), toks)
        se_tb[tb] = SE.n
    se_p0 = SE.n

    if DEBUG:
        P.wait("sync", SE, se_p0)
        n_dbg0 = dma("sync", OUT, dbg["t_p0"], A[:])

    HB = lambda cb, a, b: AR[:, 3104 + cb * 513 + a: 3104 + cb * 513 + b]
    rows_mu = ROWS[0:1, 0:512]
    rows_rs = ROWS[0:1, 512:1024]
    P.wait("vector", VE, ve_vcols)
    for cb in range(NCB):
        P.emit("vector", lambda e, cb=cb: e.tensor_copy(HB(cb, 0, 1), vcol(V_TS, cb)), VE)
    ve_xxx_tile = {}
    for tau in range(NTILE):
        t0 = tau * TN
        P.use("tensor", ("bank", ST))
        P.use("tensor", ("bank", SP2))
        P.wait("tensor", SE, se_tb[tau * 4 + 3])
        for j in range(4):
            P.emit("tensor", lambda e, j=j, tau=tau: e.matmul(
                PS[ST][0:1, j * 128:(j + 1) * 128], MU1C[:, tau * 4 + j:tau * 4 + j + 1],
                ID32[:], is_transpose=True, start=(j == 0), stop=(j == 3)), TE)
        for j in range(4):
            P.emit("tensor", lambda e, j=j, tau=tau: e.matmul(
                PS[SP2][0:1, j * 128:(j + 1) * 128], RS1C[:, tau * 4 + j:tau * 4 + j + 1],
                ID32[:], is_transpose=True, start=(j == 0), stop=(j == 3)), TE)
        t_rows = TE.n
        P.use("scalar", ("rows",))
        P.wait("scalar", TE, t_rows)
        P.emit("scalar", lambda e: e.copy(rows_mu, PS[ST][0:1, 0:512]))
        P.emit("scalar", lambda e: e.copy(rows_rs, PS[SP2][0:1, 0:512]), SE)
        se_rows = SE.n
        P.release(("bank", ST), SE)
        P.release(("bank", SP2), SE)
        P.use("tensor", ("bank", MM2))
        P.use("tensor", ("bank", MM3))
        P.wait("tensor", SE, se_rows)
        P.emit("tensor", lambda e: e.matmul(PS[MM2][:], ONEROW[:], rows_mu, start=True, stop=True), TE)
        P.emit("tensor", lambda e: e.matmul(PS[MM3][:], ONEROW[:], rows_rs, start=True, stop=True), TE)
        t_bc = TE.n
        P.release(("rows",), TE)
        P.wait("vector", TE, t_bc)
        for cb in range(NCB):
            tmp = AR[:, 7208 + (cb % 2) * 512: 7208 + (cb % 2) * 512 + 512]
            P.emit("vector", lambda e, cb=cb, t0=t0: e.tensor_tensor(HB(cb, 1, 513), a_sl(cb, t0, t0 + TN), PS[MM2][:], ALU.subtract))
            P.emit("vector", lambda e, cb=cb: e.tensor_tensor(HB(cb, 1, 513), HB(cb, 1, 513), PS[MM3][:], ALU.mult))
            P.emit("vector", lambda e, cb=cb: e.tensor_scalar(HB(cb, 1, 513), HB(cb, 1, 513), vcol(V_L1W, cb), vcol(V_L1B, cb), ALU.mult, ALU.add))
            P.emit("vector", lambda e, cb=cb, tmp=tmp: e.tensor_scalar(tmp, HB(cb, 1, 513), vcol(V_1PTS, cb), None, ALU.mult, ALU.bypass))
            P.emit("vector", lambda e, cb=cb, tmp=tmp, t0=t0: e.scalar_tensor_tensor(
                xx_sl(cb, t0, t0 + TN), HB(cb, 0, 512), vcol(V_NTS, cb), tmp, ALU.mult, ALU.add), VE)
        ve_xxx_tile[tau] = VE.n
        P.release(("bank", MM2), VE)
        P.release(("bank", MM3), VE)
        if tau < NTILE - 1:
            for cb in range(NCB):
                P.emit("vector", lambda e, cb=cb: e.tensor_copy(HB(cb, 0, 1), HB(cb, 512, 513)), VE)
    ve_p1 = VE.n

    if DEBUG:
        P.wait("sync", VE, ve_p1)
        dma("sync", OUT, dbg["t_xxx"], XX[:])

    kv_k = lambda j, a, b: AR[:, j * 2560 + a: j * 2560 + b]
    kv_v = lambda j, a, b: AR[:, j * 2560 + 1024 + a: j * 2560 + 1024 + b]
    kv_r = lambda j: AR[:, j * 2560 + 2048: j * 2560 + 2560].bitcast(BF16)
    SBASE = lambda p: 10240 + p * 770
    S_EK = lambda p: AR[:, SBASE(p): SBASE(p) + 128]
    S_EU = lambda p: AR[:, SBASE(p) + 128: SBASE(p) + 256]
    S_T1 = lambda p: AR[:, SBASE(p) + 256: SBASE(p) + 384]
    S_T2 = lambda p: AR[:, SBASE(p) + 384: SBASE(p) + 512]
    S_DB = lambda p: AR[:, SBASE(p) + 512: SBASE(p) + 641]
    S_AB = lambda p: AR[:, SBASE(p) + 641: SBASE(p) + 770]
    S_ONE = SCR[:, 0:128]

    prod_done = {}

    def emit_prod_half(quad, h):
        obs = [quad * 4 + i for i in range(4)]
        for kind, fam_b, fam in (("k", wk_b, "wk"), ("v", wv_b, "wv"), ("r", wr_b, "wr")):
            if h == 0 and quad == 0 and kind == "k":
                pass
            toks = []
            for kc in range(NCB):
                P.waitp("sync", cast_done[fam])
                tile, retok, wtok = t_load(fam_b[quad * 512:(quad + 1) * 512, kc * 128:(kc + 1) * 128])
                first_mm_te = None
                for ob_l in range(4):
                    for tt_l in range(2):
                        bank = ob_l * 2 + tt_l
                        tt = 2 * h + tt_l
                        if kc == 0:
                            P.use("tensor", ("bank", bank))
                            if kind == "k" and tt_l == 0:
                                P.use("tensor", ("kvr", ob_l))
                        if ob_l == 0 and tt_l == 0:
                            P.waitp("tensor", retok)
                            P.wait("tensor", VE, ve_p1)
                        P.emit("tensor", lambda e, b=bank, kc=kc, t=tile, tt=tt, ob_l=ob_l: e.matmul(
                            PS[b][:], t[:, ob_l * 128:(ob_l + 1) * 128], xx_sl(kc, tt * TN, (tt + 1) * TN),
                            start=(kc == 0), stop=(kc == 7)), TE)
                P.release(wtok, TE)
            for ob_l in range(4):
                for tt_l in range(2):
                    bank = ob_l * 2 + tt_l
                    j = ob_l
                    P.wait("scalar", TE, TE.n)
                    if kind == "k":
                        dst = kv_k(j, tt_l * TN, (tt_l + 1) * TN)
                    elif kind == "v":
                        dst = kv_v(j, tt_l * TN, (tt_l + 1) * TN)
                    else:
                        dst = kv_r(j)[:, tt_l * TN:(tt_l + 1) * TN]
                    P.emit("scalar", lambda e, d=dst, b=bank: e.copy(d, PS[b][:]), SE)
                    P.release(("bank", bank), SE)
        for cb in obs:
            prod_done[(cb, h)] = SE.n

    def emit_scan_quad(cbs, h):
        TC = 128
        for p, cb in enumerate(cbs):
            P.wait("vector", SE, prod_done[(cb, h)])
        for q in range(8):
            a = q * TC
            ga = h * 1024 + a
            se_ek = [0] * 4; ve_m = [0] * 4; se_rm = [0] * 4; ve_den = [0] * 4; se_sig = [0] * 4
            for p, cb in enumerate(cbs):
                j = cb % 4
                kq = kv_k(j, a, a + TC)
                P.wait("scalar", S_MEM, n_memset)
                P.emit("scalar", lambda e, k=kq, p=p: e.activation(S_EK(p), k, AF.Exp))
                P.emit("scalar", lambda e, k=kq, p=p, cb=cb: e.activation(S_EU(p), k, AF.Exp, bias=vcol(V_U, cb)), SE)
                se_ek[p] = SE.n
            for p, cb in enumerate(cbs):
                dcar = CAR[:, cb:cb + 1]
                P.wait("vector", SE, se_ek[p])
                P.emit("vector", lambda e, d=dcar, p=p: e.tensor_copy(S_DB(p)[:, 0:1], d))
                P.emit("vector", lambda e, d=dcar, p=p: e.tensor_tensor_scan(S_DB(p)[:, 1:129], S_ONE, S_EK(p), d, ALU.mult, ALU.add))
                P.emit("vector", lambda e, p=p: e.tensor_tensor(S_T1(p), S_DB(p)[:, 0:128], S_EK(p), ALU.max), VE)
                ve_m[p] = VE.n
            for p in range(4):
                P.wait("scalar", VE, ve_m[p])
                P.emit("scalar", lambda e, p=p: e.activation(S_T2(p), S_T1(p), AF.Ln))
                P.emit("scalar", lambda e, p=p: e.activation(S_T1(p), S_T2(p), AF.Exp, scale=-1.0), SE)
                se_rm[p] = SE.n
            for p, cb in enumerate(cbs):
                j = cb % 4
                vq = kv_v(j, a, a + TC)
                acar = CAR[:, NCB + cb:NCB + cb + 1]
                dcar = CAR[:, cb:cb + 1]
                P.wait("vector", SE, se_rm[p])
                P.emit("vector", lambda e, p=p: e.tensor_tensor(S_T2(p), S_DB(p)[:, 0:128], S_T1(p), ALU.mult))
                P.emit("vector", lambda e, p=p: e.tensor_tensor(S_EK(p), S_EK(p), S_T1(p), ALU.mult))
                P.emit("vector", lambda e, v=vq, p=p: e.tensor_tensor(S_EK(p), S_EK(p), v, ALU.mult))
                P.emit("vector", lambda e, ac=acar, p=p: e.tensor_copy(S_AB(p)[:, 0:1], ac))
                P.emit("vector", lambda e, ac=acar, p=p: e.tensor_tensor_scan(S_AB(p)[:, 1:129], S_T2(p), S_EK(p), ac, ALU.mult, ALU.add))
                P.emit("vector", lambda e, d=dcar, p=p: e.tensor_copy(d, S_DB(p)[:, 128:129]))
                P.emit("vector", lambda e, ac=acar, p=p: e.tensor_copy(ac, S_AB(p)[:, 128:129]))
                P.emit("vector", lambda e, p=p: e.tensor_tensor(S_T1(p), S_DB(p)[:, 0:128], S_AB(p)[:, 0:128], ALU.mult))
                P.emit("vector", lambda e, v=vq, p=p: e.tensor_tensor(S_T2(p), S_EU(p), v, ALU.mult))
                P.emit("vector", lambda e, p=p: e.tensor_tensor(S_T1(p), S_T1(p), S_T2(p), ALU.add))
                P.emit("vector", lambda e, p=p: e.tensor_tensor(S_T2(p), S_DB(p)[:, 0:128], S_EU(p), ALU.add), VE)
                ve_den[p] = VE.n
            for p, cb in enumerate(cbs):
                j = cb % 4
                rq = kv_r(j)[:, a:a + TC]
                P.wait("scalar", VE, ve_den[p])
                P.emit("scalar", lambda e, p=p: e.activation(S_T2(p), S_T2(p), AF.Ln))
                P.emit("scalar", lambda e, p=p: e.activation(S_T2(p), S_T2(p), AF.Exp, scale=-1.0))
                P.emit("scalar", lambda e, r=rq, p=p: e.activation(S_EU(p), r, AF.Exp, scale=-1.0))
                P.emit("scalar", lambda e, p=p: e.add(S_EU(p), S_EU(p), 1.0))
                P.emit("scalar", lambda e, p=p: e.activation(S_EU(p), S_EU(p), AF.Ln))
                P.emit("scalar", lambda e, p=p: e.activation(S_EU(p), S_EU(p), AF.Exp, scale=-1.0), SE)
                se_sig[p] = SE.n
            for p, cb in enumerate(cbs):
                P.wait("vector", SE, se_sig[p])
                P.emit("vector", lambda e, p=p: e.tensor_tensor(S_T1(p), S_T1(p), S_T2(p), ALU.mult))
                P.emit("vector", lambda e, p=p: e.tensor_tensor(S_T1(p), S_T1(p), S_EU(p), ALU.mult))
                P.emit("vector", lambda e, cb=cb, ga=ga, p=p: e.tensor_tensor(a_sl(cb, ga, ga + TC), S_T1(p), a_sl(cb, ga, ga + TC), ALU.add), VE)
        for cb in cbs:
            P.release(("kvr", cb % 4), VE)

    emit_prod_half(0, 0)
    emit_scan_quad([0, 1, 2, 3], 0)
    emit_prod_half(0, 1)
    emit_scan_quad([0, 1, 2, 3], 1)
    emit_prod_half(1, 0)
    emit_scan_quad([4, 5, 6, 7], 0)
    emit_prod_half(1, 1)
    emit_scan_quad([4, 5, 6, 7], 1)
    ve_scan = VE.n

    if DEBUG:
        P.wait("sync", VE, ve_scan)
        dma("sync", OUT, dbg["t_x1"], A[:])

    KCSQ = AR[:, 0:8192].bitcast(BF16)
    H2 = lambda cb, a, b: AR[:, 8192 + cb * 512 + a: 8192 + cb * 512 + b]
    RCT = AR[:, 12288:12800]
    VVT = AR[:, 12800:13312]
    TROW = SCR[0:1, 0:512]

    P.use("tensor", ("bank", ST))
    P.wait("tensor", VE, ve_scan)
    if DEBUG:
        P.wait("tensor", OUT, OUT.n)
    for cb in range(NCB):
        P.emit("tensor", lambda e, cb=cb: e.matmul(PS[ST][0:1, 0:1], ONECOL[:], A3[:, cb, 2047:2048],
                                                   start=(cb == 0), stop=(cb == 7)), TE)
    P.wait("scalar", VE, ve_scan)
    P.emit("scalar", lambda e: e.activation(H2P[:], A3[:, 0:8, 2047:2048].rearrange("p a b -> p (a b)"), AF.Square), SE)
    se_wsq = SE.n
    P.use("tensor", ("bank", SP2))
    P.wait("tensor", SE, se_wsq)
    for cb in range(NCB):
        P.emit("tensor", lambda e, cb=cb: e.matmul(PS[SP2][0:1, 0:1], ONECOL[:], H2P[:, cb:cb + 1],
                                                   start=(cb == 0), stop=(cb == 7)), TE)
    t_wssq = TE.n
    P.use("scalar", ("rows",))
    P.wait("scalar", TE, t_wssq)
    P.emit("scalar", lambda e: e.copy(ROWS[0:1, 0:1], PS[ST][0:1, 0:1]))
    P.emit("scalar", lambda e: e.copy(ROWS[0:1, 512:513], PS[SP2][0:1, 0:1]), SE)
    se_wrow = SE.n
    P.release(("bank", ST), SE)
    P.release(("bank", SP2), SE)
    P.use("vector", ("scr_p3",))
    P.wait("vector", SE, se_wrow)
    P.emit("vector", lambda e: e.tensor_scalar(rows_mu[:, 0:1], rows_mu[:, 0:1], 1.0 / C, None, ALU.mult, ALU.bypass))
    P.emit("vector", lambda e: e.tensor_scalar(rows_rs[:, 0:1], rows_rs[:, 0:1], 1.0 / C, None, ALU.mult, ALU.bypass))
    P.drain("vector")
    P.emit("vector", lambda e: e.tensor_scalar(TROW[:, 0:1], rows_mu[:, 0:1], rows_mu[:, 0:1], None, ALU.mult, ALU.bypass))
    P.drain("vector")
    P.emit("vector", lambda e: e.tensor_scalar(rows_rs[:, 0:1], rows_rs[:, 0:1], TROW[:, 0:1], LN_EPS, ALU.subtract, ALU.add), VE)
    ve_wv = VE.n
    P.wait("scalar", VE, ve_wv)
    P.emit("scalar", lambda e: e.activation(rows_rs[:, 0:1], rows_rs[:, 0:1], AF.Ln))
    P.drain("scalar")
    P.emit("scalar", lambda e: e.activation(rows_rs[:, 0:1], rows_rs[:, 0:1], AF.Exp, scale=-0.5), SE)
    se_wst = SE.n
    P.use("tensor", ("bank", SP2))
    P.wait("tensor", SE, se_wst)
    P.emit("tensor", lambda e: e.matmul(PS[SP2][:, 0:1], ONEROW[:], rows_mu[:, 0:1], start=True, stop=True))
    P.emit("tensor", lambda e: e.matmul(PS[SP2][:, 1:2], ONEROW[:], rows_rs[:, 0:1], start=True, stop=True), TE)
    t_wbc = TE.n
    P.release(("rows",), TE)
    P.wait("vector", TE, t_wbc)
    for cb in range(NCB):
        P.emit("vector", lambda e, cb=cb: e.tensor_tensor(H2P[:, cb:cb + 1], A3[:, cb, 2047:2048], PS[SP2][:, 0:1], ALU.subtract))
        P.drain("vector")
        P.emit("vector", lambda e, cb=cb: e.tensor_tensor(H2P[:, cb:cb + 1], H2P[:, cb:cb + 1], PS[SP2][:, 1:2], ALU.mult))
        P.drain("vector")
        P.emit("vector", lambda e, cb=cb: e.tensor_scalar(H2P[:, cb:cb + 1], H2P[:, cb:cb + 1], vcol(V_L2W, cb), vcol(V_L2B, cb), ALU.mult, ALU.add), VE)
        P.drain("vector")
    ve_h2p = VE.n
    P.release(("bank", SP2), VE)

    prev_tau_ve = ve_h2p
    ve_xxx2_tau = {}
    for tau in range(NTILE):
        t0 = tau * TN
        P.use("tensor", ("bank", ST))
        for cb in range(NCB):
            P.emit("tensor", lambda e, cb=cb, t0=t0: e.matmul(PS[ST][0:1, 0:512], ONECOL[:], a_sl(cb, t0, t0 + TN),
                                                              start=(cb == 0), stop=(cb == 7)), TE)
        t_sum = TE.n
        P.use("scalar", ("rows",))
        P.wait("scalar", TE, t_sum)
        P.emit("scalar", lambda e: e.copy(rows_mu, PS[ST][0:1, 0:512]), SE)
        P.release(("bank", ST), SE)
        P.use("tensor", ("bank", SP2))
        for cb in range(NCB):
            sq = cb % 2
            P.use("scalar", ("h2sq", sq))
            P.wait("scalar", VE, prev_tau_ve)
            P.emit("scalar", lambda e, cb=cb, t0=t0, s=sq: e.activation(H2(s, 0, TN), a_sl(cb, t0, t0 + TN), AF.Square), SE)
            se_q = SE.n
            P.wait("tensor", SE, se_q)
            P.emit("tensor", lambda e, cb=cb, s=sq: e.matmul(PS[SP2][0:1, 0:512], ONECOL[:], H2(s, 0, TN),
                                                             start=(cb == 0), stop=(cb == 7)), TE)
            P.release(("h2sq", sq), TE)
        t_ssq = TE.n
        P.wait("scalar", TE, t_ssq)
        P.emit("scalar", lambda e: e.copy(rows_rs, PS[SP2][0:1, 0:512]), SE)
        se_ssq = SE.n
        P.release(("bank", SP2), SE)
        P.use("vector", ("scr_p3",))
        P.wait("vector", SE, se_ssq)
        P.emit("vector", lambda e: e.tensor_scalar(rows_mu, rows_mu, 1.0 / C, None, ALU.mult, ALU.bypass))
        P.emit("vector", lambda e: e.tensor_scalar(rows_rs, rows_rs, 1.0 / C, None, ALU.mult, ALU.bypass))
        P.emit("vector", lambda e: e.tensor_tensor(TROW, rows_mu, rows_mu, ALU.mult))
        P.emit("vector", lambda e: e.tensor_tensor(rows_rs, rows_rs, TROW, ALU.subtract))
        P.emit("vector", lambda e: e.tensor_scalar(rows_rs, rows_rs, LN_EPS, None, ALU.add, ALU.bypass), VE)
        ve_v2 = VE.n
        P.release(("scr_p3",), VE)
        P.wait("scalar", VE, ve_v2)
        P.emit("scalar", lambda e: e.activation(rows_rs, rows_rs, AF.Ln))
        P.emit("scalar", lambda e: e.activation(rows_rs, rows_rs, AF.Exp, scale=-0.5), SE)
        se_st2 = SE.n
        P.use("tensor", ("bank", MM2))
        P.use("tensor", ("bank", MM3))
        P.wait("tensor", SE, se_st2)
        P.emit("tensor", lambda e: e.matmul(PS[MM2][:], ONEROW[:], rows_mu, start=True, stop=True), TE)
        P.emit("tensor", lambda e: e.matmul(PS[MM3][:], ONEROW[:], rows_rs, start=True, stop=True), TE)
        t_bc2 = TE.n
        P.release(("rows",), TE)
        P.wait("vector", TE, t_bc2)
        for cb in range(NCB):
            P.emit("vector", lambda e, cb=cb, t0=t0: e.tensor_tensor(H2(cb, 0, TN), a_sl(cb, t0, t0 + TN), PS[MM2][:], ALU.subtract))
            P.emit("vector", lambda e, cb=cb: e.tensor_tensor(H2(cb, 0, TN), H2(cb, 0, TN), PS[MM3][:], ALU.mult))
            P.emit("vector", lambda e, cb=cb: e.tensor_scalar(H2(cb, 0, TN), H2(cb, 0, TN), vcol(V_L2W, cb), vcol(V_L2B, cb), ALU.mult, ALU.add))
            P.emit("vector", lambda e, cb=cb: e.tensor_scalar(VVT, H2(cb, 0, TN), vcol(V_1MTS, cb), None, ALU.mult, ALU.bypass))
            P.emit("vector", lambda e, cb=cb, t0=t0: e.scalar_tensor_tensor(
                xx_sl(cb, t0, t0 + 1), H2P[:, cb:cb + 1], vcol(V_TS, cb), VVT[:, 0:1], ALU.mult, ALU.add))
            P.emit("vector", lambda e, cb=cb, t0=t0: e.scalar_tensor_tensor(
                xx_sl(cb, t0 + 1, t0 + TN), H2(cb, 0, TN - 1), vcol(V_TS, cb), VVT[:, 1:TN], ALU.mult, ALU.add), VE)
        ve_xxx2_tau[tau] = VE.n
        P.release(("bank", MM2), VE)
        P.release(("bank", MM3), VE)
        for cb in range(NCB):
            P.emit("vector", lambda e, cb=cb: e.tensor_copy(H2P[:, cb:cb + 1], H2(cb, TN - 1, TN)), VE)
        prev_tau_ve = VE.n

    RSC = AR[:, 12288:12800]
    SCO = RSC[:, 0:256]
    RMSK = lambda j: RSC[:, 256 + j * 16: 272 + j * 16]
    RMAX = RSC[:, 320:336]
    RKEEP = RSC[:, 336:352]
    RK4 = RSC[:, 352:368]
    RSEL = RSC[:, 368:432]
    RD4 = RSC[:, 432:436]
    RD16 = RSC[:, 436:452]

    for tb in range(NTB):
        bank = MM0 if tb % 2 == 0 else MM1
        P.use("tensor", ("bank", bank))
        if tb == 0:
            P.wait("tensor", SE, se_gate)
        for kc in range(NCB):
            P.emit("tensor", lambda e, b=bank, kc=kc, tb=tb: e.matmul(
                PS[b][:, 0:16], XNB3[:, kc, tb * 128:(tb + 1) * 128],
                GATET[:, kc * 16:(kc + 1) * 16], start=(kc == 0), stop=(kc == 7)), TE)
        t_sc = TE.n
        P.wait("scalar", TE, t_sc)
        P.emit("scalar", lambda e, b=bank, tb=tb: e.activation(
            SCO[:, tb * 16:(tb + 1) * 16], PS[b][:, 0:16], AF.Exp, scale=-1.0), SE)
        P.release(("bank", bank), SE)
    P.drain("scalar")
    P.emit("scalar", lambda e: e.add(SCO[:], SCO[:], 1.0))
    P.drain("scalar")
    P.emit("scalar", lambda e: e.activation(SCO[:], SCO[:], AF.Ln))
    P.drain("scalar")
    P.emit("scalar", lambda e: e.activation(SCO[:], SCO[:], AF.Exp, scale=-1.0), SE)
    se_sco = SE.n

    P.wait("vector", SE, se_sco)
    for B4 in range(4):
        c0 = B4 * 64
        for j in range(4):
            P.emit("vector", lambda e, c0=c0, j=j: e.tensor_tensor(
                RMSK(j), SCO[:, c0 + j * 16: c0 + (j + 1) * 16], GBR[:], ALU.add))
        P.drain("vector")
        P.emit("vector", lambda e: e.tensor_reduce(
            RMAX, RSC[:, 256:320].rearrange("p (a b) -> p a b", b=4), AX.X, ALU.max))
        P.drain("vector")
        for j in range(4):
            for G in range(4):
                P.emit("vector", lambda e, j=j, G=G: e.tensor_scalar(
                    RD4, RMAX[:, j * 4: j * 4 + 4], RMAX[:, j * 4 + G: j * 4 + G + 1], 0.0,
                    ALU.is_gt, ALU.add, accum_out=RKEEP[:, j * 4 + G: j * 4 + G + 1]))
        P.drain("vector")
        P.emit("vector", lambda e: e.tensor_scalar(RKEEP, RKEEP, 2.0, None, ALU.is_lt, ALU.bypass))
        P.drain("vector")
        P.emit("vector", lambda e: e.tensor_scalar(RK4, RKEEP, 4.0, -4.0, ALU.mult, ALU.add))
        P.drain("vector")
        for j in range(4):
            for G in range(4):
                P.emit("vector", lambda e, j=j, G=G: e.tensor_scalar(
                    RMSK(j)[:, G * 4:(G + 1) * 4], RMSK(j)[:, G * 4:(G + 1) * 4],
                    RK4[:, j * 4 + G: j * 4 + G + 1], None, ALU.add, ALU.bypass))
        P.drain("vector")
        for j in range(4):
            for ee in range(E):
                P.emit("vector", lambda e, j=j, ee=ee: e.tensor_scalar(
                    RD16, RMSK(j), RMSK(j)[:, ee:ee + 1], 0.0,
                    ALU.is_gt, ALU.add, accum_out=RSEL[:, j * 16 + ee: j * 16 + ee + 1]))
        P.drain("vector")
        P.emit("vector", lambda e: e.tensor_scalar(RSEL, RSEL, 4.0, None, ALU.is_lt, ALU.bypass))
        P.drain("vector")
        P.emit("vector", lambda e, c0=c0: e.tensor_tensor(
            SCO[:, c0:c0 + 64], RSEL, SCO[:, c0:c0 + 64], ALU.mult), VE)
        P.drain("vector")
    ve_comb = VE.n

    RCF = lambda ob: AR[:, 8192 + ob * 512: 8192 + (ob + 1) * 512]
    prev_vv_ve = prev_tau_ve
    for tau in range(NTILE):
        t0 = tau * TN
        ve_xxx2 = ve_xxx2_tau[tau]
        for hbg in range(4):
            for kc in range(NCB):
                P.waitp("sync", cast_done["wck"])
                tile, retok, wtok = t_load2(wck_b[hbg * 1024:(hbg + 1) * 1024, kc * 128:(kc + 1) * 128])
                for hb_l in range(8):
                    if kc == 0:
                        P.use("tensor", ("bank", hb_l))
                    if hb_l == 0:
                        P.waitp("tensor", retok)
                        P.wait("tensor", VE, ve_xxx2)
                    P.emit("tensor", lambda e, b=hb_l, t=tile, hb_l=hb_l, t0=t0, kc=kc: e.matmul(
                        PS[b][:], t[:, hb_l * 128:(hb_l + 1) * 128], xx_sl(kc, t0, t0 + TN),
                        start=(kc == 0), stop=(kc == 7)), TE)
                P.release(wtok, TE)
            t_kc = TE.n
            for hb_l in range(8):
                hb = hbg * 8 + hb_l
                P.use("scalar", ("scanbuf",))
                P.wait("scalar", TE, t_kc)
                P.emit("scalar", lambda e, b=hb_l: e.activation(SCR[:], PS[b][:], AF.Relu), SE)
                P.release(("bank", hb_l), SE)
                P.emit("scalar", lambda e, hb=hb: e.activation(KCSQ[:, hb * 512:(hb + 1) * 512], SCR[:], AF.Square), SE)
                P.release(("scanbuf",), SE)
        se_kcsq = SE.n
        for kc in range(NCB):
            P.waitp("sync", cast_done["wcr"])
            tile, retok, wtok = t_load2(wcr_b[0:1024, kc * 128:(kc + 1) * 128])
            for ob in range(8):
                if kc == 0:
                    P.use("tensor", ("bank", ob))
                if ob == 0:
                    P.waitp("tensor", retok)
                P.emit("tensor", lambda e, b=ob, t=tile, ob=ob, t0=t0, kc=kc: e.matmul(
                    PS[b][:], t[:, ob * 128:(ob + 1) * 128], xx_sl(kc, t0, t0 + TN),
                    start=(kc == 0), stop=(kc == 7)), TE)
            P.release(wtok, TE)
        t_rc = TE.n
        for ob in range(8):
            P.wait("scalar", TE, t_rc)
            if ob == 0:
                P.wait("scalar", VE, prev_vv_ve)
            P.emit("scalar", lambda e, ob=ob, b=ob: e.activation(RCF(ob), PS[b][:], AF.Exp, scale=-1.0), SE)
            P.release(("bank", ob), SE)
            P.emit("scalar", lambda e, ob=ob: e.add(RCF(ob), RCF(ob), 1.0))
            P.emit("scalar", lambda e, ob=ob: e.activation(RCF(ob), RCF(ob), AF.Ln))
            P.emit("scalar", lambda e, ob=ob: e.activation(RCF(ob), RCF(ob), AF.Exp, scale=-1.0), SE)
        se_rc = SE.n
        for hb in range(32):
            P.waitp("sync", cast_done["wcv"])
            tile, retok, wtok = t_load2(wcv_b[0:1024, hb * 128:(hb + 1) * 128])
            for cb in range(8):
                if hb == 0:
                    P.use("tensor", ("bank", cb))
                if cb == 0:
                    P.waitp("tensor", retok)
                    if hb == 0:
                        P.wait("tensor", SE, se_kcsq)
                P.emit("tensor", lambda e, b=cb, t=tile, cb=cb, hb=hb: e.matmul(
                    PS[b][:], t[:, cb * 128:(cb + 1) * 128], KCSQ[:, hb * 512:(hb + 1) * 512],
                    start=(hb == 0), stop=(hb == 31)), TE)
            P.release(wtok, TE)
        t_vv = TE.n
        for cb in range(8):
            P.wait("vector", TE, t_vv)
            P.wait("vector", SE, se_rc)
            P.emit("vector", lambda e, cb=cb: e.tensor_tensor(VVT, RCF(cb), PS[cb][:], ALU.mult), VE)
            P.emit("vector", lambda e, cb=cb, t0=t0: e.tensor_tensor(a_sl(cb, t0, t0 + TN), VVT, a_sl(cb, t0, t0 + TN), ALU.add), VE)
            P.release(("bank", cb), VE)
        prev_vv_ve = VE.n
    ve_p3 = VE.n
    te_p3 = TE.n
    se_p3 = SE.n

    if DEBUG:
        P.wait("sync", VE, ve_p3)
        dma("sync", OUT, dbg["t_s"], A[:])

    GPRE = AR[:, 0:2048].bitcast(BF16)
    WS1T = AR[:, 2048:4096].bitcast(BF16)
    WS3T = AR[:, 4096:6144].bitcast(BF16)
    WS2T = AR[:, 6144:8192].bitcast(BF16)
    ZMID = AR[:, 8192:9216].bitcast(BF16)
    SILU = AR[:, 9216:9728]
    HST = lambda r: AR[:, 9728 + r * 1024: 10752 + r * 1024].bitcast(BF16)
    CMF = AR[0:16, 11776:12800].bitcast(BF16)
    cmf_banks = [T0, T1, MM2, MM3]
    t_cmt = 0
    for tb in range(NTB):
        grp = tb % 4
        bank2 = cmf_banks[tb // 4]
        if grp == 0:
            P.use("tensor", ("bank", bank2))
        if tb == 0:
            P.wait("tensor", VE, ve_comb)
        P.emit("tensor", lambda e, tb=tb, b=bank2, grp=grp: e.matmul(
            PS[b][0:16, grp * 128:(grp + 1) * 128], SCO[:, tb * 16:(tb + 1) * 16], ID32[:],
            is_transpose=True, start=(grp == 0), stop=(grp == 3)), TE)
        if grp == 3:
            t_cmt = TE.n
    P.wait("scalar", TE, t_cmt)
    P.wait("scalar", VE, ve_p3)
    for g in range(4):
        P.emit("scalar", lambda e, g=g, b=cmf_banks[g]: e.copy(
            CMF[:, g * 512:(g + 1) * 512], PS[b][0:16, 0:512]), SE)
        P.release(("bank", cmf_banks[g]), SE)
    se_cmf = SE.n

    if DEBUG:
        P.wait("sync", SE, se_cmf)
        dma("sync", OUT, dbg["t_cmf"], CMF[:])
        dma("sync", OUT, dbg["t_xnb"], XNB[:])
        dma("sync", OUT, dbg["t_gt"], GATET[:])
        dma("sync", OUT, dbg["t_gbr"], GBR[:])

    W3SL = AR[:, 0:2048].bitcast(BF16)
    BAS = AR[0:16, 2048:3072].bitcast(BF16)
    P.wait("sync", TE, te_p3)
    P.wait("sync", VE, ve_p3)
    n_bas = dma("sync", S_CONST, BAS, bas_d)
    hst_r = [0]
    ve_he = 0
    w1_toks = None
    w3_toks = None

    def load_w1(e_i):
        toks = []
        for kc in range(NCB):
            P.waitp("sync", cast_done["w1"])
            tile, retok, wtok = t_load(w1_b[e_i][0:512, kc * 128:(kc + 1) * 128])
            toks.append((tile, retok, wtok))
        return toks

    def load_w3(e_i):
        toks = []
        for kc in range(NCB):
            P.waitp("sync", cast_done["w3"])
            P.use("sync", ("w3sl", kc))
            if e_i == 0 and kc == 0:
                P.wait("sync", TE, te_p3)
                P.wait("sync", VE, ve_p3)
            dstv = W3SL[:, kc * 512:(kc + 1) * 512]
            n3 = P.emit("sync", lambda e, o=dstv, i=w3_b[e_i][0:512, kc * 128:(kc + 1) * 128]: e.dma_start(
                out=o, in_=i, transpose=True), S_W3, 16)
            toks.append((dstv, (S_W3, n3), kc))
        return toks

    COMBE = AR[:, 3072:4096].bitcast(BF16)
    w1_toks = load_w1(0)
    w3_toks = load_w3(0)
    for e_i in range(E):
        cb_se = 0
        for tau in range(NTILE):
            bank = MM0 if tau % 2 == 0 else MM1
            P.use("tensor", ("bank", bank))
            if e_i == 0 and tau == 0:
                P.wait("tensor", SE, se_cmf)
                P.wait("tensor", S_CONST, n_bas)
                if DEBUG:
                    P.wait("tensor", OUT, OUT.n)
            P.emit("tensor", lambda e, e_i=e_i, tau=tau, b=bank: e.matmul(
                PS[b][:], BAS[:, e_i * 128:(e_i + 1) * 128], CMF[:, tau * 512:(tau + 1) * 512],
                start=True, stop=True), TE)
            t_cb = TE.n
            P.use("scalar", ("combe", tau))
            P.wait("scalar", TE, t_cb)
            if e_i == 0 and tau == 0:
                P.wait("scalar", VE, ve_p3)
                P.wait("scalar", TE, te_p3)
            P.emit("scalar", lambda e, tau=tau, b=bank: e.copy(COMBE[:, tau * 512:(tau + 1) * 512], PS[b][:]), SE)
            cb_se = SE.n
            P.release(("bank", bank), SE)
        for tau in range(NTILE):
            t0 = tau * TN
            t_h1_ib = [0] * 4
            t_h3_ib = [0] * 4
            for ib in range(4):
                P.use("tensor", ("bank", ib))
                for kc in range(NCB):
                    if ib == 0 and kc == 0:
                        P.waitp("tensor", w1_toks[0][1])
                        for kc2 in range(1, NCB):
                            P.waitp("tensor", w1_toks[kc2][1])
                    P.emit("tensor", lambda e, b=ib, t=w1_toks[kc][0], ib=ib, t0=t0, kc=kc: e.matmul(
                        PS[b][:], t[:, ib * 128:(ib + 1) * 128], xnb_sl(kc, t0, t0 + TN),
                        start=(kc == 0), stop=(kc == 7)), TE)
                t_h1_ib[ib] = TE.n
            for ib in range(4):
                P.use("tensor", ("bank", 4 + ib))
                for kc in range(NCB):
                    if ib == 0 and kc == 0:
                        for kc2 in range(NCB):
                            P.waitp("tensor", w3_toks[kc2][1])
                    P.emit("tensor", lambda e, b=4 + ib, t=w3_toks[kc][0], ib=ib, t0=t0, kc=kc: e.matmul(
                        PS[b][:], t[:, ib * 128:(ib + 1) * 128], xnb_sl(kc, t0, t0 + TN),
                        start=(kc == 0), stop=(kc == 7)), TE)
                t_h3_ib[ib] = TE.n
            if tau == NTILE - 1:
                for _, _, wtok in w1_toks:
                    P.release(wtok, TE)
                for _, _, kcsl in w3_toks:
                    P.release(("w3sl", kcsl), TE)
                if e_i + 1 < E:
                    nxt1 = load_w1(e_i + 1)
                    nxt3 = load_w3(e_i + 1)
            r = hst_r[0] % 2
            for ib in range(4):
                P.use("scalar", ("silu",))
                P.wait("scalar", TE, t_h1_ib[ib])
                if e_i == 0 and tau == 0 and ib == 0:
                    P.wait("scalar", VE, ve_p3)
                P.emit("scalar", lambda e, b=ib: e.activation(SILU, PS[b][:], AF.Silu), SE)
                se_si = SE.n
                P.release(("bank", ib), SE)
                if ib == 0:
                    P.use("vector", ("hst", r))
                P.wait("vector", SE, se_si)
                P.wait("vector", SE, cb_se)
                P.wait("vector", TE, t_h3_ib[ib])
                P.emit("vector", lambda e, b=4 + ib: e.tensor_tensor(SILU, SILU, PS[b][:], ALU.mult), VE)
                P.release(("bank", 4 + ib), VE)
                P.emit("vector", lambda e, r=r, ib=ib, tau=tau: e.tensor_tensor(
                    HST(r)[:, ib * 512:(ib + 1) * 512], SILU, COMBE[:, tau * 512:(tau + 1) * 512], ALU.mult), VE)
                ve_he = VE.n
                P.release(("silu",), VE)
                if ib == 3:
                    P.release(("combe", tau), VE)
            P.wait("sync", VE, ve_he)
            dma("sync", S_HS[r], H_s[tau][e_i], HST(r))
            P.release(("hst", r), S_HS[r])
            hst_r[0] += 1
        if e_i + 1 < E:
            w1_toks = nxt1
            w3_toks = nxt3
    f1_hs_toks = [(S_HS[0], S_HS[0].n), (S_HS[1], S_HS[1].n)]

    te_f1 = TE.n
    P.wait("sync", TE, te_f1)
    P.wait("sync", VE, ve_he)
    P.waitp("sync", cast_done["ws"])
    for kc in range(NCB):
        P.emit("sync", lambda e, kc=kc: e.dma_start(
            out=WS1T[:, kc * 512:(kc + 1) * 512], in_=ws1_b[0:512, kc * 128:(kc + 1) * 128], transpose=True), S_WS, 16)
    for kc in range(NCB):
        P.emit("sync", lambda e, kc=kc: e.dma_start(
            out=WS3T[:, kc * 512:(kc + 1) * 512], in_=ws3_b[0:512, kc * 128:(kc + 1) * 128], transpose=True), S_WS, 16)
    for ikc in range(4):
        P.emit("sync", lambda e, ikc=ikc: e.dma_start(
            out=WS2T[:, ikc * 1024:(ikc + 1) * 1024], in_=ws2_b[0:1024, ikc * 128:(ikc + 1) * 128], transpose=True), S_WS, 16)
    re_wsz = S_WS.n

    YB = list(range(8))
    for tau in range(NTILE):
        t0 = tau * TN
        for ib in range(4):
            b1 = 0 if ib % 2 == 0 else 1
            b3 = 2 if ib % 2 == 0 else 3
            P.use("tensor", ("bank", b1))
            P.use("tensor", ("bank", b3))
            P.wait("tensor", S_WS, re_wsz)
            for kc in range(NCB):
                P.emit("tensor", lambda e, b=b1, ib=ib, kc=kc, t0=t0: e.matmul(
                    PS[b][:], WS1T[:, kc * 512 + ib * 128: kc * 512 + (ib + 1) * 128], xnb_sl(kc, t0, t0 + TN),
                    start=(kc == 0), stop=(kc == 7)), TE)
            for kc in range(NCB):
                P.emit("tensor", lambda e, b=b3, ib=ib, kc=kc, t0=t0: e.matmul(
                    PS[b][:], WS3T[:, kc * 512 + ib * 128: kc * 512 + (ib + 1) * 128], xnb_sl(kc, t0, t0 + TN),
                    start=(kc == 0), stop=(kc == 7)), TE)
            t_z = TE.n
            P.use("scalar", ("silu",))
            P.wait("scalar", TE, t_z)
            if tau == 0 and ib == 0:
                P.wait("scalar", VE, ve_he)
            P.emit("scalar", lambda e, b=b1: e.activation(SILU, PS[b][:], AF.Silu), SE)
            se_zs = SE.n
            P.release(("bank", b1), SE)
            P.use("vector", ("zmid", ib))
            P.wait("vector", SE, se_zs)
            P.emit("vector", lambda e, ib=ib, b=b3: e.tensor_tensor(ZMID[:, ib * 512:(ib + 1) * 512], SILU, PS[b][:], ALU.mult), VE)
            P.release(("bank", b3), VE)
            P.release(("silu",), VE)
        ve_zmid = VE.n
        h_re = {}
        for b in YB:
            P.use("tensor", ("bank", b))
        for e_i in range(E):
            esub = e_i % 8
            P.use("sync", ("xxh", esub))
            if tau == 0 and e_i == 0:
                P.wait("sync", TE, te_p3)
                for tok in f1_hs_toks:
                    P.waitp("sync", tok)
            h_re[e_i] = (S_H, dma("sync", S_H, XX3b[:, esub, :], H_s[tau][e_i]))
            P.waitp("sync", cast_done["w2"])
            w2t = []
            for ikc in range(4):
                w2t.append(t_load2(w2_b[e_i][0:1024, ikc * 128:(ikc + 1) * 128]))
            P.waitp("tensor", h_re[e_i])
            for ikc in range(4):
                P.waitp("tensor", w2t[ikc][1])
                for db in range(8):
                    first = (e_i == 0 and ikc == 0)
                    P.emit("tensor", lambda e, b=YB[db], t=w2t[ikc][0], db=db, esub=esub, ikc=ikc, first=first: e.matmul(
                        PS[b][:], t[:, db * 128:(db + 1) * 128],
                        XX3b[:, esub, ikc * 512:(ikc + 1) * 512],
                        start=first, stop=False), TE)
                P.release(w2t[ikc][2], TE)
            if e_i >= 8:
                P.release(("xxh", esub), TE)
            else:
                P.release(("xxh", esub), TE)
        P.wait("tensor", VE, ve_zmid)
        for ikc in range(4):
            for db in range(8):
                P.emit("tensor", lambda e, b=YB[db], db=db, ikc=ikc: e.matmul(
                    PS[b][:], WS2T[:, ikc * 1024 + db * 128: ikc * 1024 + (db + 1) * 128],
                    ZMID[:, ikc * 512:(ikc + 1) * 512], start=False, stop=(ikc == 3)), TE)
        t_y = TE.n
        for esub in range(8):
            P.release(("xxh", esub), TE, t_y)
        for ib in range(4):
            P.release(("zmid", ib), TE, t_y)
        for db in range(8):
            P.use("scalar", ("gpre", db))
            P.wait("scalar", TE, t_y)
            P.emit("scalar", lambda e, db=db, b=YB[db]: e.copy(GPRE[:, db * 512:(db + 1) * 512], PS[b][:]), SE)
            P.release(("bank", YB[db]), SE)
        se_g = SE.n
        P.wait("sync", SE, se_g)
        if DEBUG:
            dma("sync", OUT, dbg["t_g"][tau], GPRE[:])
        n_gs = dma("sync", S_GS, g_s[tau], GPRE[:])
        for db in range(8):
            P.release(("gpre", db), S_GS, S_GS.n)
    spl_g = S_GS.n
    te_f2 = TE.n
    ve_f2 = VE.n

    STM = lambda r: AR[:, r * 1024: (r + 1) * 1024]
    XT2 = lambda r: AR[:, 2048 + r * 1024: 2048 + (r + 1) * 1024]
    OTB = lambda r: AR[:, 4096 + r * 1024: 4096 + (r + 1) * 1024]
    GTL = AR[:, 6144:8192].bitcast(BF16)
    GSL = lambda r: AR[:, 8192 + r * 1024: 8192 + (r + 1) * 1024]

    first_stm = [True]
    first_xt2 = [True]
    first_gs = [True]
    for tau in range(NTILE):
        P.use("sync", ("gtile",))
        if tau == 0:
            P.wait("sync", S_GS, spl_g)
            P.wait("sync", TE, te_f2)
        n_g = dma("sync", S_GT, GTL[:], g_s[tau])
        for tb4 in range(4):
            tb = tau * 4 + tb4
            r = tb % 2
            P.use("sync", ("xt2", r))
            if first_xt2[0]:
                P.wait("sync", TE, te_f2)
                first_xt2[0] = False
            n_x = dma("sync", S_X2[r], XT2(r), x_d[tb * 128:(tb + 1) * 128, :])
            P.use("scalar", ("stm", r))
            for g in range(2):
                bank = T0 if g == 0 else T1
                P.use("tensor", ("bank", bank))
                for j in range(4):
                    cb = g * 4 + j
                    P.emit("tensor", lambda e, cb=cb, j=j, b=bank, tb=tb: e.matmul(
                        PS[b][:, j * 128:(j + 1) * 128], A3[:, cb, tb * 128:(tb + 1) * 128],
                        ID32[:], is_transpose=True, start=(j == 0), stop=(j == 3)), TE)
                t_tr = TE.n
                P.wait("scalar", TE, t_tr)
                if first_stm[0]:
                    P.wait("scalar", S_GS, spl_g)
                    first_stm[0] = False
                P.emit("scalar", lambda e, r=r, g=g, b=bank: e.copy(STM(r)[:, g * 512:(g + 1) * 512], PS[b][:]), SE)
                P.release(("bank", bank), SE)
            bank = T0 if tb % 2 == 0 else T1
            P.use("tensor", ("bank", bank))
            P.wait("tensor", S_GT, n_g)
            for cb in range(NCB):
                P.emit("tensor", lambda e, cb=cb, b=bank, tb4=tb4: e.matmul(
                    PS[b][:].bitcast(BF16)[:, cb * 128:(cb + 1) * 128],
                    GTL[:, cb * 512 + tb4 * 128: cb * 512 + tb4 * 128 + 128],
                    IDB[:], is_transpose=True, start=(cb == 0), stop=(cb == 7)), TE)
            t_gt = TE.n
            P.use("scalar", ("gsil", r))
            P.wait("scalar", TE, t_gt)
            if first_gs[0]:
                for tok in f1_hs_toks:
                    P.waitp("scalar", tok)
                P.wait("scalar", VE, ve_f2)
                first_gs[0] = False
            P.emit("scalar", lambda e, r=r, b=bank: e.activation(GSL(r), PS[b][:].bitcast(BF16)[:, 0:1024], AF.Silu), SE)
            se_gs = SE.n
            P.release(("bank", bank), SE)
            P.use("vector", ("ot", r))
            P.wait("vector", SE, se_gs)
            P.wait("vector", S_X2[r], n_x)
            P.emit("vector", lambda e, r=r: e.tensor_tensor(STM(r), STM(r), GSL(r), ALU.mult))
            P.emit("vector", lambda e, r=r: e.tensor_tensor(OTB(r), STM(r), XT2(r), ALU.add), VE)
            ve_ot = VE.n
            P.release(("stm", r), VE)
            P.release(("gsil", r), VE)
            P.release(("xt2", r), VE)
            P.wait("sync", VE, ve_ot)
            dma("sync", S_OT[r], out_d[tb * 128:(tb + 1) * 128, :], OTB(r))
            P.release(("ot", r), S_OT[r])
        P.release(("gtile",), TE)

    P.wait("sync", S_OT[0], S_OT[0].n)
    P.wait("sync", S_OT[1], S_OT[1].n)
    if DEBUG:
        P.wait("sync", OUT, OUT.n)

    with nc.Block() as block:
        @block.sync
        def _(e):
            for f in P.steps["sync"]:
                f(e)

        @block.gpsimd
        def _(e):
            for f in P.steps["gpsimd"]:
                f(e)

        @block.tensor
        def _(e):
            for f in P.steps["tensor"]:
                f(e)

        @block.scalar
        def _(e):
            for f in P.steps["scalar"]:
                f(e)

        @block.vector
        def _(e):
            for f in P.steps["vector"]:
                f(e)

    es.close()
    return nc


_NC_CACHE = None


def kernel(**inputs):
    global _NC_CACHE
    if _NC_CACHE is None:
        _NC_CACHE = build_nc()
    nc = _NC_CACHE

    x = np.ascontiguousarray(np.asarray(inputs["L0_attn_x"], dtype=np.float32))
    _ = inputs["L0_attn_time_decay"]
    basis = np.zeros((E, E * 128), dtype=ml_dtypes.bfloat16)
    for e_i in range(E):
        basis[e_i, e_i * 128:(e_i + 1) * 128] = 1.0
    common = {
        "Wk": np.ascontiguousarray(np.asarray(inputs["L0_attn_Wk"], np.float32)),
        "Wv": np.ascontiguousarray(np.asarray(inputs["L0_attn_Wv"], np.float32)),
        "Wr": np.ascontiguousarray(np.asarray(inputs["L0_attn_Wr"], np.float32)),
        "Wck": np.ascontiguousarray(np.asarray(inputs["L0_attn_Wck"], np.float32)),
        "Wcv": np.ascontiguousarray(np.asarray(inputs["L0_attn_Wcv"], np.float32)),
        "Wcr": np.ascontiguousarray(np.asarray(inputs["L0_attn_Wcr"], np.float32)),
        "ln1w": np.ascontiguousarray(np.asarray(inputs["L0_attn_ln1_w"], np.float32)),
        "ln1b": np.ascontiguousarray(np.asarray(inputs["L0_attn_ln1_b"], np.float32)),
        "ln2w": np.ascontiguousarray(np.asarray(inputs["L0_attn_ln2_w"], np.float32)),
        "ln2b": np.ascontiguousarray(np.asarray(inputs["L0_attn_ln2_b"], np.float32)),
        "tshift": np.ascontiguousarray(np.asarray(inputs["L0_attn_time_shift"], np.float32)),
        "tfirst": np.ascontiguousarray(np.asarray(inputs["L0_attn_time_first"], np.float32)),
        "gate_w": np.ascontiguousarray(np.asarray(inputs["L0_ffn_gate_w"], np.float32)),
        "gate_bias": np.ascontiguousarray(np.asarray(inputs["L0_ffn_gate_bias"], np.float32)),
        "w1": np.ascontiguousarray(np.asarray(inputs["L0_ffn_w1"], np.float32)),
        "w2": np.ascontiguousarray(np.asarray(inputs["L0_ffn_w2"], np.float32)),
        "w3": np.ascontiguousarray(np.asarray(inputs["L0_ffn_w3"], np.float32)),
        "ws1": np.ascontiguousarray(np.asarray(inputs["L0_ffn_ws1"], np.float32)),
        "ws2": np.ascontiguousarray(np.asarray(inputs["L0_ffn_ws2"], np.float32)),
        "ws3": np.ascontiguousarray(np.asarray(inputs["L0_ffn_ws3"], np.float32)),
        "identb": np.eye(128, dtype=ml_dtypes.bfloat16),
        "ident32": np.eye(128, dtype=np.float32),
        "basis": basis,
    }
    in_maps = [dict(common, x=np.ascontiguousarray(x[b])) for b in range(8)]
    res = run_bass_kernel_spmd(nc, in_maps, core_ids=list(range(8)))
    out = np.stack([np.asarray(res.results[b]["out"], np.float32) for b in range(8)], axis=0)
    return out


if __name__ == "__main__":
    build_nc()
    print("built ok")


# version 10
# speedup vs baseline: 1.2815x; 1.0118x over previous
import numpy as np
import ml_dtypes
from contextlib import ExitStack
import concourse.bass as bass
import concourse.mybir as mybir
from concourse.bass_utils import run_bass_kernel_spmd

F32 = mybir.dt.float32
BF16 = mybir.dt.bfloat16
AF = mybir.ActivationFunctionType
ALU = mybir.AluOpType
AX = mybir.AxisListType

T = 2048
C = 1024
NCB = 8
NTB = 16
NTILE = 4
TN = 512
E = 16
LN_EPS = 1e-5

DEBUG = False


class Sem:
    def __init__(self, h):
        self.h = h
        self.n = 0


class Plan:
    ENGINES = ("sync", "gpsimd", "tensor", "scalar", "vector")

    def __init__(self):
        self.steps = {e: [] for e in self.ENGINES}
        self.free = {}

    def wait(self, eng, sem, val):
        if val > sem.n:
            raise RuntimeError(f"wait for {val}, sem at {sem.n}")
        if val > 0:
            self.steps[eng].append(lambda e, s=sem.h, v=val: e.wait_ge(s, v))

    def emit(self, eng, build, sem=None, k=1):
        if sem is not None:
            sem.n += k
            self.steps[eng].append(lambda e, b=build, s=sem.h, kk=k: b(e).then_inc(s, kk))
            return sem.n
        self.steps[eng].append(build)
        return None

    def use(self, eng, key):
        for sem, val in self.free.get(key, ()):
            self.wait(eng, sem, val)

    def release(self, key, sem, val=None):
        self.free[key] = [(sem, sem.n if val is None else val)]

    def release_multi(self, key, toks):
        self.free[key] = list(toks)

    def drain(self, eng):
        self.steps[eng].append(lambda e: e.drain())

    def waitp(self, eng, tok):
        self.wait(eng, tok[0], tok[1])


def build_nc():
    nc = bass.Bass("TRN2", target_bir_lowering=False)

    def din(name, shape, dt=F32):
        return nc.dram_tensor(name, shape, dt, kind="ExternalInput").ap()

    x_d = din("x", [T, C])
    Wk_d = din("Wk", [C, C]); Wv_d = din("Wv", [C, C]); Wr_d = din("Wr", [C, C])
    Wck_d = din("Wck", [4096, C]); Wcv_d = din("Wcv", [C, 4096]); Wcr_d = din("Wcr", [C, C])
    ln1w_d = din("ln1w", [C]); ln1b_d = din("ln1b", [C])
    ln2w_d = din("ln2w", [C]); ln2b_d = din("ln2b", [C])
    ts_d = din("tshift", [1, 1, C]); u_d = din("tfirst", [C])
    gw_d = din("gate_w", [E, C]); gb_d = din("gate_bias", [E])
    w1_d = din("w1", [E, 512, C]); w2_d = din("w2", [E, C, 512]); w3_d = din("w3", [E, 512, C])
    ws1_d = din("ws1", [512, C]); ws2_d = din("ws2", [C, 512]); ws3_d = din("ws3", [512, C])
    idb_d = din("identb", [128, 128], BF16)
    id32_d = din("ident32", [128, 128], F32)
    bas_d = din("basis", [E, E * 128], BF16)

    out_d = nc.dram_tensor("out", [T, C], F32, kind="ExternalOutput").ap()

    def scrt(name, shape):
        return nc.dram_tensor(name, shape, BF16, kind="Internal").ap()

    wk_b = scrt("wk_b", [C, C]); wv_b = scrt("wv_b", [C, C]); wr_b = scrt("wr_b", [C, C])
    wck_b = scrt("wck_b", [4096, C]); wcr_b = scrt("wcr_b", [C, C]); wcv_b = scrt("wcv_b", [C, 4096])
    w1_b = scrt("w1_b", [E, 512, C]); w3_b = scrt("w3_b", [E, 512, C]); w2_b = scrt("w2_b", [E, C, 512])
    ws1_b = scrt("ws1_b", [512, C]); ws3_b = scrt("ws3_b", [512, C]); ws2_b = scrt("ws2_b", [C, 512])
    H_s = scrt("H_s", [NTILE, E, 128, 2048])
    g_s = scrt("g_s", [NTILE, 128, 4096])

    dbg = {}
    if DEBUG:
        dbg["t_p0"] = nc.dram_tensor("t_p0", [128, NCB * T], F32, kind="ExternalOutput").ap()
        dbg["t_xxx"] = nc.dram_tensor("t_xxx", [128, NCB * T], BF16, kind="ExternalOutput").ap()
        dbg["t_x1"] = nc.dram_tensor("t_x1", [128, NCB * T], F32, kind="ExternalOutput").ap()
        dbg["t_s"] = nc.dram_tensor("t_s", [128, NCB * T], F32, kind="ExternalOutput").ap()
        dbg["t_cmf"] = nc.dram_tensor("t_cmf", [16, E * 128], BF16, kind="ExternalOutput").ap()
        dbg["t_xnb"] = nc.dram_tensor("t_xnb", [128, NCB * T], BF16, kind="ExternalOutput").ap()
        dbg["t_gt"] = nc.dram_tensor("t_gt", [128, NCB * E], BF16, kind="ExternalOutput").ap()
        dbg["t_gbr"] = nc.dram_tensor("t_gbr", [128, E], F32, kind="ExternalOutput").ap()
        dbg["t_g"] = nc.dram_tensor("t_g", [NTILE, 128, 4096], BF16, kind="ExternalOutput").ap()
        dbg["t_kv"] = nc.dram_tensor("t_kv", [128, 5120], F32, kind="ExternalOutput").ap()

    es = ExitStack()
    sb = lambda name, shape, dt: es.enter_context(nc.sbuf_tensor(name, shape, dt))

    A = sb("A_stream", [128, NCB * T], F32)
    XNB = sb("A_xnb", [128, NCB * T], BF16)
    XX = sb("A_xxx", [128, NCB * T], BF16)
    AR = sb("AR", [128, 13376], F32)
    WT = sb("W_T", [128, 8 * 512], BF16)
    WT2 = sb("W_T2", [128, 4 * 1024], BF16)
    IDB = sb("identb_s", [128, 128], BF16)
    ID32 = sb("ident32_s", [128, 128], F32)
    ONECOL = sb("onecol", [128, 1], F32)
    ONEROW = sb("onerow", [1, 128], F32)
    ROWS = sb("rows", [1, 1024], F32)
    MU1C = sb("mu1c", [128, NTB], F32)
    RS1C = sb("rs1c", [128, NTB], F32)
    VEC = sb("veccols", [128, 10 * NCB], F32)
    CAR = sb("carries", [128, 2 * NCB], F32)
    H2P = sb("h2prev", [128, NCB], F32)
    GATET = sb("gateT", [128, NCB * E], BF16)
    GBR = sb("gbrow", [128, E], F32)
    GBS = sb("gbs", [1, E], F32)
    SCR = sb("scr", [128, 512], F32)

    V_U, V_TS, V_1PTS, V_NTS, V_1MTS, V_L1W, V_L1B, V_L2W, V_L2B, V_SP = range(10)

    def vcol(i, cb):
        return VEC[:, i * NCB + cb: i * NCB + cb + 1]

    def vcols(i):
        return VEC[:, i * NCB: (i + 1) * NCB]

    PS = [es.enter_context(nc.psum_tensor(f"ps{i}", [128, 512], F32)) for i in range(8)]
    MM0, MM1, MM2, MM3, T0, T1, ST, SP2 = range(8)

    mksem = lambda nm: Sem(es.enter_context(nc.semaphore(nm)))
    TE = mksem("TE"); SE = mksem("SE"); VE = mksem("VE"); OUT = mksem("OUT")
    S_CONST = mksem("S_CONST"); S_GW = mksem("S_GW"); S_MEM = mksem("S_MEM")
    S_XT = [mksem(f"S_XT{i}") for i in range(3)]
    S_WT = [mksem(f"S_WT{i}") for i in range(8)]
    S_WU = [mksem(f"S_WU{i}") for i in range(4)]
    S_CAST = {f: mksem(f"S_C_{f}") for f in
              ("wk", "wv", "wr", "wck", "wcr", "wcv", "w1", "w3", "w2", "ws")}
    S_W3 = mksem("S_W3"); S_HS = [mksem(f"S_HS{i}") for i in range(2)]
    S_H = mksem("S_H"); S_WS = mksem("S_WS"); S_GS = mksem("S_GS")
    S_GT = mksem("S_GT")
    S_X2 = [mksem(f"S_X2{i}") for i in range(2)]
    S_OT = [mksem(f"S_OT{i}") for i in range(2)]

    P = Plan()
    free = P.free

    A3 = A[:].rearrange("p (cb t) -> p cb t", t=T)
    XNB3 = XNB[:].rearrange("p (cb t) -> p cb t", t=T)
    XX3b = XX[:].rearrange("p (s n) -> p s n", n=2048)

    def a_sl(cb, a, b):
        return A[:, cb * T + a: cb * T + b]

    def xnb_sl(cb, a, b):
        return XNB[:, cb * T + a: cb * T + b]

    def xx_sl(cb, a, b):
        return XX[:, cb * T + a: cb * T + b]

    def dma(eng, sem, out, in_):
        return P.emit(eng, lambda e, o=out, i=in_: e.dma_start(out=o, in_=i), sem, 16)

    P.emit("gpsimd", lambda e: e.memset(ONECOL[:], 1.0))
    P.emit("gpsimd", lambda e: e.memset(ONEROW[:], 1.0))
    P.emit("gpsimd", lambda e: e.memset(CAR[:, 0:NCB], 1.0))
    P.emit("gpsimd", lambda e: e.memset(CAR[:, NCB:2 * NCB], 0.0))
    P.emit("gpsimd", lambda e: e.memset(SCR[:, 0:128], 1.0), S_MEM, 1)
    n_memset = S_MEM.n

    cast_done = {}

    def cast(fam, dst, src_):
        n = dma("gpsimd", S_CAST[fam], dst, src_)
        cast_done[fam] = (S_CAST[fam], n)

    cast("wk", wk_b, Wk_d)
    cast("wv", wv_b, Wv_d)
    cast("wr", wr_b, Wr_d)

    wt_ptr = [0]
    wt2_ptr = [0]

    def t_load(src_rows):
        sl = wt_ptr[0] % 8
        wt_ptr[0] += 1
        dst = WT[:, sl * 512: sl * 512 + src_rows.shape[0]]
        P.use("sync", ("wt", sl))
        n = P.emit("sync", lambda e, o=dst, i=src_rows: e.dma_start(out=o, in_=i, transpose=True), S_WT[sl], 16)
        return dst, (S_WT[sl], n), ("wt", sl)

    def t_load2(src_rows):
        sl = wt2_ptr[0] % 4
        wt2_ptr[0] += 1
        dst = WT2[:, sl * 1024: sl * 1024 + src_rows.shape[0]]
        P.use("sync", ("wu", sl))
        n = P.emit("sync", lambda e, o=dst, i=src_rows: e.dma_start(out=o, in_=i, transpose=True), S_WU[sl], 16)
        return dst, (S_WU[sl], n), ("wu", sl)

    dma("sync", S_CONST, IDB[:], idb_d)
    dma("sync", S_CONST, ID32[:], id32_d)
    dma("sync", S_CONST, GBS[:], gb_d.rearrange("(a b) -> a b", a=1))
    VSTG = WT[:].bitcast(F32)[0:8, 0:768]
    vec_list = [(u_d, V_U), (ts_d.rearrange("x y c -> (x y c)"), V_TS), (ln1w_d, V_L1W),
                (ln1b_d, V_L1B), (ln2w_d, V_L2W), (ln2b_d, V_L2B)]
    for i, (src, _) in enumerate(vec_list):
        dma("sync", S_CONST, VSTG[:, i * 128:(i + 1) * 128], src.rearrange("(a b) -> a b", b=128))
    GWS = WT2[0:16, 0:1024]
    dma("gpsimd", S_GW, GWS, gw_d)
    n_in_const = S_CONST.n
    n_gw = S_GW.n


    P.wait("tensor", S_CONST, n_in_const)
    P.wait("tensor", S_MEM, n_memset)
    for i in range(len(vec_list)):
        P.emit("tensor", lambda e, i=i: e.matmul(
            PS[T0][:, i * 8:(i + 1) * 8], VSTG[:, i * 128:(i + 1) * 128], ID32[0:8, 0:8],
            is_transpose=True, start=(i == 0), stop=(i == len(vec_list) - 1)), TE)
    t_vec = TE.n
    P.wait("scalar", TE, t_vec)
    for i, (_, vidx) in enumerate(vec_list):
        P.emit("scalar", lambda e, i=i, v=vidx: e.copy(vcols(v), PS[T0][:, i * 8:(i + 1) * 8]), SE)
    se_vec = SE.n
    P.release(("bank", T0), SE)
    for _s in range(6):
        P.release(("wt", _s), TE, t_vec)
    P.wait("vector", SE, se_vec)
    P.emit("vector", lambda e: e.tensor_scalar(vcols(V_1PTS), vcols(V_TS), 1.0, None, ALU.add, ALU.bypass))
    P.emit("vector", lambda e: e.tensor_scalar(vcols(V_NTS), vcols(V_TS), -1.0, None, ALU.mult, ALU.bypass))
    P.emit("vector", lambda e: e.tensor_scalar(vcols(V_1MTS), vcols(V_TS), -1.0, 1.0, ALU.mult, ALU.add), VE)
    ve_vcols = VE.n

    P.wait("tensor", S_GW, n_gw)
    for kc in range(NCB):
        P.emit("tensor", lambda e, kc=kc: e.matmul(
            PS[T1][:].bitcast(BF16)[:, kc * 16:(kc + 1) * 16],
            GWS[:, kc * 128:(kc + 1) * 128], IDB[0:16, 0:16],
            is_transpose=True, start=(kc == 0), stop=(kc == 7)), TE)
    t_gate = TE.n
    P.wait("scalar", TE, t_gate)
    P.emit("scalar", lambda e: e.copy(GATET[:], PS[T1][:].bitcast(BF16)[:, 0:NCB * E]), SE)
    se_gate = SE.n
    P.release(("bank", T1), SE)
    P.release(("wu", 0), SE)
    P.emit("tensor", lambda e: e.matmul(PS[SP2][:, 0:16], ONEROW[:], GBS[:], start=True, stop=True), TE)
    t_gbr = TE.n
    P.wait("scalar", TE, t_gbr)
    P.emit("scalar", lambda e: e.copy(GBR[:], PS[SP2][:, 0:16]), SE)
    P.release(("bank", SP2), SE)

    XTM = lambda sl: AR[:, 10240 + sl * 1024: 10240 + (sl + 1) * 1024]
    SQD = AR[:, 9216:10240]
    se_tb = {}
    for tb in range(NTB):
        sl = tb % 3
        CSX = lambda i, sl=sl: AR[:, 13320 + sl * 16 + i: 13321 + sl * 16 + i]
        P.use("sync", ("xtm", sl))
        n_ld = dma("sync", S_XT[sl], XTM(sl), x_d[tb * 128:(tb + 1) * 128, :])
        P.wait("vector", S_XT[sl], n_ld)
        P.use("vector", ("csx", sl))
        P.emit("vector", lambda e, sl=sl, CSX=CSX: e.tensor_reduce(CSX(0), XTM(sl), AX.X, ALU.add), VE)
        P.drain("vector")
        P.wait("scalar", S_XT[sl], n_ld)
        P.use("scalar", ("sqd",))
        P.use("scalar", ("csxs", sl))
        P.emit("scalar", lambda e, sl=sl, CSX=CSX: e.activation(SQD, XTM(sl), AF.Square, accum_out=CSX(1)), SE)
        se_sq = SE.n
        P.wait("vector", SE, se_sq)
        P.emit("vector", lambda e, CSX=CSX: e.tensor_scalar(CSX(2), CSX(0), 1.0 / C, None, ALU.mult, ALU.bypass))
        P.emit("vector", lambda e, CSX=CSX: e.tensor_scalar(CSX(3), CSX(1), 1.0 / C, None, ALU.mult, ALU.bypass))
        P.drain("vector")
        P.emit("vector", lambda e, CSX=CSX: e.tensor_scalar(CSX(4), CSX(2), CSX(2), None, ALU.mult, ALU.bypass))
        P.drain("vector")
        P.emit("vector", lambda e, CSX=CSX: e.tensor_scalar(CSX(5), CSX(3), CSX(4), LN_EPS, ALU.subtract, ALU.add), VE)
        ve_var = VE.n
        P.wait("scalar", VE, ve_var)
        P.emit("scalar", lambda e, CSX=CSX: e.activation(CSX(6), CSX(5), AF.Ln))
        P.drain("scalar")
        P.emit("scalar", lambda e, CSX=CSX: e.activation(CSX(6), CSX(6), AF.Exp, scale=-0.5), SE)
        se_rstd = SE.n
        P.wait("vector", SE, se_rstd)
        P.emit("vector", lambda e, CSX=CSX, sl=sl: e.tensor_scalar(XTM(sl), XTM(sl), CSX(2), CSX(6), ALU.subtract, ALU.mult), VE)
        ve_xn = VE.n
        P.emit("vector", lambda e, CSX=CSX, sl=sl: e.tensor_reduce(CSX(7), XTM(sl), AX.X, ALU.add), VE)
        P.drain("vector")
        P.wait("scalar", VE, ve_xn)
        P.emit("scalar", lambda e, CSX=CSX, sl=sl: e.activation(SQD, XTM(sl), AF.Square, accum_out=CSX(8)), SE)
        se_sq1 = SE.n
        P.release(("sqd",), SE)
        P.wait("vector", SE, se_sq1)
        P.emit("vector", lambda e, CSX=CSX, tb=tb: e.tensor_scalar(MU1C[:, tb:tb + 1], CSX(7), 1.0 / C, None, ALU.mult, ALU.bypass))
        P.emit("vector", lambda e, CSX=CSX: e.tensor_scalar(CSX(9), CSX(8), 1.0 / C, None, ALU.mult, ALU.bypass))
        P.drain("vector")
        P.emit("vector", lambda e, CSX=CSX, tb=tb: e.tensor_scalar(CSX(10), MU1C[:, tb:tb + 1], MU1C[:, tb:tb + 1], None, ALU.mult, ALU.bypass))
        P.drain("vector")
        P.emit("vector", lambda e, CSX=CSX: e.tensor_scalar(CSX(9), CSX(9), CSX(10), LN_EPS, ALU.subtract, ALU.add), VE)
        ve_v1 = VE.n
        P.wait("scalar", VE, ve_v1)
        P.emit("scalar", lambda e, CSX=CSX: e.activation(CSX(9), CSX(9), AF.Ln))
        P.drain("scalar")
        P.emit("scalar", lambda e, CSX=CSX, tb=tb: e.activation(RS1C[:, tb:tb + 1], CSX(9), AF.Exp, scale=-0.5), SE)
        prev_se = SE.n
        P.release(("csx", sl), SE)
        P.release(("csxs", sl), VE, ve_v1)
        toks = [(VE, ve_v1), (SE, prev_se)]
        for g in range(2):
            bank = T0 if g == 0 else T1
            P.use("tensor", ("bank", bank))
            P.wait("tensor", VE, ve_xn)
            for j in range(4):
                cb = g * 4 + j
                P.emit("tensor", lambda e, sl=sl, cb=cb, j=j, b=bank: e.matmul(
                    PS[b][:, j * 128:(j + 1) * 128], XTM(sl)[:, cb * 128:(cb + 1) * 128],
                    ID32[:], is_transpose=True, start=(j == 0), stop=(j == 3)), TE)
            t_tr = TE.n
            toks.append((TE, t_tr))
            P.wait("scalar", TE, t_tr)
            ps4 = PS[bank][:].rearrange("p (j t) -> p j t", t=128)
            P.emit("scalar", lambda e, d=A3[:, g * 4:g * 4 + 4, tb * 128:(tb + 1) * 128], s=ps4: e.copy(d, s), SE)
            P.emit("scalar", lambda e, d=XNB3[:, g * 4:g * 4 + 4, tb * 128:(tb + 1) * 128], s=ps4: e.copy(d, s), SE)
            P.release(("bank", bank), SE)
        P.release_multi(("xtm", sl), toks)
        se_tb[tb] = SE.n
    se_p0 = SE.n

    if DEBUG:
        P.wait("sync", SE, se_p0)
        n_dbg0 = dma("sync", OUT, dbg["t_p0"], A[:])

    HB = lambda cb, a, b: AR[:, 3104 + cb * 513 + a: 3104 + cb * 513 + b]
    rows_mu = ROWS[0:1, 0:512]
    rows_rs = ROWS[0:1, 512:1024]
    P.wait("vector", VE, ve_vcols)
    for cb in range(NCB):
        P.emit("vector", lambda e, cb=cb: e.tensor_copy(HB(cb, 0, 1), vcol(V_TS, cb)), VE)
    ve_xxx_tile = {}
    for tau in range(NTILE):
        t0 = tau * TN
        P.use("tensor", ("bank", ST))
        P.use("tensor", ("bank", SP2))
        P.wait("tensor", SE, se_tb[tau * 4 + 3])
        for j in range(4):
            P.emit("tensor", lambda e, j=j, tau=tau: e.matmul(
                PS[ST][0:1, j * 128:(j + 1) * 128], MU1C[:, tau * 4 + j:tau * 4 + j + 1],
                ID32[:], is_transpose=True, start=(j == 0), stop=(j == 3)), TE)
        for j in range(4):
            P.emit("tensor", lambda e, j=j, tau=tau: e.matmul(
                PS[SP2][0:1, j * 128:(j + 1) * 128], RS1C[:, tau * 4 + j:tau * 4 + j + 1],
                ID32[:], is_transpose=True, start=(j == 0), stop=(j == 3)), TE)
        t_rows = TE.n
        P.use("scalar", ("rows",))
        P.wait("scalar", TE, t_rows)
        P.emit("scalar", lambda e: e.copy(rows_mu, PS[ST][0:1, 0:512]))
        P.emit("scalar", lambda e: e.copy(rows_rs, PS[SP2][0:1, 0:512]), SE)
        se_rows = SE.n
        P.release(("bank", ST), SE)
        P.release(("bank", SP2), SE)
        P.use("tensor", ("bank", MM2))
        P.use("tensor", ("bank", MM3))
        P.wait("tensor", SE, se_rows)
        P.emit("tensor", lambda e: e.matmul(PS[MM2][:], ONEROW[:], rows_mu, start=True, stop=True), TE)
        P.emit("tensor", lambda e: e.matmul(PS[MM3][:], ONEROW[:], rows_rs, start=True, stop=True), TE)
        t_bc = TE.n
        P.release(("rows",), TE)
        P.wait("vector", TE, t_bc)
        for cb in range(NCB):
            tmp = AR[:, 7208 + (cb % 2) * 512: 7208 + (cb % 2) * 512 + 512]
            P.emit("vector", lambda e, cb=cb, t0=t0: e.tensor_tensor(HB(cb, 1, 513), a_sl(cb, t0, t0 + TN), PS[MM2][:], ALU.subtract))
            P.emit("vector", lambda e, cb=cb: e.tensor_tensor(HB(cb, 1, 513), HB(cb, 1, 513), PS[MM3][:], ALU.mult))
            P.emit("vector", lambda e, cb=cb: e.tensor_scalar(HB(cb, 1, 513), HB(cb, 1, 513), vcol(V_L1W, cb), vcol(V_L1B, cb), ALU.mult, ALU.add))
            P.emit("vector", lambda e, cb=cb, tmp=tmp: e.tensor_scalar(tmp, HB(cb, 1, 513), vcol(V_1PTS, cb), None, ALU.mult, ALU.bypass))
            P.emit("vector", lambda e, cb=cb, tmp=tmp, t0=t0: e.scalar_tensor_tensor(
                xx_sl(cb, t0, t0 + TN), HB(cb, 0, 512), vcol(V_NTS, cb), tmp, ALU.mult, ALU.add), VE)
        ve_xxx_tile[tau] = VE.n
        P.release(("bank", MM2), VE)
        P.release(("bank", MM3), VE)
        if tau < NTILE - 1:
            for cb in range(NCB):
                P.emit("vector", lambda e, cb=cb: e.tensor_copy(HB(cb, 0, 1), HB(cb, 512, 513)), VE)
    ve_p1 = VE.n

    if DEBUG:
        P.wait("sync", VE, ve_p1)
        dma("sync", OUT, dbg["t_xxx"], XX[:])

    kv_k = lambda j, a, b: AR[:, j * 2560 + a: j * 2560 + b]
    kv_v = lambda j, a, b: AR[:, j * 2560 + 1024 + a: j * 2560 + 1024 + b]
    kv_r = lambda j: AR[:, j * 2560 + 2048: j * 2560 + 2560].bitcast(BF16)
    SBASE = lambda p: 10240 + p * 770
    S_EK = lambda p: AR[:, SBASE(p): SBASE(p) + 128]
    S_EU = lambda p: AR[:, SBASE(p) + 128: SBASE(p) + 256]
    S_T1 = lambda p: AR[:, SBASE(p) + 256: SBASE(p) + 384]
    S_T2 = lambda p: AR[:, SBASE(p) + 384: SBASE(p) + 512]
    S_DB = lambda p: AR[:, SBASE(p) + 512: SBASE(p) + 641]
    S_AB = lambda p: AR[:, SBASE(p) + 641: SBASE(p) + 770]
    S_ONE = SCR[:, 0:128]

    prod_done = {}

    def emit_prod_half(quad, h):
        obs = [quad * 4 + i for i in range(4)]
        for kind, fam_b, fam in (("k", wk_b, "wk"), ("v", wv_b, "wv"), ("r", wr_b, "wr")):
            if h == 0 and quad == 0 and kind == "k":
                pass
            toks = []
            for kc in range(NCB):
                P.waitp("sync", cast_done[fam])
                tile, retok, wtok = t_load(fam_b[quad * 512:(quad + 1) * 512, kc * 128:(kc + 1) * 128])
                first_mm_te = None
                for ob_l in range(4):
                    for tt_l in range(2):
                        bank = ob_l * 2 + tt_l
                        tt = 2 * h + tt_l
                        if kc == 0:
                            P.use("tensor", ("bank", bank))
                            if kind == "k" and tt_l == 0:
                                P.use("tensor", ("kvr", ob_l))
                        if ob_l == 0 and tt_l == 0:
                            P.waitp("tensor", retok)
                            P.wait("tensor", VE, ve_p1)
                        P.emit("tensor", lambda e, b=bank, kc=kc, t=tile, tt=tt, ob_l=ob_l: e.matmul(
                            PS[b][:], t[:, ob_l * 128:(ob_l + 1) * 128], xx_sl(kc, tt * TN, (tt + 1) * TN),
                            start=(kc == 0), stop=(kc == 7)), TE)
                P.release(wtok, TE)
            for ob_l in range(4):
                for tt_l in range(2):
                    bank = ob_l * 2 + tt_l
                    j = ob_l
                    P.wait("scalar", TE, TE.n)
                    if kind == "k":
                        dst = kv_k(j, tt_l * TN, (tt_l + 1) * TN)
                    elif kind == "v":
                        dst = kv_v(j, tt_l * TN, (tt_l + 1) * TN)
                    else:
                        dst = kv_r(j)[:, tt_l * TN:(tt_l + 1) * TN]
                    P.emit("scalar", lambda e, d=dst, b=bank: e.copy(d, PS[b][:]), SE)
                    P.release(("bank", bank), SE)
        for cb in obs:
            prod_done[(cb, h)] = SE.n

    def emit_scan_quad(cbs, h):
        TC = 128
        for p, cb in enumerate(cbs):
            P.wait("vector", SE, prod_done[(cb, h)])
        for q in range(8):
            a = q * TC
            ga = h * 1024 + a
            se_ek = [0] * 4; ve_m = [0] * 4; se_rm = [0] * 4; ve_den = [0] * 4; se_sig = [0] * 4
            for p, cb in enumerate(cbs):
                j = cb % 4
                kq = kv_k(j, a, a + TC)
                P.wait("scalar", S_MEM, n_memset)
                P.emit("scalar", lambda e, k=kq, p=p: e.activation(S_EK(p), k, AF.Exp))
                P.emit("scalar", lambda e, k=kq, p=p, cb=cb: e.activation(S_EU(p), k, AF.Exp, bias=vcol(V_U, cb)), SE)
                se_ek[p] = SE.n
            for p, cb in enumerate(cbs):
                dcar = CAR[:, cb:cb + 1]
                P.wait("vector", SE, se_ek[p])
                P.emit("vector", lambda e, d=dcar, p=p: e.tensor_copy(S_DB(p)[:, 0:1], d))
                P.emit("vector", lambda e, d=dcar, p=p: e.tensor_tensor_scan(S_DB(p)[:, 1:129], S_ONE, S_EK(p), d, ALU.mult, ALU.add))
                P.emit("vector", lambda e, p=p: e.tensor_tensor(S_T1(p), S_DB(p)[:, 0:128], S_EK(p), ALU.max), VE)
                ve_m[p] = VE.n
            for p in range(4):
                P.wait("scalar", VE, ve_m[p])
                P.emit("scalar", lambda e, p=p: e.activation(S_T2(p), S_T1(p), AF.Ln))
                P.emit("scalar", lambda e, p=p: e.activation(S_T1(p), S_T2(p), AF.Exp, scale=-1.0), SE)
                se_rm[p] = SE.n
            for p, cb in enumerate(cbs):
                j = cb % 4
                vq = kv_v(j, a, a + TC)
                acar = CAR[:, NCB + cb:NCB + cb + 1]
                dcar = CAR[:, cb:cb + 1]
                P.wait("vector", SE, se_rm[p])
                P.emit("vector", lambda e, p=p: e.tensor_tensor(S_T2(p), S_DB(p)[:, 0:128], S_T1(p), ALU.mult))
                P.emit("vector", lambda e, p=p: e.tensor_tensor(S_EK(p), S_EK(p), S_T1(p), ALU.mult))
                P.emit("vector", lambda e, v=vq, p=p: e.tensor_tensor(S_EK(p), S_EK(p), v, ALU.mult))
                P.emit("vector", lambda e, ac=acar, p=p: e.tensor_copy(S_AB(p)[:, 0:1], ac))
                P.emit("vector", lambda e, ac=acar, p=p: e.tensor_tensor_scan(S_AB(p)[:, 1:129], S_T2(p), S_EK(p), ac, ALU.mult, ALU.add))
                P.emit("vector", lambda e, d=dcar, p=p: e.tensor_copy(d, S_DB(p)[:, 128:129]))
                P.emit("vector", lambda e, ac=acar, p=p: e.tensor_copy(ac, S_AB(p)[:, 128:129]))
                P.emit("vector", lambda e, p=p: e.tensor_tensor(S_T1(p), S_DB(p)[:, 0:128], S_AB(p)[:, 0:128], ALU.mult))
                P.emit("vector", lambda e, v=vq, p=p: e.tensor_tensor(S_T2(p), S_EU(p), v, ALU.mult))
                P.emit("vector", lambda e, p=p: e.tensor_tensor(S_T1(p), S_T1(p), S_T2(p), ALU.add))
                P.emit("vector", lambda e, p=p: e.tensor_tensor(S_T2(p), S_DB(p)[:, 0:128], S_EU(p), ALU.add), VE)
                ve_den[p] = VE.n
            for p, cb in enumerate(cbs):
                j = cb % 4
                rq = kv_r(j)[:, a:a + TC]
                P.wait("scalar", VE, ve_den[p])
                P.emit("scalar", lambda e, p=p: e.activation(S_T2(p), S_T2(p), AF.Ln))
                P.emit("scalar", lambda e, p=p: e.activation(S_T2(p), S_T2(p), AF.Exp, scale=-1.0))
                P.emit("scalar", lambda e, r=rq, p=p: e.activation(S_EU(p), r, AF.Exp, scale=-1.0))
                P.emit("scalar", lambda e, p=p: e.add(S_EU(p), S_EU(p), 1.0))
                P.emit("scalar", lambda e, p=p: e.activation(S_EU(p), S_EU(p), AF.Ln))
                P.emit("scalar", lambda e, p=p: e.activation(S_EU(p), S_EU(p), AF.Exp, scale=-1.0), SE)
                se_sig[p] = SE.n
            for p, cb in enumerate(cbs):
                P.wait("vector", SE, se_sig[p])
                P.emit("vector", lambda e, p=p: e.tensor_tensor(S_T1(p), S_T1(p), S_T2(p), ALU.mult))
                P.emit("vector", lambda e, p=p: e.tensor_tensor(S_T1(p), S_T1(p), S_EU(p), ALU.mult))
                P.emit("vector", lambda e, cb=cb, ga=ga, p=p: e.tensor_tensor(a_sl(cb, ga, ga + TC), S_T1(p), a_sl(cb, ga, ga + TC), ALU.add), VE)
        for cb in cbs:
            P.release(("kvr", cb % 4), VE)

    emit_prod_half(0, 0)
    emit_scan_quad([0, 1, 2, 3], 0)
    P.wait("gpsimd", VE, VE.n)
    cast("wck", wck_b, Wck_d)
    cast("wcr", wcr_b, Wcr_d)
    cast("wcv", wcv_b, Wcv_d)
    emit_prod_half(0, 1)
    emit_scan_quad([0, 1, 2, 3], 1)
    P.wait("gpsimd", VE, VE.n)
    cast("w1", w1_b.rearrange("e i c -> (e i) c"), w1_d.rearrange("e i c -> (e i) c"))
    cast("w3", w3_b.rearrange("e i c -> (e i) c"), w3_d.rearrange("e i c -> (e i) c"))
    emit_prod_half(1, 0)
    emit_scan_quad([4, 5, 6, 7], 0)
    P.wait("gpsimd", VE, VE.n)
    cast("w2", w2_b.rearrange("e d i -> (e d) i"), w2_d.rearrange("e d i -> (e d) i"))
    dma("gpsimd", S_CAST["ws"], ws1_b, ws1_d)
    dma("gpsimd", S_CAST["ws"], ws3_b, ws3_d)
    n_ws = dma("gpsimd", S_CAST["ws"], ws2_b, ws2_d)
    cast_done["ws"] = (S_CAST["ws"], n_ws)
    emit_prod_half(1, 1)
    emit_scan_quad([4, 5, 6, 7], 1)
    ve_scan = VE.n

    if DEBUG:
        P.wait("sync", VE, ve_scan)
        dma("sync", OUT, dbg["t_x1"], A[:])

    KCSQ = AR[:, 0:8192].bitcast(BF16)
    H2 = lambda cb, a, b: AR[:, 8192 + cb * 512 + a: 8192 + cb * 512 + b]
    RCT = AR[:, 12288:12800]
    VVT = AR[:, 12800:13312]
    TROW = SCR[0:1, 0:512]

    P.use("tensor", ("bank", ST))
    P.wait("tensor", VE, ve_scan)
    if DEBUG:
        P.wait("tensor", OUT, OUT.n)
    for cb in range(NCB):
        P.emit("tensor", lambda e, cb=cb: e.matmul(PS[ST][0:1, 0:1], ONECOL[:], A3[:, cb, 2047:2048],
                                                   start=(cb == 0), stop=(cb == 7)), TE)
    P.wait("scalar", VE, ve_scan)
    P.emit("scalar", lambda e: e.activation(H2P[:], A3[:, 0:8, 2047:2048].rearrange("p a b -> p (a b)"), AF.Square), SE)
    se_wsq = SE.n
    P.use("tensor", ("bank", SP2))
    P.wait("tensor", SE, se_wsq)
    for cb in range(NCB):
        P.emit("tensor", lambda e, cb=cb: e.matmul(PS[SP2][0:1, 0:1], ONECOL[:], H2P[:, cb:cb + 1],
                                                   start=(cb == 0), stop=(cb == 7)), TE)
    t_wssq = TE.n
    P.use("scalar", ("rows",))
    P.wait("scalar", TE, t_wssq)
    P.emit("scalar", lambda e: e.copy(ROWS[0:1, 0:1], PS[ST][0:1, 0:1]))
    P.emit("scalar", lambda e: e.copy(ROWS[0:1, 512:513], PS[SP2][0:1, 0:1]), SE)
    se_wrow = SE.n
    P.release(("bank", ST), SE)
    P.release(("bank", SP2), SE)
    P.use("vector", ("scr_p3",))
    P.wait("vector", SE, se_wrow)
    P.emit("vector", lambda e: e.tensor_scalar(rows_mu[:, 0:1], rows_mu[:, 0:1], 1.0 / C, None, ALU.mult, ALU.bypass))
    P.emit("vector", lambda e: e.tensor_scalar(rows_rs[:, 0:1], rows_rs[:, 0:1], 1.0 / C, None, ALU.mult, ALU.bypass))
    P.drain("vector")
    P.emit("vector", lambda e: e.tensor_scalar(TROW[:, 0:1], rows_mu[:, 0:1], rows_mu[:, 0:1], None, ALU.mult, ALU.bypass))
    P.drain("vector")
    P.emit("vector", lambda e: e.tensor_scalar(rows_rs[:, 0:1], rows_rs[:, 0:1], TROW[:, 0:1], LN_EPS, ALU.subtract, ALU.add), VE)
    ve_wv = VE.n
    P.wait("scalar", VE, ve_wv)
    P.emit("scalar", lambda e: e.activation(rows_rs[:, 0:1], rows_rs[:, 0:1], AF.Ln))
    P.drain("scalar")
    P.emit("scalar", lambda e: e.activation(rows_rs[:, 0:1], rows_rs[:, 0:1], AF.Exp, scale=-0.5), SE)
    se_wst = SE.n
    P.use("tensor", ("bank", SP2))
    P.wait("tensor", SE, se_wst)
    P.emit("tensor", lambda e: e.matmul(PS[SP2][:, 0:1], ONEROW[:], rows_mu[:, 0:1], start=True, stop=True))
    P.emit("tensor", lambda e: e.matmul(PS[SP2][:, 1:2], ONEROW[:], rows_rs[:, 0:1], start=True, stop=True), TE)
    t_wbc = TE.n
    P.release(("rows",), TE)
    P.wait("vector", TE, t_wbc)
    for cb in range(NCB):
        P.emit("vector", lambda e, cb=cb: e.tensor_tensor(H2P[:, cb:cb + 1], A3[:, cb, 2047:2048], PS[SP2][:, 0:1], ALU.subtract))
        P.drain("vector")
        P.emit("vector", lambda e, cb=cb: e.tensor_tensor(H2P[:, cb:cb + 1], H2P[:, cb:cb + 1], PS[SP2][:, 1:2], ALU.mult))
        P.drain("vector")
        P.emit("vector", lambda e, cb=cb: e.tensor_scalar(H2P[:, cb:cb + 1], H2P[:, cb:cb + 1], vcol(V_L2W, cb), vcol(V_L2B, cb), ALU.mult, ALU.add), VE)
        P.drain("vector")
    ve_h2p = VE.n
    P.release(("bank", SP2), VE)

    prev_tau_ve = ve_h2p
    ve_xxx2_tau = {}
    for tau in range(NTILE):
        t0 = tau * TN
        P.use("tensor", ("bank", ST))
        for cb in range(NCB):
            P.emit("tensor", lambda e, cb=cb, t0=t0: e.matmul(PS[ST][0:1, 0:512], ONECOL[:], a_sl(cb, t0, t0 + TN),
                                                              start=(cb == 0), stop=(cb == 7)), TE)
        t_sum = TE.n
        P.use("scalar", ("rows",))
        P.wait("scalar", TE, t_sum)
        P.emit("scalar", lambda e: e.copy(rows_mu, PS[ST][0:1, 0:512]), SE)
        P.release(("bank", ST), SE)
        P.use("tensor", ("bank", SP2))
        for cb in range(NCB):
            sq = cb % 2
            P.use("scalar", ("h2sq", sq))
            P.wait("scalar", VE, prev_tau_ve)
            P.emit("scalar", lambda e, cb=cb, t0=t0, s=sq: e.activation(H2(s, 0, TN), a_sl(cb, t0, t0 + TN), AF.Square), SE)
            se_q = SE.n
            P.wait("tensor", SE, se_q)
            P.emit("tensor", lambda e, cb=cb, s=sq: e.matmul(PS[SP2][0:1, 0:512], ONECOL[:], H2(s, 0, TN),
                                                             start=(cb == 0), stop=(cb == 7)), TE)
            P.release(("h2sq", sq), TE)
        t_ssq = TE.n
        P.wait("scalar", TE, t_ssq)
        P.emit("scalar", lambda e: e.copy(rows_rs, PS[SP2][0:1, 0:512]), SE)
        se_ssq = SE.n
        P.release(("bank", SP2), SE)
        P.use("vector", ("scr_p3",))
        P.wait("vector", SE, se_ssq)
        P.emit("vector", lambda e: e.tensor_scalar(rows_mu, rows_mu, 1.0 / C, None, ALU.mult, ALU.bypass))
        P.emit("vector", lambda e: e.tensor_scalar(rows_rs, rows_rs, 1.0 / C, None, ALU.mult, ALU.bypass))
        P.emit("vector", lambda e: e.tensor_tensor(TROW, rows_mu, rows_mu, ALU.mult))
        P.emit("vector", lambda e: e.tensor_tensor(rows_rs, rows_rs, TROW, ALU.subtract))
        P.emit("vector", lambda e: e.tensor_scalar(rows_rs, rows_rs, LN_EPS, None, ALU.add, ALU.bypass), VE)
        ve_v2 = VE.n
        P.release(("scr_p3",), VE)
        P.wait("scalar", VE, ve_v2)
        P.emit("scalar", lambda e: e.activation(rows_rs, rows_rs, AF.Ln))
        P.emit("scalar", lambda e: e.activation(rows_rs, rows_rs, AF.Exp, scale=-0.5), SE)
        se_st2 = SE.n
        P.use("tensor", ("bank", MM2))
        P.use("tensor", ("bank", MM3))
        P.wait("tensor", SE, se_st2)
        P.emit("tensor", lambda e: e.matmul(PS[MM2][:], ONEROW[:], rows_mu, start=True, stop=True), TE)
        P.emit("tensor", lambda e: e.matmul(PS[MM3][:], ONEROW[:], rows_rs, start=True, stop=True), TE)
        t_bc2 = TE.n
        P.release(("rows",), TE)
        P.wait("vector", TE, t_bc2)
        for cb in range(NCB):
            P.emit("vector", lambda e, cb=cb, t0=t0: e.tensor_tensor(H2(cb, 0, TN), a_sl(cb, t0, t0 + TN), PS[MM2][:], ALU.subtract))
            P.emit("vector", lambda e, cb=cb: e.tensor_tensor(H2(cb, 0, TN), H2(cb, 0, TN), PS[MM3][:], ALU.mult))
            P.emit("vector", lambda e, cb=cb: e.tensor_scalar(H2(cb, 0, TN), H2(cb, 0, TN), vcol(V_L2W, cb), vcol(V_L2B, cb), ALU.mult, ALU.add))
            P.emit("vector", lambda e, cb=cb: e.tensor_scalar(VVT, H2(cb, 0, TN), vcol(V_1MTS, cb), None, ALU.mult, ALU.bypass))
            P.emit("vector", lambda e, cb=cb, t0=t0: e.scalar_tensor_tensor(
                xx_sl(cb, t0, t0 + 1), H2P[:, cb:cb + 1], vcol(V_TS, cb), VVT[:, 0:1], ALU.mult, ALU.add))
            P.emit("vector", lambda e, cb=cb, t0=t0: e.scalar_tensor_tensor(
                xx_sl(cb, t0 + 1, t0 + TN), H2(cb, 0, TN - 1), vcol(V_TS, cb), VVT[:, 1:TN], ALU.mult, ALU.add), VE)
        ve_xxx2_tau[tau] = VE.n
        P.release(("bank", MM2), VE)
        P.release(("bank", MM3), VE)
        for cb in range(NCB):
            P.emit("vector", lambda e, cb=cb: e.tensor_copy(H2P[:, cb:cb + 1], H2(cb, TN - 1, TN)), VE)
        prev_tau_ve = VE.n

    RSC = AR[:, 12288:12800]
    SCO = RSC[:, 0:256]
    RMSK = lambda j: RSC[:, 256 + j * 16: 272 + j * 16]
    RMAX = RSC[:, 320:336]
    RKEEP = RSC[:, 336:352]
    RK4 = RSC[:, 352:368]
    RSEL = RSC[:, 368:432]
    RD4 = RSC[:, 432:436]
    RD16 = RSC[:, 436:452]

    for tb in range(NTB):
        bank = MM0 if tb % 2 == 0 else MM1
        P.use("tensor", ("bank", bank))
        if tb == 0:
            P.wait("tensor", SE, se_gate)
        for kc in range(NCB):
            P.emit("tensor", lambda e, b=bank, kc=kc, tb=tb: e.matmul(
                PS[b][:, 0:16], XNB3[:, kc, tb * 128:(tb + 1) * 128],
                GATET[:, kc * 16:(kc + 1) * 16], start=(kc == 0), stop=(kc == 7)), TE)
        t_sc = TE.n
        P.wait("scalar", TE, t_sc)
        P.emit("scalar", lambda e, b=bank, tb=tb: e.activation(
            SCO[:, tb * 16:(tb + 1) * 16], PS[b][:, 0:16], AF.Exp, scale=-1.0), SE)
        P.release(("bank", bank), SE)
    P.drain("scalar")
    P.emit("scalar", lambda e: e.add(SCO[:], SCO[:], 1.0))
    P.drain("scalar")
    P.emit("scalar", lambda e: e.activation(SCO[:], SCO[:], AF.Ln))
    P.drain("scalar")
    P.emit("scalar", lambda e: e.activation(SCO[:], SCO[:], AF.Exp, scale=-1.0), SE)
    se_sco = SE.n

    P.wait("vector", SE, se_sco)
    for B4 in range(4):
        c0 = B4 * 64
        for j in range(4):
            P.emit("vector", lambda e, c0=c0, j=j: e.tensor_tensor(
                RMSK(j), SCO[:, c0 + j * 16: c0 + (j + 1) * 16], GBR[:], ALU.add))
        P.drain("vector")
        P.emit("vector", lambda e: e.tensor_reduce(
            RMAX, RSC[:, 256:320].rearrange("p (a b) -> p a b", b=4), AX.X, ALU.max))
        P.drain("vector")
        for j in range(4):
            for G in range(4):
                P.emit("vector", lambda e, j=j, G=G: e.tensor_scalar(
                    RD4, RMAX[:, j * 4: j * 4 + 4], RMAX[:, j * 4 + G: j * 4 + G + 1], 0.0,
                    ALU.is_gt, ALU.add, accum_out=RKEEP[:, j * 4 + G: j * 4 + G + 1]))
        P.drain("vector")
        P.emit("vector", lambda e: e.tensor_scalar(RKEEP, RKEEP, 2.0, None, ALU.is_lt, ALU.bypass))
        P.drain("vector")
        P.emit("vector", lambda e: e.tensor_scalar(RK4, RKEEP, 4.0, -4.0, ALU.mult, ALU.add))
        P.drain("vector")
        for j in range(4):
            for G in range(4):
                P.emit("vector", lambda e, j=j, G=G: e.tensor_scalar(
                    RMSK(j)[:, G * 4:(G + 1) * 4], RMSK(j)[:, G * 4:(G + 1) * 4],
                    RK4[:, j * 4 + G: j * 4 + G + 1], None, ALU.add, ALU.bypass))
        P.drain("vector")
        for j in range(4):
            for ee in range(E):
                P.emit("vector", lambda e, j=j, ee=ee: e.tensor_scalar(
                    RD16, RMSK(j), RMSK(j)[:, ee:ee + 1], 0.0,
                    ALU.is_gt, ALU.add, accum_out=RSEL[:, j * 16 + ee: j * 16 + ee + 1]))
        P.drain("vector")
        P.emit("vector", lambda e: e.tensor_scalar(RSEL, RSEL, 4.0, None, ALU.is_lt, ALU.bypass))
        P.drain("vector")
        P.emit("vector", lambda e, c0=c0: e.tensor_tensor(
            SCO[:, c0:c0 + 64], RSEL, SCO[:, c0:c0 + 64], ALU.mult), VE)
        P.drain("vector")
    ve_comb = VE.n

    RCF = lambda ob: AR[:, 8192 + ob * 512: 8192 + (ob + 1) * 512]
    prev_vv_ve = prev_tau_ve
    for tau in range(NTILE):
        t0 = tau * TN
        ve_xxx2 = ve_xxx2_tau[tau]
        for hbg in range(4):
            for kc in range(NCB):
                P.waitp("sync", cast_done["wck"])
                tile, retok, wtok = t_load2(wck_b[hbg * 1024:(hbg + 1) * 1024, kc * 128:(kc + 1) * 128])
                for hb_l in range(8):
                    if kc == 0:
                        P.use("tensor", ("bank", hb_l))
                    if hb_l == 0:
                        P.waitp("tensor", retok)
                        P.wait("tensor", VE, ve_xxx2)
                    P.emit("tensor", lambda e, b=hb_l, t=tile, hb_l=hb_l, t0=t0, kc=kc: e.matmul(
                        PS[b][:], t[:, hb_l * 128:(hb_l + 1) * 128], xx_sl(kc, t0, t0 + TN),
                        start=(kc == 0), stop=(kc == 7)), TE)
                P.release(wtok, TE)
            t_kc = TE.n
            for hb_l in range(8):
                hb = hbg * 8 + hb_l
                P.use("scalar", ("scanbuf",))
                P.wait("scalar", TE, t_kc)
                P.emit("scalar", lambda e, b=hb_l: e.activation(SCR[:], PS[b][:], AF.Relu), SE)
                P.release(("bank", hb_l), SE)
                P.emit("scalar", lambda e, hb=hb: e.activation(KCSQ[:, hb * 512:(hb + 1) * 512], SCR[:], AF.Square), SE)
                P.release(("scanbuf",), SE)
        se_kcsq = SE.n
        for kc in range(NCB):
            P.waitp("sync", cast_done["wcr"])
            tile, retok, wtok = t_load2(wcr_b[0:1024, kc * 128:(kc + 1) * 128])
            for ob in range(8):
                if kc == 0:
                    P.use("tensor", ("bank", ob))
                if ob == 0:
                    P.waitp("tensor", retok)
                P.emit("tensor", lambda e, b=ob, t=tile, ob=ob, t0=t0, kc=kc: e.matmul(
                    PS[b][:], t[:, ob * 128:(ob + 1) * 128], xx_sl(kc, t0, t0 + TN),
                    start=(kc == 0), stop=(kc == 7)), TE)
            P.release(wtok, TE)
        t_rc = TE.n
        for ob in range(8):
            P.wait("scalar", TE, t_rc)
            if ob == 0:
                P.wait("scalar", VE, prev_vv_ve)
            P.emit("scalar", lambda e, ob=ob, b=ob: e.activation(RCF(ob), PS[b][:], AF.Exp, scale=-1.0), SE)
            P.release(("bank", ob), SE)
            P.emit("scalar", lambda e, ob=ob: e.add(RCF(ob), RCF(ob), 1.0))
            P.emit("scalar", lambda e, ob=ob: e.activation(RCF(ob), RCF(ob), AF.Ln))
            P.emit("scalar", lambda e, ob=ob: e.activation(RCF(ob), RCF(ob), AF.Exp, scale=-1.0), SE)
        se_rc = SE.n
        for hb in range(32):
            P.waitp("sync", cast_done["wcv"])
            tile, retok, wtok = t_load2(wcv_b[0:1024, hb * 128:(hb + 1) * 128])
            for cb in range(8):
                if hb == 0:
                    P.use("tensor", ("bank", cb))
                if cb == 0:
                    P.waitp("tensor", retok)
                    if hb == 0:
                        P.wait("tensor", SE, se_kcsq)
                P.emit("tensor", lambda e, b=cb, t=tile, cb=cb, hb=hb: e.matmul(
                    PS[b][:], t[:, cb * 128:(cb + 1) * 128], KCSQ[:, hb * 512:(hb + 1) * 512],
                    start=(hb == 0), stop=(hb == 31)), TE)
            P.release(wtok, TE)
        t_vv = TE.n
        for cb in range(8):
            P.wait("vector", TE, t_vv)
            P.wait("vector", SE, se_rc)
            P.emit("vector", lambda e, cb=cb: e.tensor_tensor(VVT, RCF(cb), PS[cb][:], ALU.mult), VE)
            P.emit("vector", lambda e, cb=cb, t0=t0: e.tensor_tensor(a_sl(cb, t0, t0 + TN), VVT, a_sl(cb, t0, t0 + TN), ALU.add), VE)
            P.release(("bank", cb), VE)
        prev_vv_ve = VE.n
    ve_p3 = VE.n
    te_p3 = TE.n
    se_p3 = SE.n

    if DEBUG:
        P.wait("sync", VE, ve_p3)
        dma("sync", OUT, dbg["t_s"], A[:])

    GPRE = AR[:, 0:2048].bitcast(BF16)
    WS1T = AR[:, 2048:4096].bitcast(BF16)
    WS3T = AR[:, 4096:6144].bitcast(BF16)
    WS2T = AR[:, 6144:8192].bitcast(BF16)
    ZMID = AR[:, 8192:9216].bitcast(BF16)
    SILU = AR[:, 9216:9728]
    HST = lambda r: AR[:, 9728 + r * 1024: 10752 + r * 1024].bitcast(BF16)
    CMF = AR[0:16, 11776:12800].bitcast(BF16)
    cmf_banks = [T0, T1, MM2, MM3]
    t_cmt = 0
    for tb in range(NTB):
        grp = tb % 4
        bank2 = cmf_banks[tb // 4]
        if grp == 0:
            P.use("tensor", ("bank", bank2))
        if tb == 0:
            P.wait("tensor", VE, ve_comb)
        P.emit("tensor", lambda e, tb=tb, b=bank2, grp=grp: e.matmul(
            PS[b][0:16, grp * 128:(grp + 1) * 128], SCO[:, tb * 16:(tb + 1) * 16], ID32[:],
            is_transpose=True, start=(grp == 0), stop=(grp == 3)), TE)
        if grp == 3:
            t_cmt = TE.n
    P.wait("scalar", TE, t_cmt)
    P.wait("scalar", VE, ve_p3)
    for g in range(4):
        P.emit("scalar", lambda e, g=g, b=cmf_banks[g]: e.copy(
            CMF[:, g * 512:(g + 1) * 512], PS[b][0:16, 0:512]), SE)
        P.release(("bank", cmf_banks[g]), SE)
    se_cmf = SE.n

    if DEBUG:
        P.wait("sync", SE, se_cmf)
        dma("sync", OUT, dbg["t_cmf"], CMF[:])
        dma("sync", OUT, dbg["t_xnb"], XNB[:])
        dma("sync", OUT, dbg["t_gt"], GATET[:])
        dma("sync", OUT, dbg["t_gbr"], GBR[:])

    W3SL = AR[:, 0:2048].bitcast(BF16)
    BAS = AR[0:16, 2048:3072].bitcast(BF16)
    P.wait("sync", TE, te_p3)
    P.wait("sync", VE, ve_p3)
    n_bas = dma("sync", S_CONST, BAS, bas_d)
    hst_r = [0]
    ve_he = 0
    w1_toks = None
    w3_toks = None

    def load_w1(e_i):
        toks = []
        for kc in range(NCB):
            P.waitp("sync", cast_done["w1"])
            tile, retok, wtok = t_load(w1_b[e_i][0:512, kc * 128:(kc + 1) * 128])
            toks.append((tile, retok, wtok))
        return toks

    def load_w3(e_i):
        toks = []
        for kc in range(NCB):
            P.waitp("sync", cast_done["w3"])
            P.use("sync", ("w3sl", kc))
            if e_i == 0 and kc == 0:
                P.wait("sync", TE, te_p3)
                P.wait("sync", VE, ve_p3)
            dstv = W3SL[:, kc * 512:(kc + 1) * 512]
            n3 = P.emit("sync", lambda e, o=dstv, i=w3_b[e_i][0:512, kc * 128:(kc + 1) * 128]: e.dma_start(
                out=o, in_=i, transpose=True), S_W3, 16)
            toks.append((dstv, (S_W3, n3), kc))
        return toks

    COMBE = AR[:, 3072:4096].bitcast(BF16)
    w1_toks = load_w1(0)
    w3_toks = load_w3(0)
    for e_i in range(E):
        cb_se = 0
        for tau in range(NTILE):
            bank = MM0 if tau % 2 == 0 else MM1
            P.use("tensor", ("bank", bank))
            if e_i == 0 and tau == 0:
                P.wait("tensor", SE, se_cmf)
                P.wait("tensor", S_CONST, n_bas)
                if DEBUG:
                    P.wait("tensor", OUT, OUT.n)
            P.emit("tensor", lambda e, e_i=e_i, tau=tau, b=bank: e.matmul(
                PS[b][:], BAS[:, e_i * 128:(e_i + 1) * 128], CMF[:, tau * 512:(tau + 1) * 512],
                start=True, stop=True), TE)
            t_cb = TE.n
            P.use("scalar", ("combe", tau))
            P.wait("scalar", TE, t_cb)
            if e_i == 0 and tau == 0:
                P.wait("scalar", VE, ve_p3)
                P.wait("scalar", TE, te_p3)
            P.emit("scalar", lambda e, tau=tau, b=bank: e.copy(COMBE[:, tau * 512:(tau + 1) * 512], PS[b][:]), SE)
            cb_se = SE.n
            P.release(("bank", bank), SE)
        for tau in range(NTILE):
            t0 = tau * TN
            t_h1_ib = [0] * 4
            t_h3_ib = [0] * 4
            for ib in range(4):
                P.use("tensor", ("bank", ib))
                for kc in range(NCB):
                    if ib == 0 and kc == 0:
                        P.waitp("tensor", w1_toks[0][1])
                        for kc2 in range(1, NCB):
                            P.waitp("tensor", w1_toks[kc2][1])
                    P.emit("tensor", lambda e, b=ib, t=w1_toks[kc][0], ib=ib, t0=t0, kc=kc: e.matmul(
                        PS[b][:], t[:, ib * 128:(ib + 1) * 128], xnb_sl(kc, t0, t0 + TN),
                        start=(kc == 0), stop=(kc == 7)), TE)
                t_h1_ib[ib] = TE.n
            for ib in range(4):
                P.use("tensor", ("bank", 4 + ib))
                for kc in range(NCB):
                    if ib == 0 and kc == 0:
                        for kc2 in range(NCB):
                            P.waitp("tensor", w3_toks[kc2][1])
                    P.emit("tensor", lambda e, b=4 + ib, t=w3_toks[kc][0], ib=ib, t0=t0, kc=kc: e.matmul(
                        PS[b][:], t[:, ib * 128:(ib + 1) * 128], xnb_sl(kc, t0, t0 + TN),
                        start=(kc == 0), stop=(kc == 7)), TE)
                t_h3_ib[ib] = TE.n
            if tau == NTILE - 1:
                for _, _, wtok in w1_toks:
                    P.release(wtok, TE)
                for _, _, kcsl in w3_toks:
                    P.release(("w3sl", kcsl), TE)
                if e_i + 1 < E:
                    nxt1 = load_w1(e_i + 1)
                    nxt3 = load_w3(e_i + 1)
            r = hst_r[0] % 2
            for ib in range(4):
                P.use("scalar", ("silu",))
                P.wait("scalar", TE, t_h1_ib[ib])
                if e_i == 0 and tau == 0 and ib == 0:
                    P.wait("scalar", VE, ve_p3)
                P.emit("scalar", lambda e, b=ib: e.activation(SILU, PS[b][:], AF.Silu), SE)
                se_si = SE.n
                P.release(("bank", ib), SE)
                if ib == 0:
                    P.use("vector", ("hst", r))
                P.wait("vector", SE, se_si)
                P.wait("vector", SE, cb_se)
                P.wait("vector", TE, t_h3_ib[ib])
                P.emit("vector", lambda e, b=4 + ib: e.tensor_tensor(SILU, SILU, PS[b][:], ALU.mult), VE)
                P.release(("bank", 4 + ib), VE)
                P.emit("vector", lambda e, r=r, ib=ib, tau=tau: e.tensor_tensor(
                    HST(r)[:, ib * 512:(ib + 1) * 512], SILU, COMBE[:, tau * 512:(tau + 1) * 512], ALU.mult), VE)
                ve_he = VE.n
                P.release(("silu",), VE)
                if ib == 3:
                    P.release(("combe", tau), VE)
            P.wait("sync", VE, ve_he)
            dma("sync", S_HS[r], H_s[tau][e_i], HST(r))
            P.release(("hst", r), S_HS[r])
            hst_r[0] += 1
        if e_i + 1 < E:
            w1_toks = nxt1
            w3_toks = nxt3
    f1_hs_toks = [(S_HS[0], S_HS[0].n), (S_HS[1], S_HS[1].n)]

    te_f1 = TE.n
    P.wait("sync", TE, te_f1)
    P.wait("sync", VE, ve_he)
    P.waitp("sync", cast_done["ws"])
    for kc in range(NCB):
        P.emit("sync", lambda e, kc=kc: e.dma_start(
            out=WS1T[:, kc * 512:(kc + 1) * 512], in_=ws1_b[0:512, kc * 128:(kc + 1) * 128], transpose=True), S_WS, 16)
    for kc in range(NCB):
        P.emit("sync", lambda e, kc=kc: e.dma_start(
            out=WS3T[:, kc * 512:(kc + 1) * 512], in_=ws3_b[0:512, kc * 128:(kc + 1) * 128], transpose=True), S_WS, 16)
    for ikc in range(4):
        P.emit("sync", lambda e, ikc=ikc: e.dma_start(
            out=WS2T[:, ikc * 1024:(ikc + 1) * 1024], in_=ws2_b[0:1024, ikc * 128:(ikc + 1) * 128], transpose=True), S_WS, 16)
    re_wsz = S_WS.n

    YB = list(range(8))
    for tau in range(NTILE):
        t0 = tau * TN
        for ib in range(4):
            b1 = 0 if ib % 2 == 0 else 1
            b3 = 2 if ib % 2 == 0 else 3
            P.use("tensor", ("bank", b1))
            P.use("tensor", ("bank", b3))
            P.wait("tensor", S_WS, re_wsz)
            for kc in range(NCB):
                P.emit("tensor", lambda e, b=b1, ib=ib, kc=kc, t0=t0: e.matmul(
                    PS[b][:], WS1T[:, kc * 512 + ib * 128: kc * 512 + (ib + 1) * 128], xnb_sl(kc, t0, t0 + TN),
                    start=(kc == 0), stop=(kc == 7)), TE)
            for kc in range(NCB):
                P.emit("tensor", lambda e, b=b3, ib=ib, kc=kc, t0=t0: e.matmul(
                    PS[b][:], WS3T[:, kc * 512 + ib * 128: kc * 512 + (ib + 1) * 128], xnb_sl(kc, t0, t0 + TN),
                    start=(kc == 0), stop=(kc == 7)), TE)
            t_z = TE.n
            P.use("scalar", ("silu",))
            P.wait("scalar", TE, t_z)
            if tau == 0 and ib == 0:
                P.wait("scalar", VE, ve_he)
            P.emit("scalar", lambda e, b=b1: e.activation(SILU, PS[b][:], AF.Silu), SE)
            se_zs = SE.n
            P.release(("bank", b1), SE)
            P.use("vector", ("zmid", ib))
            P.wait("vector", SE, se_zs)
            P.emit("vector", lambda e, ib=ib, b=b3: e.tensor_tensor(ZMID[:, ib * 512:(ib + 1) * 512], SILU, PS[b][:], ALU.mult), VE)
            P.release(("bank", b3), VE)
            P.release(("silu",), VE)
        ve_zmid = VE.n
        h_re = {}
        for b in YB:
            P.use("tensor", ("bank", b))
        for e_i in range(E):
            esub = e_i % 8
            P.use("sync", ("xxh", esub))
            if tau == 0 and e_i == 0:
                P.wait("sync", TE, te_p3)
                for tok in f1_hs_toks:
                    P.waitp("sync", tok)
            h_re[e_i] = (S_H, dma("sync", S_H, XX3b[:, esub, :], H_s[tau][e_i]))
            P.waitp("sync", cast_done["w2"])
            w2t = []
            for ikc in range(4):
                w2t.append(t_load2(w2_b[e_i][0:1024, ikc * 128:(ikc + 1) * 128]))
            P.waitp("tensor", h_re[e_i])
            for ikc in range(4):
                P.waitp("tensor", w2t[ikc][1])
                for db in range(8):
                    first = (e_i == 0 and ikc == 0)
                    P.emit("tensor", lambda e, b=YB[db], t=w2t[ikc][0], db=db, esub=esub, ikc=ikc, first=first: e.matmul(
                        PS[b][:], t[:, db * 128:(db + 1) * 128],
                        XX3b[:, esub, ikc * 512:(ikc + 1) * 512],
                        start=first, stop=False), TE)
                P.release(w2t[ikc][2], TE)
            if e_i >= 8:
                P.release(("xxh", esub), TE)
            else:
                P.release(("xxh", esub), TE)
        P.wait("tensor", VE, ve_zmid)
        for ikc in range(4):
            for db in range(8):
                P.emit("tensor", lambda e, b=YB[db], db=db, ikc=ikc: e.matmul(
                    PS[b][:], WS2T[:, ikc * 1024 + db * 128: ikc * 1024 + (db + 1) * 128],
                    ZMID[:, ikc * 512:(ikc + 1) * 512], start=False, stop=(ikc == 3)), TE)
        t_y = TE.n
        for esub in range(8):
            P.release(("xxh", esub), TE, t_y)
        for ib in range(4):
            P.release(("zmid", ib), TE, t_y)
        for db in range(8):
            P.use("scalar", ("gpre", db))
            P.wait("scalar", TE, t_y)
            P.emit("scalar", lambda e, db=db, b=YB[db]: e.copy(GPRE[:, db * 512:(db + 1) * 512], PS[b][:]), SE)
            P.release(("bank", YB[db]), SE)
        se_g = SE.n
        P.wait("sync", SE, se_g)
        if DEBUG:
            dma("sync", OUT, dbg["t_g"][tau], GPRE[:])
        n_gs = dma("sync", S_GS, g_s[tau], GPRE[:])
        for db in range(8):
            P.release(("gpre", db), S_GS, S_GS.n)
    spl_g = S_GS.n
    te_f2 = TE.n
    ve_f2 = VE.n

    STM = lambda r: AR[:, r * 1024: (r + 1) * 1024]
    XT2 = lambda r: AR[:, 2048 + r * 1024: 2048 + (r + 1) * 1024]
    OTB = lambda r: AR[:, 4096 + r * 1024: 4096 + (r + 1) * 1024]
    GTL = AR[:, 6144:8192].bitcast(BF16)
    GSL = lambda r: AR[:, 8192 + r * 1024: 8192 + (r + 1) * 1024]

    first_stm = [True]
    first_xt2 = [True]
    first_gs = [True]
    for tau in range(NTILE):
        P.use("sync", ("gtile",))
        if tau == 0:
            P.wait("sync", S_GS, spl_g)
            P.wait("sync", TE, te_f2)
        n_g = dma("sync", S_GT, GTL[:], g_s[tau])
        for tb4 in range(4):
            tb = tau * 4 + tb4
            r = tb % 2
            P.use("sync", ("xt2", r))
            if first_xt2[0]:
                P.wait("sync", TE, te_f2)
                first_xt2[0] = False
            n_x = dma("sync", S_X2[r], XT2(r), x_d[tb * 128:(tb + 1) * 128, :])
            P.use("scalar", ("stm", r))
            for g in range(2):
                bank = T0 if g == 0 else T1
                P.use("tensor", ("bank", bank))
                for j in range(4):
                    cb = g * 4 + j
                    P.emit("tensor", lambda e, cb=cb, j=j, b=bank, tb=tb: e.matmul(
                        PS[b][:, j * 128:(j + 1) * 128], A3[:, cb, tb * 128:(tb + 1) * 128],
                        ID32[:], is_transpose=True, start=(j == 0), stop=(j == 3)), TE)
                t_tr = TE.n
                P.wait("scalar", TE, t_tr)
                if first_stm[0]:
                    P.wait("scalar", S_GS, spl_g)
                    first_stm[0] = False
                P.emit("scalar", lambda e, r=r, g=g, b=bank: e.copy(STM(r)[:, g * 512:(g + 1) * 512], PS[b][:]), SE)
                P.release(("bank", bank), SE)
            bank = T0 if tb % 2 == 0 else T1
            P.use("tensor", ("bank", bank))
            P.wait("tensor", S_GT, n_g)
            for cb in range(NCB):
                P.emit("tensor", lambda e, cb=cb, b=bank, tb4=tb4: e.matmul(
                    PS[b][:].bitcast(BF16)[:, cb * 128:(cb + 1) * 128],
                    GTL[:, cb * 512 + tb4 * 128: cb * 512 + tb4 * 128 + 128],
                    IDB[:], is_transpose=True, start=(cb == 0), stop=(cb == 7)), TE)
            t_gt = TE.n
            P.use("scalar", ("gsil", r))
            P.wait("scalar", TE, t_gt)
            if first_gs[0]:
                for tok in f1_hs_toks:
                    P.waitp("scalar", tok)
                P.wait("scalar", VE, ve_f2)
                first_gs[0] = False
            P.emit("scalar", lambda e, r=r, b=bank: e.activation(GSL(r), PS[b][:].bitcast(BF16)[:, 0:1024], AF.Silu), SE)
            se_gs = SE.n
            P.release(("bank", bank), SE)
            P.use("vector", ("ot", r))
            P.wait("vector", SE, se_gs)
            P.wait("vector", S_X2[r], n_x)
            P.emit("vector", lambda e, r=r: e.tensor_tensor(STM(r), STM(r), GSL(r), ALU.mult))
            P.emit("vector", lambda e, r=r: e.tensor_tensor(OTB(r), STM(r), XT2(r), ALU.add), VE)
            ve_ot = VE.n
            P.release(("stm", r), VE)
            P.release(("gsil", r), VE)
            P.release(("xt2", r), VE)
            P.wait("sync", VE, ve_ot)
            dma("sync", S_OT[r], out_d[tb * 128:(tb + 1) * 128, :], OTB(r))
            P.release(("ot", r), S_OT[r])
        P.release(("gtile",), TE)

    P.wait("sync", S_OT[0], S_OT[0].n)
    P.wait("sync", S_OT[1], S_OT[1].n)
    if DEBUG:
        P.wait("sync", OUT, OUT.n)

    with nc.Block() as block:
        @block.sync
        def _(e):
            for f in P.steps["sync"]:
                f(e)

        @block.gpsimd
        def _(e):
            for f in P.steps["gpsimd"]:
                f(e)

        @block.tensor
        def _(e):
            for f in P.steps["tensor"]:
                f(e)

        @block.scalar
        def _(e):
            for f in P.steps["scalar"]:
                f(e)

        @block.vector
        def _(e):
            for f in P.steps["vector"]:
                f(e)

    es.close()
    return nc


_NC_CACHE = None


def kernel(**inputs):
    global _NC_CACHE
    if _NC_CACHE is None:
        _NC_CACHE = build_nc()
    nc = _NC_CACHE

    x = np.ascontiguousarray(np.asarray(inputs["L0_attn_x"], dtype=np.float32))
    _ = inputs["L0_attn_time_decay"]
    basis = np.zeros((E, E * 128), dtype=ml_dtypes.bfloat16)
    for e_i in range(E):
        basis[e_i, e_i * 128:(e_i + 1) * 128] = 1.0
    common = {
        "Wk": np.ascontiguousarray(np.asarray(inputs["L0_attn_Wk"], np.float32)),
        "Wv": np.ascontiguousarray(np.asarray(inputs["L0_attn_Wv"], np.float32)),
        "Wr": np.ascontiguousarray(np.asarray(inputs["L0_attn_Wr"], np.float32)),
        "Wck": np.ascontiguousarray(np.asarray(inputs["L0_attn_Wck"], np.float32)),
        "Wcv": np.ascontiguousarray(np.asarray(inputs["L0_attn_Wcv"], np.float32)),
        "Wcr": np.ascontiguousarray(np.asarray(inputs["L0_attn_Wcr"], np.float32)),
        "ln1w": np.ascontiguousarray(np.asarray(inputs["L0_attn_ln1_w"], np.float32)),
        "ln1b": np.ascontiguousarray(np.asarray(inputs["L0_attn_ln1_b"], np.float32)),
        "ln2w": np.ascontiguousarray(np.asarray(inputs["L0_attn_ln2_w"], np.float32)),
        "ln2b": np.ascontiguousarray(np.asarray(inputs["L0_attn_ln2_b"], np.float32)),
        "tshift": np.ascontiguousarray(np.asarray(inputs["L0_attn_time_shift"], np.float32)),
        "tfirst": np.ascontiguousarray(np.asarray(inputs["L0_attn_time_first"], np.float32)),
        "gate_w": np.ascontiguousarray(np.asarray(inputs["L0_ffn_gate_w"], np.float32)),
        "gate_bias": np.ascontiguousarray(np.asarray(inputs["L0_ffn_gate_bias"], np.float32)),
        "w1": np.ascontiguousarray(np.asarray(inputs["L0_ffn_w1"], np.float32)),
        "w2": np.ascontiguousarray(np.asarray(inputs["L0_ffn_w2"], np.float32)),
        "w3": np.ascontiguousarray(np.asarray(inputs["L0_ffn_w3"], np.float32)),
        "ws1": np.ascontiguousarray(np.asarray(inputs["L0_ffn_ws1"], np.float32)),
        "ws2": np.ascontiguousarray(np.asarray(inputs["L0_ffn_ws2"], np.float32)),
        "ws3": np.ascontiguousarray(np.asarray(inputs["L0_ffn_ws3"], np.float32)),
        "identb": np.eye(128, dtype=ml_dtypes.bfloat16),
        "ident32": np.eye(128, dtype=np.float32),
        "basis": basis,
    }
    in_maps = [dict(common, x=np.ascontiguousarray(x[b])) for b in range(8)]
    res = run_bass_kernel_spmd(nc, in_maps, core_ids=list(range(8)))
    out = np.stack([np.asarray(res.results[b]["out"], np.float32) for b in range(8)], axis=0)
    return out


if __name__ == "__main__":
    build_nc()
    print("built ok")
